# version 01
# baseline (speedup 1.0000x reference)
import numpy as np
import ml_dtypes
from contextlib import ExitStack

import concourse.bass as bass
import concourse.mybir as mybir
from concourse.bass_utils import run_bass_kernel_spmd

F32 = mybir.dt.float32
BF16 = mybir.dt.bfloat16
AF = mybir.ActivationFunctionType
ALU = mybir.AluOpType
AX = mybir.AxisListType

T = 4096
D = 2048
NHL = 8
HD = 128
HF = 2816
NHC = HF // 128
NTT = T // 128
NTC = T // 512
NDC = D // 128
SCALE = 1.0 / float(np.sqrt(128.0))
LN_EPS = 1e-5
INV_D = 1.0 / D
Q_S1 = 0.5 / (5632.0 * 2048.0)
Q_S2 = 0.5e-8
GROUPS = [[0, 1], [2, 3], [4, 5], [6, 7]]

UPTO = "full"
DEBUG_DUMPS = ()


class Prog:

    def __init__(self):
        self.items = []
        self.c = 0
        self.d = 0
        self.last = None

    def emit(self, eng, fn, dma=False, inc=None):
        wait = self.last
        if dma:
            self.d += 16
            inc_spec = ("d", 16)
            self.last = ("d", self.d)
        else:
            self.c += inc if inc else 1
            inc_spec = ("c", inc if inc else 1)
            self.last = ("c", self.c)
        self.items.append((eng, fn, wait, inc_spec))


def build_program(upto=UPTO, debug=()):
    nc = bass.Bass("TRN2", target_bir_lowering=False)

    x = nc.dram_tensor("x", [T, D], F32, kind="ExternalInput").ap()
    wqT = nc.dram_tensor("wqT", [D, 1024], F32, kind="ExternalInput").ap()
    wkT = nc.dram_tensor("wkT", [D, 1024], F32, kind="ExternalInput").ap()
    wvT = nc.dram_tensor("wvT", [D, 1024], F32, kind="ExternalInput").ap()
    woT = nc.dram_tensor("woT", [1024, D], F32, kind="ExternalInput").ap()
    w1T = nc.dram_tensor("w1T", [D, HF], F32, kind="ExternalInput").ap()
    w2T = nc.dram_tensor("w2T", [D, HF], F32, kind="ExternalInput").ap()
    w3T = nc.dram_tensor("w3T", [HF, D], F32, kind="ExternalInput").ap()
    out = nc.dram_tensor("out", [T, D], F32, kind="ExternalOutput").ap()

    xnT_s = nc.dram_tensor("xnT_s", [NTC, 128, NDC * 512], BF16, kind="Internal").ap()
    kT_s = nc.dram_tensor("kT_s", [NHL, 128, T], BF16, kind="Internal").ap()
    qT_s = nc.dram_tensor("qT_s", [NHL, 128, T], BF16, kind="Internal").ap()
    v_s = nc.dram_tensor("v_s", [4, NTT, 128, 256], BF16, kind="Internal").ap()
    w1q_s = nc.dram_tensor("w1q_s", [NHC, 128, D], BF16, kind="Internal").ap()
    w2q_s = nc.dram_tensor("w2q_s", [NHC, 128, D], BF16, kind="Internal").ap()
    w3q_s = nc.dram_tensor("w3q_s", [NHC, 128, D], BF16, kind="Internal").ap()
    s_part = nc.dram_tensor("s_part", [T, D], F32, kind="Internal").ap()
    s_full = nc.dram_tensor("s_full", [T, D], F32, kind="Internal").ap()
    f_part = nc.dram_tensor("f_part", [T, D], F32, kind="Internal").ap()
    f_full = nc.dram_tensor("f_full", [T, D], F32, kind="Internal").ap()
    gam_part = nc.dram_tensor("gam_part", [1, 4], F32, kind="Internal").ap()
    gam_full = nc.dram_tensor("gam_full", [1, 4], F32, kind="Internal").ap()
    colb = nc.dram_tensor("colb", [128], F32, kind="Internal").ap()

    dbg = {}
    scratch_map = {"xnT_s": xnT_s, "kT_s": kT_s, "qT_s": qT_s, "v_s": v_s,
                   "w1q_s": w1q_s, "w2q_s": w2q_s, "w3q_s": w3q_s,
                   "s_part": s_part, "s_full": s_full, "f_part": f_part,
                   "f_full": f_full, "gam_full": gam_full}
    for name in debug:
        src = scratch_map[name]
        dbg[name] = (nc.dram_tensor("dbg_" + name, list(src.shape),
                                    src.dtype, kind="ExternalOutput").ap(), src)

    ident_dram = nc.inline_tensor(np.eye(128, dtype=ml_dtypes.bfloat16), "ident_c")
    ones_dram = nc.inline_tensor(np.ones((1, 128), dtype=np.float32), "ones_c")
    _mk = np.zeros((4, 128, 512), dtype=np.float32)
    for _i in range(4):
        _p = np.arange(128)[:, None]
        _f = np.arange(512)[None, :]
        _mk[_i] = np.where(128 * _i + _p - _f >= 0, 0.0, -1e30)
    cmask_dram = nc.inline_tensor(_mk, "cmask_c")

    p = Prog()

    with ExitStack() as root:
        sb = lambda n, s, dt: root.enter_context(nc.sbuf_tensor(n, s, dt))
        ident = sb("ident_sb", [128, 128], BF16)
        ones_f = sb("ones_f", [1, 128], F32)
        h_bc = sb("h_bc", [128, 4], F32)
        hn_bc = sb("hn_bc", [128, 4], F32)
        cmask = sb("cmask_sb", [128, 4 * 512], F32)

        p.emit("sync", lambda e: e.dma_start(out=ident[:], in_=ident_dram.ap()), dma=True)
        p.emit("sync", lambda e: e.dma_start(out=ones_f[:], in_=ones_dram.ap()), dma=True)
        p.emit("sync", lambda e: e.dma_start(
            out=cmask[:].rearrange("p (c f) -> p c f", c=4),
            in_=cmask_dram.ap().rearrange("c p f -> p c f")), dma=True)

        with ExitStack() as ph:
            sb1 = lambda n, s, dt: ph.enter_context(nc.sbuf_tensor(n, s, dt))
            xt = sb1("xt", [128, D], F32)
            xsq = sb1("xsq", [128, D], F32)
            xn4 = [sb1(f"xn{i}", [128, D], BF16) for i in range(4)]
            xnTc = sb1("xnTc", [128, NDC * 512], BF16)
            nm = sb1("nm", [128, 1], F32)
            ssq = sb1("ssq", [128, 1], F32)
            t1v = sb1("t1v", [128, 1], F32)
            msq = sb1("msq", [128, 1], F32)
            veps = sb1("veps", [128, 1], F32)
            stdv = sb1("stdv", [128, 1], F32)
            rstd = sb1("rstd", [128, 1], F32)
            ptr = ph.enter_context(nc.psum_tensor("ptr", [128, 512], BF16))

            for tc in range(NTC):
                for i in range(4):
                    ti = tc * 4 + i
                    xi = xn4[i]
                    p.emit("sync", lambda e, ti=ti: e.dma_start(
                        out=xt[:], in_=x[ti * 128:(ti + 1) * 128, :]), dma=True)
                    p.emit("vector", lambda e: e.reduce_sum(
                        out=nm[:], in_=xt[:], axis=AX.X, negate=True))
                    p.emit("vector", lambda e: e.tensor_scalar_mul(nm[:], nm[:], INV_D))
                    p.emit("scalar", lambda e: e.activation(
                        out=xsq[:], in_=xt[:], func=AF.Square, accum_out=ssq[:]))
                    p.emit("vector", lambda e: e.tensor_scalar(
                        out=t1v[:], in0=ssq[:], scalar1=INV_D, scalar2=LN_EPS,
                        op0=ALU.mult, op1=ALU.add))
                    p.emit("vector", lambda e: e.tensor_single_scalar(
                        out=msq[:], in_=nm[:], scalar=nm[:], op=ALU.mult))
                    p.emit("vector", lambda e: e.tensor_sub(veps[:], t1v[:], msq[:]))
                    p.emit("scalar", lambda e: e.activation(
                        out=stdv[:], in_=veps[:], func=AF.Sqrt))
                    p.emit("vector", lambda e: e.reciprocal(rstd[:], stdv[:]))
                    p.emit("vector", lambda e, xi=xi: e.tensor_scalar(
                        out=xi[:], in0=xt[:], scalar1=nm[:], scalar2=rstd[:],
                        op0=ALU.add, op1=ALU.mult))
                for dc in range(NDC):
                    for i in range(4):
                        p.emit("tensor", lambda e, i=i, dc=dc: e.transpose(
                            ptr[:, i * 128:(i + 1) * 128],
                            xn4[i][:, dc * 128:(dc + 1) * 128], ident[:]))
                    p.emit("vector", lambda e, dc=dc: e.tensor_copy(
                        out=xnTc[:, dc * 512:(dc + 1) * 512], in_=ptr[:]))
                p.emit("sync", lambda e, tc=tc: e.dma_start(
                    out=xnT_s[tc], in_=xnTc[:]), dma=True)
        if upto == "p1a":
            return _finish(nc, p, dbg, root)

        with ExitStack() as ph:
            sb1 = lambda n, s, dt: ph.enter_context(nc.sbuf_tensor(n, s, dt))
            wq_sb = sb1("wq_sb", [128, NDC * 1024], BF16)
            wk_sb = sb1("wk_sb", [128, NDC * 1024], BF16)
            wv_sb = sb1("wv_sb", [128, NDC * 1024], BF16)
            wload = sb1("wload", [128, 2816], F32)

            for wdst, wsrc in ((wq_sb, wqT), (wk_sb, wkT), (wv_sb, wvT)):
                for dc in range(NDC):
                    p.emit("sync", lambda e, wsrc=wsrc, dc=dc: e.dma_start(
                        out=wload[:, 0:1024], in_=wsrc[dc * 128:(dc + 1) * 128, :]),
                        dma=True)
                    p.emit("scalar", lambda e, wdst=wdst, dc=dc: e.copy(
                        out=wdst[:, dc * 1024:(dc + 1) * 1024], in_=wload[:, 0:1024]))

            gcols = sb1("gcols", [128, 24], F32)
            gcol = sb1("gcol", [128, 1], F32)
            grow = sb1("grow", [1, 128], F32)
            gpart = sb1("gpart", [1, 4], F32)
            p.emit("vector", lambda e: e.memset(gpart[:], 0.0))
            for wi, wsrc in enumerate((w1T, w2T, w3T)):
                nblk = NDC if wi < 2 else NHC
                ncol = HF if wi < 2 else D
                for blk_i in range(nblk):
                    p.emit("sync", lambda e, wsrc=wsrc, blk_i=blk_i, ncol=ncol: e.dma_start(
                        out=wload[:, 0:ncol],
                        in_=wsrc[blk_i * 128:(blk_i + 1) * 128, :]), dma=True)
                    p.emit("vector", lambda e, blk_i=blk_i, ncol=ncol: e.reduce_sum(
                        out=gcols[:, blk_i:blk_i + 1], in_=wload[:, 0:ncol],
                        axis=AX.X, apply_absolute_value=True))
                p.emit("vector", lambda e, nblk=nblk: e.reduce_sum(
                    out=gcol[:], in_=gcols[:, 0:nblk], axis=AX.X))
                p.emit("sync", lambda e: e.dma_start(out=colb, in_=gcol[:]), dma=True)
                p.emit("sync", lambda e: e.dma_start(
                    out=grow[:], in_=colb.rearrange("(o p) -> o p", o=1)), dma=True)
                p.emit("vector", lambda e, wi=wi: e.reduce_sum(
                    out=gpart[:, wi:wi + 1], in_=grow[:], axis=AX.X))
            p.emit("sync", lambda e: e.dma_start(out=gam_part, in_=gpart[:]), dma=True)
            p.emit("gpsimd", lambda e: e.collective_compute(
                "AllReduce", ALU.add, replica_groups=GROUPS,
                ins=[gam_part.opt()], outs=[gam_full.opt()]))
            grow4 = sb1("grow4", [1, 4], F32)
            hrow = sb1("hrow", [1, 4], F32)
            hnrow = sb1("hnrow", [1, 4], F32)
            p.emit("sync", lambda e: e.dma_start(out=grow4[:], in_=gam_full), dma=True)
            p.emit("vector", lambda e: e.tensor_scalar(
                out=hrow[:], in0=grow4[:], scalar1=Q_S1, scalar2=Q_S2,
                op0=ALU.mult, op1=ALU.add))
            p.emit("vector", lambda e: e.tensor_scalar_mul(hnrow[:], hrow[:], -1.0))
            pbc = ph.enter_context(nc.psum_tensor("pbc", [128, 4], F32))
            p.emit("tensor", lambda e: e.matmul(
                pbc[:], ones_f[:], hrow[:], start=True, stop=True))
            p.emit("vector", lambda e: e.tensor_copy(out=h_bc[:], in_=pbc[:]))
            p.emit("tensor", lambda e: e.matmul(
                pbc[:], ones_f[:], hnrow[:], start=True, stop=True))
            p.emit("vector", lambda e: e.tensor_copy(out=hn_bc[:], in_=pbc[:]))

            wfp = sb1("wfp", [128, D], F32)
            tneg = sb1("tneg", [128, D], BF16)
            wqt = sb1("wqt", [128, D], BF16)
            for wi, (wsrc, wdst_s) in enumerate(((w1T, w1q_s), (w2T, w2q_s), (w3T, w3q_s))):
                for hc in range(NHC):
                    if wi < 2:
                        src = wsrc[:, hc * 128:(hc + 1) * 128].rearrange(
                            "(c p) f -> p c f", p=128)
                        dst = wfp[:].rearrange("p (c f) -> p c f", c=NDC)
                        p.emit("sync", lambda e, src=src, dst=dst: e.dma_start(
                            out=dst, in_=src), dma=True)
                    else:
                        p.emit("sync", lambda e, wsrc=wsrc, hc=hc: e.dma_start(
                            out=wfp[:], in_=wsrc[hc * 128:(hc + 1) * 128, :]), dma=True)
                    p.emit("vector", lambda e, wi=wi: e.tensor_single_scalar(
                        out=tneg[:], in_=wfp[:], scalar=hn_bc[:, wi:wi + 1],
                        op=ALU.is_lt))
                    p.emit("vector", lambda e, wi=wi: e.scalar_tensor_tensor(
                        out=wqt[:], in0=wfp[:], scalar=h_bc[:, wi:wi + 1],
                        in1=tneg[:], op0=ALU.is_gt, op1=ALU.subtract))
                    p.emit("sync", lambda e, wdst_s=wdst_s, hc=hc: e.dma_start(
                        out=wdst_s[hc], in_=wqt[:]), dma=True)
            if upto == "quant":
                return _finish(nc, p, dbg, root)

            xnT_sb = sb1("xnT_sb", [128, NDC * 512], BF16)
            pstage = sb1("pstage", [128, 512], BF16)
            pproj = ph.enter_context(nc.psum_tensor("pproj", [128, 512], F32))
            for tc in range(NTC):
                p.emit("sync", lambda e, tc=tc: e.dma_start(
                    out=xnT_sb[:], in_=xnT_s[tc]), dma=True)
                for wsb, dst_s in ((wk_sb, kT_s), (wq_sb, qT_s)):
                    for oc in range(NHL):
                        for dc in range(NDC):
                            p.emit("tensor", lambda e, wsb=wsb, oc=oc, dc=dc: e.matmul(
                                pproj[:],
                                wsb[:, dc * 1024 + oc * 128: dc * 1024 + (oc + 1) * 128],
                                xnT_sb[:, dc * 512:(dc + 1) * 512],
                                start=(dc == 0), stop=(dc == NDC - 1)))
                        p.emit("vector", lambda e: e.tensor_copy(
                            out=pstage[:], in_=pproj[:]))
                        p.emit("sync", lambda e, dst_s=dst_s, oc=oc, tc=tc: e.dma_start(
                            out=dst_s[oc, :, tc * 512:(tc + 1) * 512], in_=pstage[:]),
                            dma=True)
                for sv in range(4):
                    st_g = tc * 4 + sv
                    for dvc in range(2):
                        for dc in range(NDC):
                            p.emit("tensor", lambda e, sv=sv, dvc=dvc, dc=dc: e.matmul(
                                pproj[:],
                                xnT_sb[:, dc * 512 + sv * 128: dc * 512 + (sv + 1) * 128],
                                wv_sb[:, dc * 1024 + dvc * 512: dc * 1024 + (dvc + 1) * 512],
                                start=(dc == 0), stop=(dc == NDC - 1)))
                        p.emit("vector", lambda e: e.tensor_copy(
                            out=pstage[:], in_=pproj[:]))
                        for u in range(2):
                            p.emit("sync", lambda e, dvc=dvc, u=u, st_g=st_g: e.dma_start(
                                out=v_s[2 * dvc + u, st_g],
                                in_=pstage[:, u * 256:(u + 1) * 256]), dma=True)
        if upto == "proj":
            return _finish(nc, p, dbg, root)

        with ExitStack() as ph:
            sb1 = lambda n, s, dt: ph.enter_context(nc.sbuf_tensor(n, s, dt))
            zT_sb = sb1("zT_sb", [128, NHL * NTT * 128], BF16)
            v_hp = sb1("v_hp", [128, NTT * 256], BF16)
            kT_sb = sb1("kT_sb", [128, T], BF16)
            qT_sb = sb1("qT_sb", [128, T], BF16)
            rowbuf = sb1("rowbuf", [128, T], F32)
            pbuf = sb1("pbuf", [128, T], BF16)
            ptbuf = sb1("ptbuf", [128, NTT * 512], BF16)
            negmax = sb1("negmax", [128, 1], F32)
            lsum = sb1("lsum", [128, 1], F32)
            rl = sb1("rl", [128, 1], F32)
            psum_s = ph.enter_context(nc.psum_tensor("psum_s", [128, 512], F32))
            psum_pt = ph.enter_context(nc.psum_tensor("psum_pt", [128, 512], BF16))
            psum_z = ph.enter_context(nc.psum_tensor("psum_z", [128, 512], F32))

            for hp in range(4):
                p.emit("sync", lambda e, hp=hp: e.dma_start(
                    out=v_hp[:].rearrange("p (c f) -> p c f", c=NTT),
                    in_=v_s[hp].rearrange("c p f -> p c f")), dma=True)
                for hh in range(2):
                    h = hp * 2 + hh
                    p.emit("sync", lambda e, h=h: e.dma_start(
                        out=kT_sb[:], in_=kT_s[h]), dma=True)
                    p.emit("sync", lambda e, h=h: e.dma_start(
                        out=qT_sb[:], in_=qT_s[h]), dma=True)
                    for st in range(NTC):
                        nsc = 4 * (st + 1)
                        for i in range(4):
                            ti = st * 4 + i
                            nL = 128 * (ti + 1)
                            nsc_i = ti + 1
                            for kc in range(st + 1):
                                p.emit("tensor", lambda e, ti=ti, kc=kc: e.matmul(
                                    psum_s[:],
                                    qT_sb[:, ti * 128:(ti + 1) * 128],
                                    kT_sb[:, kc * 512:(kc + 1) * 512],
                                    start=True, stop=True))
                                if kc == st:
                                    p.emit("vector", lambda e, kc=kc, i=i:
                                           e.scalar_tensor_tensor(
                                        out=rowbuf[:, kc * 512:(kc + 1) * 512],
                                        in0=psum_s[:], scalar=SCALE,
                                        in1=cmask[:, i * 512:(i + 1) * 512],
                                        op0=ALU.mult, op1=ALU.add))
                                else:
                                    p.emit("scalar", lambda e, kc=kc: e.activation(
                                        out=rowbuf[:, kc * 512:(kc + 1) * 512],
                                        in_=psum_s[:], func=AF.Copy, scale=SCALE))
                            p.emit("vector", lambda e, st=st: e.reduce_max(
                                out=negmax[:], in_=rowbuf[:, 0:(st + 1) * 512],
                                axis=AX.X, negate=True))
                            p.emit("scalar", lambda e, st=st: e.activation(
                                out=pbuf[:, 0:(st + 1) * 512],
                                in_=rowbuf[:, 0:(st + 1) * 512], func=AF.Exp,
                                bias=negmax[:], accum_out=lsum[:]))
                            p.emit("vector", lambda e: e.reciprocal(rl[:], lsum[:]))
                            p.emit("vector", lambda e, st=st: e.tensor_single_scalar(
                                out=pbuf[:, 0:(st + 1) * 512],
                                in_=pbuf[:, 0:(st + 1) * 512],
                                scalar=rl[:], op=ALU.mult))
                            for scq in range(st + 1):
                                cnt = min(4, nsc_i - 4 * scq)
                                for u in range(cnt):
                                    sc = scq * 4 + u
                                    p.emit("tensor", lambda e, sc=sc, u=u: e.transpose(
                                        psum_pt[:, u * 128:(u + 1) * 128],
                                        pbuf[:, sc * 128:(sc + 1) * 128], ident[:]))
                                dst = ptbuf[:].rearrange(
                                    "p (c f) -> p c f", f=512)[
                                    :, scq * 4: scq * 4 + cnt, i * 128:(i + 1) * 128]
                                src = psum_pt[:, 0:cnt * 128].rearrange(
                                    "p (c f) -> p c f", f=128)
                                p.emit("vector", lambda e, dst=dst, src=src:
                                       e.tensor_copy(out=dst, in_=src))
                            for sc in range(nsc_i, nsc):
                                p.emit("vector", lambda e, sc=sc, i=i: e.memset(
                                    ptbuf[:, sc * 512 + i * 128: sc * 512 + (i + 1) * 128],
                                    0.0))
                        for sc in range(nsc):
                            p.emit("tensor", lambda e, sc=sc, hh=hh: e.matmul(
                                psum_z[:],
                                v_hp[:, sc * 256 + hh * 128: sc * 256 + (hh + 1) * 128],
                                ptbuf[:, sc * 512:(sc + 1) * 512],
                                start=(sc == 0), stop=(sc == nsc - 1)))
                        p.emit("vector", lambda e, h=h, st=st: e.tensor_copy(
                            out=zT_sb[:, (h * NTT + st * 4) * 128:
                                      (h * NTT + st * 4 + 4) * 128],
                            in_=psum_z[:]))
            if upto == "p2":
                return _finish(nc, p, dbg, root)

            wo_sb = sb1("wo_sb", [128, NHL * D], BF16)
            wol = rowbuf[:, 0:D]
            sstage = rowbuf[:, D:2 * D]
            psum_wo = ph.enter_context(nc.psum_tensor("psum_wo", [128, 512], F32))
            for hc in range(NHL):
                p.emit("sync", lambda e, hc=hc: e.dma_start(
                    out=wol[:], in_=woT[hc * 128:(hc + 1) * 128, :]), dma=True)
                p.emit("scalar", lambda e, hc=hc: e.copy(
                    out=wo_sb[:, hc * D:(hc + 1) * D], in_=wol[:]))
            for tt in range(NTT):
                for jc in range(4):
                    for hc in range(NHL):
                        p.emit("tensor", lambda e, tt=tt, jc=jc, hc=hc: e.matmul(
                            psum_wo[:],
                            zT_sb[:, (hc * NTT + tt) * 128:(hc * NTT + tt + 1) * 128],
                            wo_sb[:, hc * D + jc * 512: hc * D + (jc + 1) * 512],
                            start=(hc == 0), stop=(hc == NHL - 1)))
                    p.emit("vector", lambda e, jc=jc: e.tensor_copy(
                        out=sstage[:, jc * 512:(jc + 1) * 512], in_=psum_wo[:]))
                p.emit("sync", lambda e, tt=tt: e.dma_start(
                    out=s_part[tt * 128:(tt + 1) * 128, :], in_=sstage[:]), dma=True)
        p.emit("gpsimd", lambda e: e.collective_compute(
            "AllReduce", ALU.add, replica_groups=GROUPS,
            ins=[s_part.opt()], outs=[s_full.opt()]))
        if upto == "p2c":
            return _finish(nc, p, dbg, root)

        with ExitStack() as ph:
            sb1 = lambda n, s, dt: ph.enter_context(nc.sbuf_tensor(n, s, dt))
            xnT0 = sb1("xnT0", [128, NDC * 512], BF16)
            xnT1 = sb1("xnT1", [128, NDC * 512], BF16)
            w1q_sb = sb1("w1q_sb", [128, D], BF16)
            w2q_sb = sb1("w2q_sb", [128, D], BF16)
            w3q_sb = sb1("w3q_sb", [128, 512], BF16)
            hidT = sb1("hidT", [128, NHC * 1024], BF16)
            sig = sb1("sig", [128, 512], F32)
            fstage = sb1("fstage", [128, 512], F32)
            for pair in range(4):
                with ExitStack() as pp:
                    px1 = pp.enter_context(nc.psum_tensor(f"px1_{pair}", [128, 512], F32))
                    px2 = pp.enter_context(nc.psum_tensor(f"px2_{pair}", [128, 512], F32))
                    p.emit("sync", lambda e, pair=pair: e.dma_start(
                        out=xnT0[:], in_=xnT_s[2 * pair]), dma=True)
                    p.emit("sync", lambda e, pair=pair: e.dma_start(
                        out=xnT1[:], in_=xnT_s[2 * pair + 1]), dma=True)
                    for hc in range(NHC):
                        p.emit("sync", lambda e, hc=hc: e.dma_start(
                            out=w1q_sb[:], in_=w1q_s[hc]), dma=True)
                        p.emit("sync", lambda e, hc=hc: e.dma_start(
                            out=w2q_sb[:], in_=w2q_s[hc]), dma=True)
                        for sub, xnT_b in ((0, xnT0), (1, xnT1)):
                            for wq_b, px in ((w1q_sb, px1), (w2q_sb, px2)):
                                for dc in range(NDC):
                                    p.emit("tensor", lambda e, wq_b=wq_b, px=px,
                                           xnT_b=xnT_b, dc=dc: e.matmul(
                                        px[:],
                                        wq_b[:, dc * 128:(dc + 1) * 128],
                                        xnT_b[:, dc * 512:(dc + 1) * 512],
                                        start=(dc == 0), stop=(dc == NDC - 1)))
                            p.emit("scalar", lambda e, px1=px1: e.activation(
                                out=sig[:], in_=px1[:], func=AF.Sigmoid))
                            p.emit("vector", lambda e, hc=hc, sub=sub, px2=px2:
                                   e.tensor_mul(
                                hidT[:, hc * 1024 + sub * 512:
                                     hc * 1024 + (sub + 1) * 512],
                                sig[:], px2[:]))
                with ExitStack() as pp:
                    pf = [pp.enter_context(nc.psum_tensor(f"pf{pair}_{k}", [128, 512], F32))
                          for k in range(8)]
                    for jc in range(4):
                        for hc in range(NHC):
                            p.emit("sync", lambda e, hc=hc, jc=jc: e.dma_start(
                                out=w3q_sb[:],
                                in_=w3q_s[hc, :, jc * 512:(jc + 1) * 512]), dma=True)
                            for ttl in range(8):
                                p.emit("tensor", lambda e, hc=hc, ttl=ttl,
                                       pf_t=pf[ttl]: e.matmul(
                                    pf_t[:],
                                    hidT[:, hc * 1024 + ttl * 128:
                                         hc * 1024 + (ttl + 1) * 128],
                                    w3q_sb[:],
                                    start=(hc == 0), stop=(hc == NHC - 1)))
                        for ttl in range(8):
                            tt = pair * 8 + ttl
                            p.emit("vector", lambda e, pf_t=pf[ttl]: e.tensor_copy(
                                out=fstage[:], in_=pf_t[:]))
                            p.emit("sync", lambda e, tt=tt, jc=jc: e.dma_start(
                                out=f_part[tt * 128:(tt + 1) * 128,
                                           jc * 512:(jc + 1) * 512],
                                in_=fstage[:]), dma=True)
        p.emit("gpsimd", lambda e: e.collective_compute(
            "AllReduce", ALU.add, replica_groups=GROUPS,
            ins=[f_part.opt()], outs=[f_full.opt()]))
        if upto == "p3":
            return _finish(nc, p, dbg, root)

        with ExitStack() as ph:
            sb1 = lambda n, s, dt: ph.enter_context(nc.sbuf_tensor(n, s, dt))
            fsb = sb1("fsb", [128, D], F32)
            ssb = sb1("ssb", [128, D], F32)
            xsb = sb1("xsb", [128, D], F32)
            gsb = sb1("gsb", [128, D], F32)
            osb = sb1("osb", [128, D], F32)
            for tt in range(NTT):
                p.emit("sync", lambda e, tt=tt: e.dma_start(
                    out=fsb[:], in_=f_full[tt * 128:(tt + 1) * 128, :]), dma=True)
                p.emit("sync", lambda e, tt=tt: e.dma_start(
                    out=ssb[:], in_=s_full[tt * 128:(tt + 1) * 128, :]), dma=True)
                p.emit("sync", lambda e, tt=tt: e.dma_start(
                    out=xsb[:], in_=x[tt * 128:(tt + 1) * 128, :]), dma=True)
                p.emit("scalar", lambda e: e.activation(
                    out=gsb[:], in_=fsb[:], func=AF.Silu))
                p.emit("vector", lambda e: e.tensor_mul(gsb[:], gsb[:], ssb[:]))
                p.emit("vector", lambda e: e.tensor_add(osb[:], gsb[:], xsb[:]))
                p.emit("sync", lambda e, tt=tt: e.dma_start(
                    out=out[tt * 128:(tt + 1) * 128, :], in_=osb[:]), dma=True)
        return _finish(nc, p, dbg, root)


def _finish(nc, p, dbg, root):
    for name, (dbg_ap, src_ap) in dbg.items():
        p.emit("sync", lambda e, dbg_ap=dbg_ap, ap=src_ap: e.dma_start(
            out=dbg_ap, in_=ap), dma=True)
    cs = root.enter_context(nc.semaphore("cs"))
    ds = root.enter_context(nc.semaphore("ds"))
    sems = {"c": cs, "d": ds}
    blk = root.enter_context(nc.Block())

    def run_engine(key):
        def body(eng):
            for ekey, fn, wait, inc_spec in p.items:
                if ekey != key:
                    continue
                if wait is not None:
                    eng.wait_ge(sems[wait[0]], wait[1])
                inst = fn(eng)
                inst.then_inc(sems[inc_spec[0]], inc_spec[1])
            if key == "sync":
                eng.wait_ge(sems["c"], p.c)
                eng.wait_ge(sems["d"], p.d)
        return body

    blk.sync(run_engine("sync"))
    blk.tensor(run_engine("tensor"))
    blk.vector(run_engine("vector"))
    blk.scalar(run_engine("scalar"))
    blk.gpsimd(run_engine("gpsimd"))
    return nc, p


_NC_CACHE = {}


def _get_program():
    key = (UPTO, tuple(DEBUG_DUMPS))
    if key not in _NC_CACHE:
        _NC_CACHE[key] = build_program(UPTO, DEBUG_DUMPS)
    return _NC_CACHE[key]


def kernel(L0_attn_x, L0_attn_Wq, L0_attn_Wk, L0_attn_Wv, L0_attn_Wo,
           L0_ffn_w1, L0_ffn_w2, L0_ffn_w3):
    x = np.ascontiguousarray(np.asarray(L0_attn_x, dtype=np.float32))
    wq = np.asarray(L0_attn_Wq, dtype=np.float32)
    wk = np.asarray(L0_attn_Wk, dtype=np.float32)
    wv = np.asarray(L0_attn_Wv, dtype=np.float32)
    wo = np.asarray(L0_attn_Wo, dtype=np.float32)
    w1 = np.asarray(L0_ffn_w1, dtype=np.float32)
    w2 = np.asarray(L0_ffn_w2, dtype=np.float32)
    w3 = np.asarray(L0_ffn_w3, dtype=np.float32)

    wqT = np.ascontiguousarray(wq.T)
    wkT = np.ascontiguousarray(wk.T)
    wvT = np.ascontiguousarray(wv.T)
    woT = np.ascontiguousarray(wo.T)
    w1T = np.ascontiguousarray(w1.T)
    w2T = np.ascontiguousarray(w2.T)
    w3T = np.ascontiguousarray(w3.T)

    nc, _ = _get_program()
    in_maps = []
    for c in range(8):
        b, g = c // 2, c % 2
        in_maps.append({
            "x": np.ascontiguousarray(x[b]),
            "wqT": np.ascontiguousarray(wqT[:, g * 1024:(g + 1) * 1024]),
            "wkT": np.ascontiguousarray(wkT[:, g * 1024:(g + 1) * 1024]),
            "wvT": np.ascontiguousarray(wvT[:, g * 1024:(g + 1) * 1024]),
            "woT": np.ascontiguousarray(woT[g * 1024:(g + 1) * 1024, :]),
            "w1T": np.ascontiguousarray(w1T[:, g * HF:(g + 1) * HF]),
            "w2T": np.ascontiguousarray(w2T[:, g * HF:(g + 1) * HF]),
            "w3T": np.ascontiguousarray(w3T[g * HF:(g + 1) * HF, :]),
        })
    res = run_bass_kernel_spmd(nc, in_maps, core_ids=list(range(8)))
    out = np.empty((4, T, D), dtype=np.float32)
    for b in range(4):
        out[b] = res.results[2 * b]["out"]
    return out


# version 03
# speedup vs baseline: 2.1938x; 2.1938x over previous
import numpy as np
import ml_dtypes
from contextlib import ExitStack

import concourse.bass as bass
import concourse.mybir as mybir
from concourse.bass_utils import run_bass_kernel_spmd

F32 = mybir.dt.float32
BF16 = mybir.dt.bfloat16
AF = mybir.ActivationFunctionType
ALU = mybir.AluOpType
AX = mybir.AxisListType

T = 4096
D = 2048
NHL = 8
HF = 2816
NHC = HF // 128
NTT = T // 128
NTC = T // 512
NDC = D // 128
SCALE = 1.0 / float(np.sqrt(128.0))
LN_EPS = 1e-5
INV_D = 1.0 / D
Q_S1 = 0.5 / (5632.0 * 2048.0)
Q_S2 = 0.5e-8
GROUPS = [[0, 1], [2, 3], [4, 5], [6, 7]]

UPTO = "full"
FIFO_SKIP = True
DMA_STRICT = False
DEBUG_DUMPS = ()

QINFO = {
    "tensor": ("tensor", 1),
    "vector": ("vector", 1),
    "scalar": ("scalar", 1),
    "gpsimd": ("gpsimd", 1),
    "ld": ("sync", 16),
    "st": ("scalar", 16),
}
QUEUES = tuple(QINFO)


class Prog:
    def __init__(self):
        self.items = []
        self.res = {}
        self.pending = {q: set() for q in QUEUES}
        self.last_in_q = {q: None for q in QUEUES}

    def emit(self, q, fn, rd=(), wr=(), tag=""):
        idx = len(self.items)
        deps = set(self.pending[q])
        self.pending[q] = set()
        for k in rd:
            e = self.res.get(k)
            if e is not None and e["w"] is not None:
                deps.add(e["w"])
        for k in wr:
            e = self.res.get(k)
            if e is not None:
                if e["w"] is not None:
                    deps.add(e["w"])
                deps.update(e["r"].values())
        for k in rd:
            self.res.setdefault(k, {"w": None, "r": {}})["r"][q] = idx
        for k in wr:
            e = self.res.setdefault(k, {"w": None, "r": {}})
            e["w"] = idx
            e["r"] = {}
        deps.discard(idx)
        self.items.append({"q": q, "fn": fn, "deps": deps, "tag": tag})
        self.last_in_q[q] = idx
        return idx

    def barrier(self):
        lasts = {i for i in self.last_in_q.values() if i is not None}
        for q in QUEUES:
            self.pending[q] |= lasts
        self.res = {}


def _finish(nc, p, dbg, root):
    for name, (dbg_ap, src_ap) in dbg.items():
        p.emit("st", lambda e, dbg_ap=dbg_ap, ap=src_ap: e.dma_start(
            out=dbg_ap, in_=ap), rd=(name,))
    if dbg:
        pass

    items = p.items
    n = len(items)
    hard = [None] * n
    for idx, it in enumerate(items):
        my_eng = QINFO[it["q"]][0]
        hd = []
        for d in it["deps"]:
            dq = items[d]["q"]
            d_eng, d_inc = QINFO[dq][0], QINFO[dq][1]
            if FIFO_SKIP and d_inc == 1 and d_eng == my_eng == "tensor" \
                    and it["q"] == "tensor":
                continue
            hd.append(d)
        hard[idx] = hd
    needed = set()
    for hd in hard:
        needed.update(d for d in hd if QINFO[items[d]["q"]][1] == 1)
    NLANE = 8
    val = [0] * n
    sem_of = [None] * n
    cnt = {}
    dseq = {q: 0 for q in QUEUES}
    for idx, it in enumerate(items):
        q = it["q"]
        if QINFO[q][1] == 16:
            lane = dseq[q] % NLANE
            dseq[q] += 1
            sid = f"{q}{lane}"
            cnt[sid] = cnt.get(sid, 0) + 16
            val[idx] = cnt[sid]
            sem_of[idx] = sid
            it["inc"] = 16
        elif idx in needed:
            cnt[q] = cnt.get(q, 0) + 1
            val[idx] = cnt[q]
            sem_of[idx] = q
            it["inc"] = 1
        else:
            it["inc"] = 0
        it["sem"] = sem_of[idx]
    satisfied = {q: {} for q in QUEUES}
    waits = [[] for _ in range(n)]
    for idx, it in enumerate(items):
        q = it["q"]
        for d in sorted(hard[idx]):
            sid = sem_of[d]
            v = val[d]
            assert v > 0 and sid is not None
            if satisfied[q].get(sid, 0) >= v:
                continue
            waits[idx].append((sid, v))
            satisfied[q][sid] = v
        it["waits"] = waits[idx]

    sems = {}
    for sid in cnt:
        sems[sid] = root.enter_context(nc.semaphore("sem_" + sid))
    blk = root.enter_context(nc.Block())

    final_counts = dict(cnt)

    def run_body(body_eng):
        def body(eng):
            for it in items:
                if QINFO[it["q"]][0] != body_eng:
                    continue
                for (dq, v) in it["waits"]:
                    eng.wait_ge(sems[dq], v)
                inst = it["fn"](eng)
                if it["inc"]:
                    inst.then_inc(sems[it["sem"]], it["inc"])
            if body_eng == "sync":
                for sid, v in final_counts.items():
                    eng.wait_ge(sems[sid], v)
        return body

    blk.sync(run_body("sync"))
    blk.tensor(run_body("tensor"))
    blk.vector(run_body("vector"))
    blk.scalar(run_body("scalar"))
    blk.gpsimd(run_body("gpsimd"))
    return nc, p


def build_program(upto=UPTO, debug=()):
    nc = bass.Bass("TRN2", target_bir_lowering=False)

    x = nc.dram_tensor("x", [T, D], F32, kind="ExternalInput").ap()
    wqT = nc.dram_tensor("wqT", [D, 1024], F32, kind="ExternalInput").ap()
    wkT = nc.dram_tensor("wkT", [D, 1024], F32, kind="ExternalInput").ap()
    wvT = nc.dram_tensor("wvT", [D, 1024], F32, kind="ExternalInput").ap()
    woT = nc.dram_tensor("woT", [1024, D], F32, kind="ExternalInput").ap()
    w1T = nc.dram_tensor("w1T", [D, HF], F32, kind="ExternalInput").ap()
    w2T = nc.dram_tensor("w2T", [D, HF], F32, kind="ExternalInput").ap()
    w3T = nc.dram_tensor("w3T", [HF, D], F32, kind="ExternalInput").ap()
    out = nc.dram_tensor("out", [T, D], F32, kind="ExternalOutput").ap()

    xnT_s = nc.dram_tensor("xnT_s", [NTC, 128, NDC * 512], BF16, kind="Internal").ap()
    kT_s = nc.dram_tensor("kT_s", [NHL, 128, T], BF16, kind="Internal").ap()
    qT_s = nc.dram_tensor("qT_s", [NHL, 128, T], BF16, kind="Internal").ap()
    v_s = nc.dram_tensor("v_s", [NHL, NTT, 128, 128], BF16, kind="Internal").ap()
    w1q_s = nc.dram_tensor("w1q_s", [NHC, 128, D], BF16, kind="Internal").ap()
    w2q_s = nc.dram_tensor("w2q_s", [NHC, 128, D], BF16, kind="Internal").ap()
    w3q_s = nc.dram_tensor("w3q_s", [NHC, 128, D], BF16, kind="Internal").ap()
    s_part = nc.dram_tensor("s_part", [T, D], F32, kind="Internal").ap()
    s_full = nc.dram_tensor("s_full", [T, D], F32, kind="Internal").ap()
    f_part = nc.dram_tensor("f_part", [T, D], F32, kind="Internal").ap()
    f_full = nc.dram_tensor("f_full", [T, D], F32, kind="Internal").ap()
    gam_part = nc.dram_tensor("gam_part", [1, 4], F32, kind="Internal").ap()
    gam_full = nc.dram_tensor("gam_full", [1, 4], F32, kind="Internal").ap()
    colb = nc.dram_tensor("colb", [128], F32, kind="Internal").ap()

    dbg = {}
    scratch_map = {"xnT_s": xnT_s, "kT_s": kT_s, "qT_s": qT_s, "v_s": v_s,
                   "w1q_s": w1q_s, "w2q_s": w2q_s, "w3q_s": w3q_s,
                   "s_part": s_part, "s_full": s_full, "f_part": f_part,
                   "f_full": f_full, "gam_full": gam_full}
    for name in debug:
        if name not in scratch_map:
            continue
        src = scratch_map[name]
        dbg[name] = (nc.dram_tensor("dbg_" + name, list(src.shape),
                                    src.dtype, kind="ExternalOutput").ap(), src)

    ident_dram = nc.inline_tensor(np.eye(128, dtype=ml_dtypes.bfloat16), "ident_c")
    ones_dram = nc.inline_tensor(np.ones((1, 128), dtype=np.float32), "ones_c")
    _mk = np.zeros((4, 128, 512), dtype=np.float32)
    for _i in range(4):
        _p = np.arange(128)[:, None]
        _f = np.arange(512)[None, :]
        _mk[_i] = np.where(128 * _i + _p - _f >= 0, 0.0, -1e30)
    cmask_dram = nc.inline_tensor(_mk.astype(ml_dtypes.bfloat16), "cmask_c")

    p = Prog()

    with ExitStack() as root:
        sb = lambda n_, s_, dt: root.enter_context(nc.sbuf_tensor(n_, s_, dt))
        ident = sb("ident_sb", [128, 128], BF16)
        ones_f = sb("ones_f", [1, 128], F32)
        h_bc = sb("h_bc", [128, 4], F32)
        hn_bc = sb("hn_bc", [128, 4], F32)
        cmask = sb("cmask_sb", [128, 4 * 512], BF16)

        p.emit("ld", lambda e: e.dma_start(out=ident[:], in_=ident_dram.ap()),
               wr=("ident",))
        p.emit("ld", lambda e: e.dma_start(out=ones_f[:], in_=ones_dram.ap()),
               wr=("ones",))
        p.emit("ld", lambda e: e.dma_start(
            out=cmask[:].rearrange("p (c f) -> p c f", c=4),
            in_=cmask_dram.ap().rearrange("c p f -> p c f")), wr=("cmask",))

        with ExitStack() as ph:
            sb1 = lambda n_, s_, dt: ph.enter_context(nc.sbuf_tensor(n_, s_, dt))
            xt = [sb1(f"xt{j}", [128, D], F32) for j in range(2)]
            xsq = sb1("xsq", [128, D], F32)
            xn4 = [sb1(f"xn{i}", [128, D], BF16) for i in range(4)]
            xnTc = [sb1(f"xnTc{j}", [128, NDC * 512], BF16) for j in range(2)]
            lnv = [sb1(f"lnv{j}", [128, 8], F32) for j in range(2)]
            ptr = [ph.enter_context(nc.psum_tensor(f"ptr{j}", [128, 512], BF16))
                   for j in range(2)]

            for tc in range(NTC):
                cp = tc % 2
                for i in range(4):
                    ti = tc * 4 + i
                    j = ti % 2
                    xb, lv, xi = xt[j], lnv[j], xn4[i]
                    kx, kl, kxn = f"xt{j}", f"lnv{j}", f"xn{i}"
                    nm, ssq, t1v = lv[:, 0:1], lv[:, 1:2], lv[:, 2:3]
                    msq, veps, stdv, rstd = lv[:, 3:4], lv[:, 4:5], lv[:, 5:6], lv[:, 6:7]
                    p.emit("ld", lambda e, ti=ti, xb=xb: e.dma_start(
                        out=xb[:], in_=x[ti * 128:(ti + 1) * 128, :]), wr=(kx,))
                    p.emit("vector", lambda e, xb=xb, nm=nm: e.reduce_sum(
                        out=nm, in_=xb[:], axis=AX.X, negate=True),
                        rd=(kx,), wr=(kl,))
                    p.emit("vector", lambda e, nm=nm: e.tensor_scalar_mul(
                        nm, nm, INV_D), rd=(kl,), wr=(kl,))
                    p.emit("scalar", lambda e, xb=xb, ssq=ssq: e.activation(
                        out=xsq[:], in_=xb[:], func=AF.Square, accum_out=ssq),
                        rd=(kx,), wr=("xsq", kl + "s"))
                    p.emit("vector", lambda e, ssq=ssq, t1v=t1v: e.tensor_scalar(
                        out=t1v, in0=ssq, scalar1=INV_D, scalar2=LN_EPS,
                        op0=ALU.mult, op1=ALU.add), rd=(kl + "s",), wr=(kl,))
                    p.emit("vector", lambda e, nm=nm, msq=msq: e.tensor_single_scalar(
                        out=msq, in_=nm, scalar=nm, op=ALU.mult), rd=(kl,), wr=(kl,))
                    p.emit("vector", lambda e, t1v=t1v, msq=msq, veps=veps:
                           e.tensor_sub(veps, t1v, msq), rd=(kl,), wr=(kl,))
                    p.emit("scalar", lambda e, veps=veps, stdv=stdv: e.activation(
                        out=stdv, in_=veps, func=AF.Sqrt), rd=(kl,), wr=(kl + "s",))
                    p.emit("vector", lambda e, stdv=stdv, rstd=rstd: e.reciprocal(
                        rstd, stdv), rd=(kl + "s",), wr=(kl,))
                    p.emit("vector", lambda e, xb=xb, nm=nm, rstd=rstd, xi=xi:
                           e.tensor_scalar(
                        out=xi[:], in0=xb[:], scalar1=nm, scalar2=rstd,
                        op0=ALU.add, op1=ALU.mult), rd=(kx, kl), wr=(kxn,))
                for dc in range(NDC):
                    pj = dc % 2
                    for i in range(4):
                        p.emit("tensor", lambda e, i=i, dc=dc, pj=pj: e.transpose(
                            ptr[pj][:, i * 128:(i + 1) * 128],
                            xn4[i][:, dc * 128:(dc + 1) * 128], ident[:]),
                            rd=(f"xn{i}", "ident"), wr=(f"ptr{pj}",))
                    p.emit("vector", lambda e, dc=dc, pj=pj, cp=cp: e.tensor_copy(
                        out=xnTc[cp][:, dc * 512:(dc + 1) * 512], in_=ptr[pj][:]),
                        rd=(f"ptr{pj}",), wr=(f"xnTc{cp}", f"ptr{pj}"))
                p.emit("st", lambda e, tc=tc, cp=cp: e.dma_start(
                    out=xnT_s[tc], in_=xnTc[cp][:]), rd=(f"xnTc{cp}",))
        p.barrier()
        if upto == "p1a":
            return _finish(nc, p, dbg, root)

        with ExitStack() as ph:
            sb1 = lambda n_, s_, dt: ph.enter_context(nc.sbuf_tensor(n_, s_, dt))
            wq_sb = sb1("wq_sb", [128, NDC * 1024], BF16)
            wk_sb = sb1("wk_sb", [128, NDC * 1024], BF16)
            wv_sb = sb1("wv_sb", [128, NDC * 1024], BF16)
            wload = [sb1(f"wload{j}", [128, 2816], F32) for j in range(2)]

            li = 0
            for wname, wdst, wsrc in (("q", wq_sb, wqT), ("k", wk_sb, wkT),
                                      ("v", wv_sb, wvT)):
                for dc in range(NDC):
                    j = li % 2
                    li += 1
                    p.emit("ld", lambda e, wsrc=wsrc, dc=dc, j=j: e.dma_start(
                        out=wload[j][:, 0:1024],
                        in_=wsrc[dc * 128:(dc + 1) * 128, :]), wr=(f"wload{j}",))
                    p.emit("scalar", lambda e, wdst=wdst, dc=dc, j=j: e.copy(
                        out=wdst[:, dc * 1024:(dc + 1) * 1024],
                        in_=wload[j][:, 0:1024]),
                        rd=(f"wload{j}",), wr=("w" + wname, f"wload{j}"))

            gcols = sb1("gcols", [128, 24], F32)
            gcol = sb1("gcol", [128, 1], F32)
            grow = sb1("grow", [1, 128], F32)
            gpart = sb1("gpart", [1, 4], F32)
            p.emit("vector", lambda e: e.memset(gpart[:], 0.0), wr=("gpart",))
            for wi, wsrc in enumerate((w1T, w2T, w3T)):
                nblk = NDC if wi < 2 else NHC
                ncol = HF if wi < 2 else D
                for blk_i in range(nblk):
                    j = li % 2
                    li += 1
                    p.emit("ld", lambda e, wsrc=wsrc, blk_i=blk_i, ncol=ncol, j=j:
                           e.dma_start(
                        out=wload[j][:, 0:ncol],
                        in_=wsrc[blk_i * 128:(blk_i + 1) * 128, :]), wr=(f"wload{j}",))
                    p.emit("vector", lambda e, blk_i=blk_i, ncol=ncol, j=j:
                           e.reduce_sum(
                        out=gcols[:, blk_i:blk_i + 1], in_=wload[j][:, 0:ncol],
                        axis=AX.X, apply_absolute_value=True),
                        rd=(f"wload{j}",), wr=("gcols", f"wload{j}"))
                p.emit("vector", lambda e, nblk=nblk: e.reduce_sum(
                    out=gcol[:], in_=gcols[:, 0:nblk], axis=AX.X),
                    rd=("gcols",), wr=("gcol",))
                p.emit("st", lambda e: e.dma_start(out=colb, in_=gcol[:]),
                       rd=("gcol",), wr=("colb",))
                p.emit("ld", lambda e: e.dma_start(
                    out=grow[:], in_=colb.rearrange("(o p) -> o p", o=1)),
                    rd=("colb",), wr=("grow",))
                p.emit("vector", lambda e, wi=wi: e.reduce_sum(
                    out=gpart[:, wi:wi + 1], in_=grow[:], axis=AX.X),
                    rd=("grow",), wr=("gpart", "grow"))
            p.emit("st", lambda e: e.dma_start(out=gam_part, in_=gpart[:]),
                   rd=("gpart",), wr=("gam_part",))
            p.emit("gpsimd", lambda e: e.collective_compute(
                "AllReduce", ALU.add, replica_groups=GROUPS,
                ins=[gam_part.opt()], outs=[gam_full.opt()]),
                rd=("gam_part",), wr=("gam_full",))
            grow4 = sb1("grow4", [1, 4], F32)
            hrow = sb1("hrow", [1, 4], F32)
            hnrow = sb1("hnrow", [1, 4], F32)
            p.emit("ld", lambda e: e.dma_start(out=grow4[:], in_=gam_full),
                   rd=("gam_full",), wr=("grow4",))
            p.emit("vector", lambda e: e.tensor_scalar(
                out=hrow[:], in0=grow4[:], scalar1=Q_S1, scalar2=Q_S2,
                op0=ALU.mult, op1=ALU.add), rd=("grow4",), wr=("hrow",))
            p.emit("vector", lambda e: e.tensor_scalar_mul(hnrow[:], hrow[:], -1.0),
                   rd=("hrow",), wr=("hnrow",))
            pbc = ph.enter_context(nc.psum_tensor("pbc", [128, 4], F32))
            p.emit("tensor", lambda e: e.matmul(
                pbc[:], ones_f[:], hrow[:], start=True, stop=True),
                rd=("ones", "hrow"), wr=("pbc",))
            p.emit("vector", lambda e: e.tensor_copy(out=h_bc[:], in_=pbc[:]),
                   rd=("pbc",), wr=("h_bc", "pbc"))
            p.emit("tensor", lambda e: e.matmul(
                pbc[:], ones_f[:], hnrow[:], start=True, stop=True),
                rd=("ones", "hnrow"), wr=("pbc",))
            p.emit("vector", lambda e: e.tensor_copy(out=hn_bc[:], in_=pbc[:]),
                   rd=("pbc",), wr=("hn_bc", "pbc"))

            wfp = [sb1(f"wfp{j}", [128, D], F32) for j in range(2)]
            tneg = [sb1(f"tneg{j}", [128, D], BF16) for j in range(2)]
            wqt = [sb1(f"wqt{j}", [128, D], BF16) for j in range(2)]
            qi = 0
            for wi, (wsrc, wdst_s) in enumerate(((w1T, w1q_s), (w2T, w2q_s),
                                                 (w3T, w3q_s))):
                for hc in range(NHC):
                    j = qi % 2
                    qi += 1
                    if wi < 2:
                        src = wsrc[:, hc * 128:(hc + 1) * 128].rearrange(
                            "(c p) f -> p c f", p=128)
                        dst = wfp[j][:].rearrange("p (c f) -> p c f", c=NDC)
                        p.emit("ld", lambda e, src=src, dst=dst: e.dma_start(
                            out=dst, in_=src), wr=(f"wfp{j}",))
                    else:
                        p.emit("ld", lambda e, wsrc=wsrc, hc=hc, j=j: e.dma_start(
                            out=wfp[j][:], in_=wsrc[hc * 128:(hc + 1) * 128, :]),
                            wr=(f"wfp{j}",))
                    p.emit("vector", lambda e, wi=wi, j=j: e.tensor_single_scalar(
                        out=tneg[j][:], in_=wfp[j][:], scalar=hn_bc[:, wi:wi + 1],
                        op=ALU.is_lt), rd=(f"wfp{j}", "hn_bc"), wr=(f"tneg{j}",))
                    p.emit("vector", lambda e, wi=wi, j=j: e.scalar_tensor_tensor(
                        out=wqt[j][:], in0=wfp[j][:], scalar=h_bc[:, wi:wi + 1],
                        in1=tneg[j][:], op0=ALU.is_gt, op1=ALU.subtract),
                        rd=(f"wfp{j}", f"tneg{j}", "h_bc"), wr=(f"wqt{j}", f"wfp{j}"))
                    p.emit("st", lambda e, wdst_s=wdst_s, hc=hc, j=j: e.dma_start(
                        out=wdst_s[hc], in_=wqt[j][:]), rd=(f"wqt{j}",))
            if upto == "quant":
                p.barrier()
                return _finish(nc, p, dbg, root)

            xnT_sb = [sb1(f"xnT_sb{j}", [128, NDC * 512], BF16) for j in range(2)]
            pstage = [sb1(f"pstage{j}", [128, 512], BF16) for j in range(2)]
            pproj = [ph.enter_context(nc.psum_tensor(f"pproj{j}", [128, 512], F32))
                     for j in range(2)]
            ei = 0
            for tc in range(NTC):
                cj = tc % 2
                p.emit("ld", lambda e, tc=tc, cj=cj: e.dma_start(
                    out=xnT_sb[cj][:], in_=xnT_s[tc]), rd=("xnTw",), wr=(f"xnT{cj}",))
                for wkey, wsb, dst_s in (("wk", wk_sb, kT_s), ("wq", wq_sb, qT_s)):
                    for oc in range(NHL):
                        j = ei % 2
                        ei += 1
                        for dc in range(NDC):
                            p.emit("tensor", lambda e, wsb=wsb, oc=oc, dc=dc, j=j,
                                   cj=cj: e.matmul(
                                pproj[j][:],
                                wsb[:, dc * 1024 + oc * 128: dc * 1024 + (oc + 1) * 128],
                                xnT_sb[cj][:, dc * 512:(dc + 1) * 512],
                                start=(dc == 0), stop=(dc == NDC - 1)),
                                rd=(wkey, f"xnT{cj}"), wr=(f"pproj{j}",))
                        p.emit("vector", lambda e, j=j: e.tensor_copy(
                            out=pstage[j][:], in_=pproj[j][:]),
                            rd=(f"pproj{j}",), wr=(f"pstage{j}", f"pproj{j}"))
                        p.emit("st", lambda e, dst_s=dst_s, oc=oc, tc=tc, j=j:
                               e.dma_start(
                            out=dst_s[oc, :, tc * 512:(tc + 1) * 512],
                            in_=pstage[j][:]), rd=(f"pstage{j}",))
                for sv in range(4):
                    st_g = tc * 4 + sv
                    for dvc in range(2):
                        j = ei % 2
                        ei += 1
                        for dc in range(NDC):
                            p.emit("tensor", lambda e, sv=sv, dvc=dvc, dc=dc, j=j,
                                   cj=cj: e.matmul(
                                pproj[j][:],
                                xnT_sb[cj][:, dc * 512 + sv * 128: dc * 512 + (sv + 1) * 128],
                                wv_sb[:, dc * 1024 + dvc * 512: dc * 1024 + (dvc + 1) * 512],
                                start=(dc == 0), stop=(dc == NDC - 1)),
                                rd=("wv", f"xnT{cj}"), wr=(f"pproj{j}",))
                        p.emit("vector", lambda e, j=j: e.tensor_copy(
                            out=pstage[j][:], in_=pproj[j][:]),
                            rd=(f"pproj{j}",), wr=(f"pstage{j}", f"pproj{j}"))
                        for u in range(4):
                            p.emit("st", lambda e, dvc=dvc, u=u, st_g=st_g, j=j:
                                   e.dma_start(
                                out=v_s[4 * dvc + u, st_g],
                                in_=pstage[j][:, u * 128:(u + 1) * 128]),
                                rd=(f"pstage{j}",))
        p.barrier()
        if upto == "proj":
            return _finish(nc, p, dbg, root)

        with ExitStack() as ph:
            sb1 = lambda n_, s_, dt: ph.enter_context(nc.sbuf_tensor(n_, s_, dt))
            zT_sb = sb1("zT_sb", [128, NHL * NTT * 128], BF16)
            rowbuf = [sb1(f"rowbuf{j}", [128, T], F32) for j in range(2)]
            pa = ph.enter_context(ExitStack())
            sb2 = lambda n_, s_, dt: pa.enter_context(nc.sbuf_tensor(n_, s_, dt))
            v_hb = [sb2(f"v_hb{j}", [128, NTT * 128], BF16) for j in range(2)]
            kT_sb = [sb2(f"kT_sb{j}", [128, T], BF16) for j in range(2)]
            qT_sb = [sb2(f"qT_sb{j}", [128, T], BF16) for j in range(2)]
            pbuf = [sb2(f"pbuf{j}", [128, T], BF16) for j in range(2)]
            ptbuf = sb2("ptbuf", [128, NTT * 512], BF16)
            smv = [sb2(f"smv{j}", [128, 4], F32) for j in range(2)]
            psum_s = [pa.enter_context(nc.psum_tensor(f"psum_s{j}", [128, 512], F32))
                      for j in range(4)]
            psum_pt = [pa.enter_context(nc.psum_tensor(f"psum_pt{j}", [128, 512], BF16))
                       for j in range(2)]
            psum_z = [pa.enter_context(nc.psum_tensor(f"psum_z{j}", [128, 512], F32))
                      for j in range(2)]
            s_ctr = [0]

            for h in range(NHL):
                hj = h % 2
                p.emit("ld", lambda e, h=h, hj=hj: e.dma_start(
                    out=v_hb[hj][:].rearrange("p (c f) -> p c f", c=NTT),
                    in_=v_s[h].rearrange("c p f -> p c f")), wr=(f"v{hj}",))
                if True:
                    vj = hj
                    hh = 0
                    p.emit("ld", lambda e, h=h, hj=hj: e.dma_start(
                        out=kT_sb[hj][:], in_=kT_s[h]), wr=(f"k{hj}",))
                    p.emit("ld", lambda e, h=h, hj=hj: e.dma_start(
                        out=qT_sb[hj][:], in_=qT_s[h]), wr=(f"q{hj}",))

                    def tile_S(ti, tj, hj):
                        st = ti // 4
                        i = ti % 4
                        for kc in range(st + 1):
                            sj = s_ctr[0] % 4
                            s_ctr[0] += 1
                            p.emit("tensor", lambda e, ti=ti, kc=kc, sj=sj, hj=hj:
                                   e.matmul(
                                psum_s[sj][:],
                                qT_sb[hj][:, ti * 128:(ti + 1) * 128],
                                kT_sb[hj][:, kc * 512:(kc + 1) * 512],
                                start=True, stop=True),
                                rd=(f"q{hj}", f"k{hj}"), wr=(f"ps{sj}",))
                            if kc == st:
                                p.emit("vector", lambda e, kc=kc, i=i, tj=tj, sj=sj:
                                       e.scalar_tensor_tensor(
                                    out=rowbuf[tj][:, kc * 512:(kc + 1) * 512],
                                    in0=psum_s[sj][:], scalar=SCALE,
                                    in1=cmask[:, i * 512:(i + 1) * 512],
                                    op0=ALU.mult, op1=ALU.add),
                                    rd=(f"ps{sj}", "cmask"),
                                    wr=(f"row{tj}", f"ps{sj}"))
                            else:
                                p.emit("scalar", lambda e, kc=kc, tj=tj, sj=sj:
                                       e.activation(
                                    out=rowbuf[tj][:, kc * 512:(kc + 1) * 512],
                                    in_=psum_s[sj][:], func=AF.Copy, scale=SCALE),
                                    rd=(f"ps{sj}",), wr=(f"row{tj}", f"ps{sj}"))

                    def tile_soft(ti, tj):
                        st = ti // 4
                        negmax = smv[tj][:, 0:1]
                        lsum = smv[tj][:, 1:2]
                        rl = smv[tj][:, 2:3]
                        p.emit("vector", lambda e, st=st, tj=tj, negmax=negmax:
                               e.reduce_max(
                            out=negmax, in_=rowbuf[tj][:, st * 512:(st + 1) * 512],
                            axis=AX.X, negate=True),
                            rd=(f"row{tj}",), wr=(f"smv{tj}",))
                        p.emit("scalar", lambda e, st=st, tj=tj, negmax=negmax,
                               lsum=lsum: e.activation(
                            out=pbuf[tj][:, 0:(st + 1) * 512],
                            in_=rowbuf[tj][:, 0:(st + 1) * 512], func=AF.Exp,
                            bias=negmax, accum_out=lsum),
                            rd=(f"row{tj}", f"smv{tj}"), wr=(f"pb{tj}", f"smv{tj}s"))
                        p.emit("vector", lambda e, rl=rl, lsum=lsum: e.reciprocal(
                            rl, lsum), rd=(f"smv{tj}s",), wr=(f"smv{tj}",))
                        p.emit("vector", lambda e, st=st, tj=tj, rl=rl:
                               e.tensor_single_scalar(
                            out=pbuf[tj][:, 0:(st + 1) * 512],
                            in_=pbuf[tj][:, 0:(st + 1) * 512],
                            scalar=rl, op=ALU.mult),
                            rd=(f"pb{tj}", f"smv{tj}"), wr=(f"pb{tj}",))

                    def tile_pt(ti, tj):
                        st = ti // 4
                        i = ti % 4
                        nsc_i = ti + 1
                        nsc = 4 * (st + 1)
                        for scq in range(st + 1):
                            cnt = min(4, nsc_i - 4 * scq)
                            pj = scq % 2
                            for u in range(cnt):
                                sc = scq * 4 + u
                                p.emit("tensor", lambda e, sc=sc, u=u, tj=tj, pj=pj:
                                       e.transpose(
                                    psum_pt[pj][:, u * 128:(u + 1) * 128],
                                    pbuf[tj][:, sc * 128:(sc + 1) * 128], ident[:]),
                                    rd=(f"pb{tj}", "ident"), wr=(f"ppt{pj}",))
                            dst = ptbuf[:].rearrange("p (c f) -> p c f", f=512)[
                                :, scq * 4: scq * 4 + cnt, i * 128:(i + 1) * 128]
                            src = psum_pt[pj][:, 0:cnt * 128].rearrange(
                                "p (c f) -> p c f", f=128)
                            p.emit("vector", lambda e, dst=dst, src=src:
                                   e.tensor_copy(out=dst, in_=src),
                                   rd=(f"ppt{pj}",), wr=("ptbuf", f"ppt{pj}"))
                        for sc in range(nsc_i, nsc):
                            p.emit("vector", lambda e, sc=sc, i=i: e.memset(
                                ptbuf[:, sc * 512 + i * 128: sc * 512 + (i + 1) * 128],
                                0.0), wr=("ptbuf",))

                    for st in range(NTC):
                        nsc = 4 * (st + 1)
                        zj = st % 2
                        for i in range(4):
                            ti = st * 4 + i
                            tj = ti % 2
                            tile_S(ti, tj, hj)
                            if i > 0:
                                tprev = ti - 1
                                tile_soft(tprev, tprev % 2)
                                tile_pt(tprev, tprev % 2)
                        ti = st * 4 + 3
                        tile_soft(ti, ti % 2)
                        tile_pt(ti, ti % 2)
                        for sc in range(nsc):
                            p.emit("tensor", lambda e, sc=sc, vj=vj, zj=zj:
                                   e.matmul(
                                psum_z[zj][:],
                                v_hb[vj][:, sc * 128:(sc + 1) * 128],
                                ptbuf[:, sc * 512:(sc + 1) * 512],
                                start=(sc == 0), stop=(sc == nsc - 1)),
                                rd=(f"v{vj}", "ptbuf"), wr=(f"pz{zj}",))
                        p.emit("vector", lambda e, h=h, st=st, zj=zj: e.tensor_copy(
                            out=zT_sb[:, (h * NTT + st * 4) * 128:
                                      (h * NTT + st * 4 + 4) * 128],
                            in_=psum_z[zj][:]),
                            rd=(f"pz{zj}",), wr=("zT", f"pz{zj}"))
            if upto == "p2":
                p.barrier()
                return _finish(nc, p, dbg, root)

            pa.close()
            p.barrier()
            wo_sb = sb1("wo_sb", [128, NHL * D], BF16)
            wol = rowbuf[0][:, 0:D]
            sstage = [rowbuf[1][:, 0:D], rowbuf[1][:, D:2 * D]]
            psum_wo = [ph.enter_context(
                nc.psum_tensor(f"psum_wo{j}", [128, 512], F32)) for j in range(2)]
            for hc in range(NHL):
                p.emit("ld", lambda e, hc=hc: e.dma_start(
                    out=wol, in_=woT[hc * 128:(hc + 1) * 128, :]), wr=("wol",))
                p.emit("scalar", lambda e, hc=hc: e.copy(
                    out=wo_sb[:, hc * D:(hc + 1) * D], in_=wol),
                    rd=("wol",), wr=("wo", "wol"))
            for tt in range(NTT):
                sj = tt % 2
                for jc in range(4):
                    wj = jc % 2
                    for hc in range(NHL):
                        p.emit("tensor", lambda e, tt=tt, jc=jc, hc=hc, wj=wj:
                               e.matmul(
                            psum_wo[wj][:],
                            zT_sb[:, (hc * NTT + tt) * 128:(hc * NTT + tt + 1) * 128],
                            wo_sb[:, hc * D + jc * 512: hc * D + (jc + 1) * 512],
                            start=(hc == 0), stop=(hc == NHL - 1)),
                            rd=("zT", "wo"), wr=(f"pwo{wj}",))
                    p.emit("vector", lambda e, jc=jc, wj=wj, sj=sj: e.tensor_copy(
                        out=sstage[sj][:, jc * 512:(jc + 1) * 512],
                        in_=psum_wo[wj][:]),
                        rd=(f"pwo{wj}",), wr=(f"sst{sj}", f"pwo{wj}"))
                p.emit("st", lambda e, tt=tt, sj=sj: e.dma_start(
                    out=s_part[tt * 128:(tt + 1) * 128, :], in_=sstage[sj][:]),
                    rd=(f"sst{sj}",))
        p.barrier()
        p.emit("gpsimd", lambda e: e.collective_compute(
            "AllReduce", ALU.add, replica_groups=GROUPS,
            ins=[s_part.opt()], outs=[s_full.opt()]),
            rd=("s_part@post",), wr=("s_full",))
        if upto == "p2c":
            return _finish(nc, p, dbg, root)

        with ExitStack() as ph:
            sb1 = lambda n_, s_, dt: ph.enter_context(nc.sbuf_tensor(n_, s_, dt))
            xnT0 = sb1("xnT0", [128, NDC * 512], BF16)
            xnT1 = sb1("xnT1", [128, NDC * 512], BF16)
            w1q_sb = [sb1(f"w1q_sb{j}", [128, D], BF16) for j in range(2)]
            w2q_sb = [sb1(f"w2q_sb{j}", [128, D], BF16) for j in range(2)]
            w3q_sb = [sb1(f"w3q_sb{j}", [128, 512], BF16) for j in range(2)]
            hidT = sb1("hidT", [128, NHC * 1024], BF16)
            sig = [sb1(f"sig{j}", [128, 512], F32) for j in range(2)]
            fstage = [sb1(f"fstage{j}", [128, 512], F32) for j in range(2)]
            px1 = [ph.enter_context(nc.psum_tensor(f"px1{j}", [128, 512], F32))
                   for j in range(2)]
            px2 = [ph.enter_context(nc.psum_tensor(f"px2{j}", [128, 512], F32))
                   for j in range(2)]
            pf = [ph.enter_context(nc.psum_tensor(f"pf{k}", [128, 512], F32))
                  for k in range(4)]
            for pair in range(4):
                p.emit("ld", lambda e, pair=pair: e.dma_start(
                    out=xnT0[:], in_=xnT_s[2 * pair]), wr=("x0",))
                p.emit("ld", lambda e, pair=pair: e.dma_start(
                    out=xnT1[:], in_=xnT_s[2 * pair + 1]), wr=("x1",))
                for hc in range(NHC):
                    wj = hc % 2
                    p.emit("ld", lambda e, hc=hc, wj=wj: e.dma_start(
                        out=w1q_sb[wj][:], in_=w1q_s[hc]), wr=(f"w1{wj}",))
                    p.emit("ld", lambda e, hc=hc, wj=wj: e.dma_start(
                        out=w2q_sb[wj][:], in_=w2q_s[hc]), wr=(f"w2{wj}",))
                    for sub, (xnT_b, xk) in ((0, (xnT0, "x0")), (1, (xnT1, "x1"))):
                        sj = (2 * hc + sub) % 2
                        for wq_b, wkk, px in ((w1q_sb[wj], f"w1{wj}", px1[sj]),
                                              (w2q_sb[wj], f"w2{wj}", px2[sj])):
                            for dc in range(NDC):
                                p.emit("tensor", lambda e, wq_b=wq_b, px=px,
                                       xnT_b=xnT_b, dc=dc: e.matmul(
                                    px[:],
                                    wq_b[:, dc * 128:(dc + 1) * 128],
                                    xnT_b[:, dc * 512:(dc + 1) * 512],
                                    start=(dc == 0), stop=(dc == NDC - 1)),
                                    rd=(wkk, xk),
                                    wr=(("px1" if px in px1 else "px2") + str(sj),))
                        p.emit("scalar", lambda e, px=px1[sj], sg=sig[sj]:
                               e.activation(
                            out=sg[:], in_=px[:], func=AF.Sigmoid),
                            rd=(f"px1{sj}",), wr=(f"sig{sj}", f"px1{sj}"))
                        p.emit("vector", lambda e, hc=hc, sub=sub, px=px2[sj],
                               sg=sig[sj]: e.tensor_mul(
                            hidT[:, hc * 1024 + sub * 512:
                                 hc * 1024 + (sub + 1) * 512],
                            sg[:], px[:]),
                            rd=(f"sig{sj}", f"px2{sj}"), wr=("hidT", f"px2{sj}"),
                            tag=f"hid_p{pair}_hc{hc}_s{sub}")
                if pair == 0 and "hid0" in debug:
                    hd = nc.dram_tensor("dbg_hid0", [128, NHC * 1024], BF16,
                                        kind="ExternalOutput").ap()
                    p.emit("st", lambda e, hd=hd: e.dma_start(
                        out=hd, in_=hidT[:]), rd=("hidT",))
                for jc in range(4):
                    for g2 in range(2):
                        for hc in range(NHC):
                            wj = hc % 2
                            p.emit("ld", lambda e, hc=hc, jc=jc, wj=wj:
                                   e.dma_start(
                                out=w3q_sb[wj][:],
                                in_=w3q_s[hc, :, jc * 512:(jc + 1) * 512]),
                                wr=(f"w3{wj}",))
                            for tg in range(4):
                                ttl = g2 * 4 + tg
                                p.emit("tensor", lambda e, hc=hc, ttl=ttl,
                                       pfk=pf[ttl % 4], wj=wj: e.matmul(
                                    pfk[:],
                                    hidT[:, hc * 1024 + ttl * 128:
                                         hc * 1024 + (ttl + 1) * 128],
                                    w3q_sb[wj][:],
                                    start=(hc == 0), stop=(hc == NHC - 1)),
                                    rd=("hidT", f"w3{wj}"),
                                    wr=(f"pf{ttl % 4}",),
                                    tag=f"w3mm_p{pair}_jc{jc}_g{g2}_hc{hc}_t{tg}")
                        for tg in range(4):
                            ttl = g2 * 4 + tg
                            tt = pair * 8 + ttl
                            fj = ttl % 2
                            p.emit("vector", lambda e, pfk=pf[ttl % 4], fj=fj:
                                   e.tensor_copy(out=fstage[fj][:], in_=pfk[:]),
                                rd=(f"pf{ttl % 4}",), wr=(f"fst{fj}", f"pf{ttl % 4}"))
                            p.emit("st", lambda e, tt=tt, jc=jc, fj=fj: e.dma_start(
                                out=f_part[tt * 128:(tt + 1) * 128,
                                           jc * 512:(jc + 1) * 512],
                                in_=fstage[fj][:]), rd=(f"fst{fj}",))
        p.barrier()
        p.emit("gpsimd", lambda e: e.collective_compute(
            "AllReduce", ALU.add, replica_groups=GROUPS,
            ins=[f_part.opt()], outs=[f_full.opt()]),
            rd=("f_part@post",), wr=("f_full",))
        if upto == "p3":
            return _finish(nc, p, dbg, root)

        with ExitStack() as ph:
            sb1 = lambda n_, s_, dt: ph.enter_context(nc.sbuf_tensor(n_, s_, dt))
            fsb = [sb1(f"fsb{j}", [128, D], F32) for j in range(2)]
            ssb = [sb1(f"ssb{j}", [128, D], F32) for j in range(2)]
            xsb = [sb1(f"xsb{j}", [128, D], F32) for j in range(2)]
            gsb = [sb1(f"gsb{j}", [128, D], F32) for j in range(2)]
            osb = [sb1(f"osb{j}", [128, D], F32) for j in range(2)]
            for tt in range(NTT):
                j = tt % 2
                p.emit("ld", lambda e, tt=tt, j=j: e.dma_start(
                    out=fsb[j][:], in_=f_full[tt * 128:(tt + 1) * 128, :]),
                    rd=("f_full",), wr=(f"fsb{j}",))
                p.emit("ld", lambda e, tt=tt, j=j: e.dma_start(
                    out=ssb[j][:], in_=s_full[tt * 128:(tt + 1) * 128, :]),
                    rd=("s_full",), wr=(f"ssb{j}",))
                p.emit("ld", lambda e, tt=tt, j=j: e.dma_start(
                    out=xsb[j][:], in_=x[tt * 128:(tt + 1) * 128, :]),
                    wr=(f"xsb{j}",))
                p.emit("scalar", lambda e, j=j: e.activation(
                    out=gsb[j][:], in_=fsb[j][:], func=AF.Silu),
                    rd=(f"fsb{j}",), wr=(f"gsb{j}", f"fsb{j}"))
                p.emit("vector", lambda e, j=j: e.tensor_mul(
                    gsb[j][:], gsb[j][:], ssb[j][:]),
                    rd=(f"gsb{j}", f"ssb{j}"), wr=(f"gsb{j}",))
                p.emit("vector", lambda e, j=j: e.tensor_add(
                    osb[j][:], gsb[j][:], xsb[j][:]),
                    rd=(f"gsb{j}", f"xsb{j}"), wr=(f"osb{j}",))
                p.emit("st", lambda e, tt=tt, j=j: e.dma_start(
                    out=out[tt * 128:(tt + 1) * 128, :], in_=osb[j][:]),
                    rd=(f"osb{j}",))
        return _finish(nc, p, dbg, root)


_NC_CACHE = {}


def _get_program():
    key = (UPTO, tuple(DEBUG_DUMPS))
    if key not in _NC_CACHE:
        _NC_CACHE[key] = build_program(UPTO, DEBUG_DUMPS)
    return _NC_CACHE[key]


def kernel(L0_attn_x, L0_attn_Wq, L0_attn_Wk, L0_attn_Wv, L0_attn_Wo,
           L0_ffn_w1, L0_ffn_w2, L0_ffn_w3):
    x = np.ascontiguousarray(np.asarray(L0_attn_x, dtype=np.float32))
    wq = np.asarray(L0_attn_Wq, dtype=np.float32)
    wk = np.asarray(L0_attn_Wk, dtype=np.float32)
    wv = np.asarray(L0_attn_Wv, dtype=np.float32)
    wo = np.asarray(L0_attn_Wo, dtype=np.float32)
    w1 = np.asarray(L0_ffn_w1, dtype=np.float32)
    w2 = np.asarray(L0_ffn_w2, dtype=np.float32)
    w3 = np.asarray(L0_ffn_w3, dtype=np.float32)

    wqT = np.ascontiguousarray(wq.T)
    wkT = np.ascontiguousarray(wk.T)
    wvT = np.ascontiguousarray(wv.T)
    woT = np.ascontiguousarray(wo.T)
    w1T = np.ascontiguousarray(w1.T)
    w2T = np.ascontiguousarray(w2.T)
    w3T = np.ascontiguousarray(w3.T)

    nc, _ = _get_program()
    in_maps = []
    for c in range(8):
        b, g = c // 2, c % 2
        in_maps.append({
            "x": np.ascontiguousarray(x[b]),
            "wqT": np.ascontiguousarray(wqT[:, g * 1024:(g + 1) * 1024]),
            "wkT": np.ascontiguousarray(wkT[:, g * 1024:(g + 1) * 1024]),
            "wvT": np.ascontiguousarray(wvT[:, g * 1024:(g + 1) * 1024]),
            "woT": np.ascontiguousarray(woT[g * 1024:(g + 1) * 1024, :]),
            "w1T": np.ascontiguousarray(w1T[:, g * HF:(g + 1) * HF]),
            "w2T": np.ascontiguousarray(w2T[:, g * HF:(g + 1) * HF]),
            "w3T": np.ascontiguousarray(w3T[g * HF:(g + 1) * HF, :]),
        })
    res = run_bass_kernel_spmd(nc, in_maps, core_ids=list(range(8)))
    out = np.empty((4, T, D), dtype=np.float32)
    for b in range(4):
        out[b] = res.results[2 * b]["out"]
    return out


# version 04
# speedup vs baseline: 2.5610x; 1.1674x over previous
import numpy as np
import ml_dtypes
from contextlib import ExitStack

import concourse.bass as bass
import concourse.mybir as mybir
from concourse.bass_utils import run_bass_kernel_spmd

F32 = mybir.dt.float32
BF16 = mybir.dt.bfloat16
AF = mybir.ActivationFunctionType
ALU = mybir.AluOpType
AX = mybir.AxisListType

T = 4096
D = 2048
NHL = 8
HF = 2816
NHC = HF // 128
NTT = T // 128
NTC = T // 512
NDC = D // 128
SCALE = 1.0 / float(np.sqrt(128.0))
LN_EPS = 1e-5
INV_D = 1.0 / D
Q_S1 = 0.5 / (5632.0 * 2048.0)
Q_S2 = 0.5e-8
GROUPS = [[0, 1], [2, 3], [4, 5], [6, 7]]

UPTO = "full"
FIFO_SKIP = True
DMA_STRICT = False
DEBUG_DUMPS = ()

QINFO = {
    "tensor": ("tensor", 1),
    "vector": ("vector", 1),
    "scalar": ("scalar", 1),
    "gpsimd": ("gpsimd", 1),
    "ld": ("sync", 16),
    "st": ("scalar", 16),
}
QUEUES = tuple(QINFO)


class Prog:
    def __init__(self):
        self.items = []
        self.res = {}
        self.pending = {q: set() for q in QUEUES}
        self.last_in_q = {q: None for q in QUEUES}

    def emit(self, q, fn, rd=(), wr=(), tag="", after=()):
        idx = len(self.items)
        deps = set(self.pending[q])
        deps.update(after)
        self.pending[q] = set()
        for k in rd:
            e = self.res.get(k)
            if e is not None and e["w"] is not None:
                deps.add(e["w"])
        for k in wr:
            e = self.res.get(k)
            if e is not None:
                if e["w"] is not None:
                    deps.add(e["w"])
                deps.update(e["r"].values())
        for k in rd:
            self.res.setdefault(k, {"w": None, "r": {}})["r"][q] = idx
        for k in wr:
            e = self.res.setdefault(k, {"w": None, "r": {}})
            e["w"] = idx
            e["r"] = {}
        deps.discard(idx)
        self.items.append({"q": q, "fn": fn, "deps": deps, "tag": tag})
        self.last_in_q[q] = idx
        return idx

    def barrier(self):
        lasts = {i for i in self.last_in_q.values() if i is not None}
        for q in QUEUES:
            self.pending[q] |= lasts
        self.res = {}


def _finish(nc, p, dbg, root):
    for name, (dbg_ap, src_ap) in dbg.items():
        p.emit("st", lambda e, dbg_ap=dbg_ap, ap=src_ap: e.dma_start(
            out=dbg_ap, in_=ap), rd=(name,))
    if dbg:
        pass

    items = p.items
    n = len(items)
    hard = [None] * n
    for idx, it in enumerate(items):
        my_eng = QINFO[it["q"]][0]
        hd = []
        for d in it["deps"]:
            dq = items[d]["q"]
            d_eng, d_inc = QINFO[dq][0], QINFO[dq][1]
            if FIFO_SKIP and d_inc == 1 and d_eng == my_eng == "tensor" \
                    and it["q"] == "tensor":
                continue
            hd.append(d)
        hard[idx] = hd
    needed = set()
    for hd in hard:
        needed.update(d for d in hd if QINFO[items[d]["q"]][1] == 1)
    NLANE = 8
    val = [0] * n
    sem_of = [None] * n
    cnt = {}
    dseq = {q: 0 for q in QUEUES}
    for idx, it in enumerate(items):
        q = it["q"]
        if QINFO[q][1] == 16:
            lane = dseq[q] % NLANE
            dseq[q] += 1
            sid = f"{q}{lane}"
            cnt[sid] = cnt.get(sid, 0) + 16
            val[idx] = cnt[sid]
            sem_of[idx] = sid
            it["inc"] = 16
        elif idx in needed:
            cnt[q] = cnt.get(q, 0) + 1
            val[idx] = cnt[q]
            sem_of[idx] = q
            it["inc"] = 1
        else:
            it["inc"] = 0
        it["sem"] = sem_of[idx]
    satisfied = {q: {} for q in QUEUES}
    waits = [[] for _ in range(n)]
    for idx, it in enumerate(items):
        q = it["q"]
        for d in sorted(hard[idx]):
            sid = sem_of[d]
            v = val[d]
            assert v > 0 and sid is not None
            if satisfied[q].get(sid, 0) >= v:
                continue
            waits[idx].append((sid, v))
            satisfied[q][sid] = v
        it["waits"] = waits[idx]

    sems = {}
    for sid in cnt:
        sems[sid] = root.enter_context(nc.semaphore("sem_" + sid))
    blk = root.enter_context(nc.Block())

    final_counts = dict(cnt)

    def run_body(body_eng):
        def body(eng):
            for it in items:
                if QINFO[it["q"]][0] != body_eng:
                    continue
                for (dq, v) in it["waits"]:
                    eng.wait_ge(sems[dq], v)
                inst = it["fn"](eng)
                if it["inc"]:
                    inst.then_inc(sems[it["sem"]], it["inc"])
            if body_eng == "sync":
                for sid, v in final_counts.items():
                    eng.wait_ge(sems[sid], v)
        return body

    blk.sync(run_body("sync"))
    blk.tensor(run_body("tensor"))
    blk.vector(run_body("vector"))
    blk.scalar(run_body("scalar"))
    blk.gpsimd(run_body("gpsimd"))
    return nc, p


def build_program(upto=UPTO, debug=()):
    nc = bass.Bass("TRN2", target_bir_lowering=False)

    x = nc.dram_tensor("x", [T, D], F32, kind="ExternalInput").ap()
    wqT = nc.dram_tensor("wqT", [D, 1024], F32, kind="ExternalInput").ap()
    wkT = nc.dram_tensor("wkT", [D, 1024], F32, kind="ExternalInput").ap()
    wvT = nc.dram_tensor("wvT", [D, 1024], F32, kind="ExternalInput").ap()
    woT = nc.dram_tensor("woT", [1024, D], F32, kind="ExternalInput").ap()
    w1T = nc.dram_tensor("w1T", [D, HF], F32, kind="ExternalInput").ap()
    w2T = nc.dram_tensor("w2T", [D, HF], F32, kind="ExternalInput").ap()
    w3T = nc.dram_tensor("w3T", [HF, D], F32, kind="ExternalInput").ap()
    out = nc.dram_tensor("out", [T, D], F32, kind="ExternalOutput").ap()

    xnT_s = nc.dram_tensor("xnT_s", [NTC, 128, NDC * 512], BF16, kind="Internal").ap()
    kT_s = nc.dram_tensor("kT_s", [NHL, 128, T], BF16, kind="Internal").ap()
    qT_s = nc.dram_tensor("qT_s", [NHL, 128, T], BF16, kind="Internal").ap()
    v_s = nc.dram_tensor("v_s", [NHL, NTT, 128, 128], BF16, kind="Internal").ap()
    w1q_s = nc.dram_tensor("w1q_s", [NHC, 128, D], BF16, kind="Internal").ap()
    w2q_s = nc.dram_tensor("w2q_s", [NHC, 128, D], BF16, kind="Internal").ap()
    w3q_s = nc.dram_tensor("w3q_s", [NHC, 128, D], BF16, kind="Internal").ap()
    s_part = nc.dram_tensor("s_part", [T, D], F32, kind="Internal").ap()
    s_full = nc.dram_tensor("s_full", [T, D], F32, kind="Internal").ap()
    f_part = nc.dram_tensor("f_part", [T, D], F32, kind="Internal").ap()
    f_full = nc.dram_tensor("f_full", [T, D], F32, kind="Internal").ap()
    gam_part = nc.dram_tensor("gam_part", [1, 4], F32, kind="Internal").ap()
    gam_full = nc.dram_tensor("gam_full", [1, 4], F32, kind="Internal").ap()
    colb = nc.dram_tensor("colb", [128], F32, kind="Internal").ap()

    dbg = {}
    scratch_map = {"xnT_s": xnT_s, "kT_s": kT_s, "qT_s": qT_s, "v_s": v_s,
                   "w1q_s": w1q_s, "w2q_s": w2q_s, "w3q_s": w3q_s,
                   "s_part": s_part, "s_full": s_full, "f_part": f_part,
                   "f_full": f_full, "gam_full": gam_full}
    for name in debug:
        if name not in scratch_map:
            continue
        src = scratch_map[name]
        dbg[name] = (nc.dram_tensor("dbg_" + name, list(src.shape),
                                    src.dtype, kind="ExternalOutput").ap(), src)

    ident_dram = nc.inline_tensor(np.eye(128, dtype=ml_dtypes.bfloat16), "ident_c")
    ones_dram = nc.inline_tensor(np.ones((1, 128), dtype=np.float32), "ones_c")
    _mk = np.zeros((4, 128, 512), dtype=np.float32)
    for _i in range(4):
        _p = np.arange(128)[:, None]
        _f = np.arange(512)[None, :]
        _mk[_i] = np.where(128 * _i + _p - _f >= 0, 1.0, 0.0)
    cmask_dram = nc.inline_tensor(_mk.astype(ml_dtypes.bfloat16), "cmask_c")

    p = Prog()

    with ExitStack() as root:
        sb = lambda n_, s_, dt: root.enter_context(nc.sbuf_tensor(n_, s_, dt))
        ident = sb("ident_sb", [128, 128], BF16)
        ones_f = sb("ones_f", [1, 128], F32)
        h_bc = sb("h_bc", [128, 4], F32)
        hn_bc = sb("hn_bc", [128, 4], F32)
        cmask = sb("cmask_sb", [128, 4 * 512], BF16)

        p.emit("ld", lambda e: e.dma_start(out=ident[:], in_=ident_dram.ap()),
               wr=("ident",))
        p.emit("ld", lambda e: e.dma_start(out=ones_f[:], in_=ones_dram.ap()),
               wr=("ones",))
        p.emit("ld", lambda e: e.dma_start(
            out=cmask[:].rearrange("p (c f) -> p c f", c=4),
            in_=cmask_dram.ap().rearrange("c p f -> p c f")), wr=("cmask",))

        with ExitStack() as ph:
            sb1 = lambda n_, s_, dt: ph.enter_context(nc.sbuf_tensor(n_, s_, dt))
            xt = [sb1(f"xt{j}", [128, D], F32) for j in range(2)]
            xsq = sb1("xsq", [128, D], F32)
            xn4 = [sb1(f"xn{i}", [128, D], BF16) for i in range(4)]
            xnTc = [sb1(f"xnTc{j}", [128, NDC * 512], BF16) for j in range(2)]
            lnv = [sb1(f"lnv{j}", [128, 8], F32) for j in range(2)]
            ptr = [ph.enter_context(nc.psum_tensor(f"ptr{j}", [128, 512], BF16))
                   for j in range(2)]

            for tc in range(NTC):
                cp = tc % 2
                for i in range(4):
                    ti = tc * 4 + i
                    j = ti % 2
                    xb, lv, xi = xt[j], lnv[j], xn4[i]
                    kx, kl, kxn = f"xt{j}", f"lnv{j}", f"xn{i}"
                    nm, ssq, t1v = lv[:, 0:1], lv[:, 1:2], lv[:, 2:3]
                    msq, veps, stdv, rstd = lv[:, 3:4], lv[:, 4:5], lv[:, 5:6], lv[:, 6:7]
                    p.emit("ld", lambda e, ti=ti, xb=xb: e.dma_start(
                        out=xb[:], in_=x[ti * 128:(ti + 1) * 128, :]), wr=(kx,))
                    p.emit("vector", lambda e, xb=xb, nm=nm: e.reduce_sum(
                        out=nm, in_=xb[:], axis=AX.X, negate=True),
                        rd=(kx,), wr=(kl,))
                    p.emit("vector", lambda e, nm=nm: e.tensor_scalar_mul(
                        nm, nm, INV_D), rd=(kl,), wr=(kl,))
                    p.emit("scalar", lambda e, xb=xb, ssq=ssq: e.activation(
                        out=xsq[:], in_=xb[:], func=AF.Square, accum_out=ssq),
                        rd=(kx,), wr=("xsq", kl + "s"))
                    p.emit("vector", lambda e, ssq=ssq, t1v=t1v: e.tensor_scalar(
                        out=t1v, in0=ssq, scalar1=INV_D, scalar2=LN_EPS,
                        op0=ALU.mult, op1=ALU.add), rd=(kl + "s",), wr=(kl,))
                    p.emit("vector", lambda e, nm=nm, msq=msq: e.tensor_single_scalar(
                        out=msq, in_=nm, scalar=nm, op=ALU.mult), rd=(kl,), wr=(kl,))
                    p.emit("vector", lambda e, t1v=t1v, msq=msq, veps=veps:
                           e.tensor_sub(veps, t1v, msq), rd=(kl,), wr=(kl,))
                    p.emit("scalar", lambda e, veps=veps, stdv=stdv: e.activation(
                        out=stdv, in_=veps, func=AF.Sqrt), rd=(kl,), wr=(kl + "s",))
                    p.emit("vector", lambda e, stdv=stdv, rstd=rstd: e.reciprocal(
                        rstd, stdv), rd=(kl + "s",), wr=(kl,))
                    p.emit("vector", lambda e, xb=xb, nm=nm, rstd=rstd, xi=xi:
                           e.tensor_scalar(
                        out=xi[:], in0=xb[:], scalar1=nm, scalar2=rstd,
                        op0=ALU.add, op1=ALU.mult), rd=(kx, kl), wr=(kxn,))
                for dc in range(NDC):
                    pj = dc % 2
                    for i in range(4):
                        p.emit("tensor", lambda e, i=i, dc=dc, pj=pj: e.transpose(
                            ptr[pj][:, i * 128:(i + 1) * 128],
                            xn4[i][:, dc * 128:(dc + 1) * 128], ident[:]),
                            rd=(f"xn{i}", "ident"), wr=(f"ptr{pj}",))
                    p.emit("vector", lambda e, dc=dc, pj=pj, cp=cp: e.tensor_copy(
                        out=xnTc[cp][:, dc * 512:(dc + 1) * 512], in_=ptr[pj][:]),
                        rd=(f"ptr{pj}",), wr=(f"xnTc{cp}", f"ptr{pj}"))
                p.emit("st", lambda e, tc=tc, cp=cp: e.dma_start(
                    out=xnT_s[tc], in_=xnTc[cp][:]), rd=(f"xnTc{cp}",))
            gw = [sb1(f"gw{j}", [128, 2816], F32) for j in range(2)]
            gcols = sb1("gcols", [128, 40], F32)
            gcol = sb1("gcol", [128, 1], F32)
            grow = sb1("grow", [1, 128], F32)
            gpart = sb1("gpart", [1, 4], F32)
            p.emit("vector", lambda e: e.memset(gpart[:], 0.0), wr=("gpart",))
            li = 0
            for wi, wsrc in enumerate((w1T, w2T, w3T)):
                nblk = NDC if wi < 2 else NHC
                ncol = HF if wi < 2 else D
                for blk_i in range(nblk):
                    j = li % 2
                    li += 1
                    p.emit("ld", lambda e, wsrc=wsrc, blk_i=blk_i, ncol=ncol, j=j:
                           e.dma_start(
                        out=gw[j][:, 0:ncol],
                        in_=wsrc[blk_i * 128:(blk_i + 1) * 128, :]), wr=(f"gw{j}",))
                    p.emit("vector", lambda e, blk_i=blk_i, ncol=ncol, j=j:
                           e.reduce_sum(
                        out=gcols[:, blk_i:blk_i + 1],
                        in_=gw[j][:, 0:ncol],
                        axis=AX.X, apply_absolute_value=True),
                        rd=(f"gw{j}",), wr=(f"gcols{blk_i}", f"gw{j}"))
                p.emit("vector", lambda e, nblk=nblk: e.reduce_sum(
                    out=gcol[:], in_=gcols[:, 0:nblk], axis=AX.X),
                    rd=tuple(f"gcols{b2}" for b2 in range(nblk)), wr=("gcol",))
                p.emit("st", lambda e: e.dma_start(out=colb, in_=gcol[:]),
                       rd=("gcol",), wr=("colb",))
                p.emit("ld", lambda e: e.dma_start(
                    out=grow[:], in_=colb.rearrange("(o p) -> o p", o=1)),
                    rd=("colb",), wr=("grow",))
                p.emit("vector", lambda e, wi=wi: e.reduce_sum(
                    out=gpart[:, wi:wi + 1], in_=grow[:], axis=AX.X),
                    rd=("grow",), wr=("gpart", "grow", "gcol"))
            p.emit("st", lambda e: e.dma_start(out=gam_part, in_=gpart[:]),
                   rd=("gpart",), wr=("gam_part",))
            p.emit("gpsimd", lambda e: e.collective_compute(
                "AllReduce", ALU.add, replica_groups=GROUPS,
                ins=[gam_part.opt()], outs=[gam_full.opt()]),
                rd=("gam_part",), wr=("gam_full",))
        p.barrier()
        if upto == "p1a":
            return _finish(nc, p, dbg, root)

        with ExitStack() as ph:
            sb1 = lambda n_, s_, dt: ph.enter_context(nc.sbuf_tensor(n_, s_, dt))
            wq_sb = sb1("wq_sb", [128, NDC * 1024], BF16)
            wk_sb = sb1("wk_sb", [128, NDC * 1024], BF16)
            wv_sb = sb1("wv_sb", [128, NDC * 1024], BF16)
            wload = [sb1(f"wload{j}", [128, 2816], F32) for j in range(2)]
            li = 0
            for wname, wdst, wsrc in (("q", wq_sb, wqT), ("k", wk_sb, wkT),
                                      ("v", wv_sb, wvT)):
                for dc in range(NDC):
                    j = li % 2
                    li += 1
                    p.emit("ld", lambda e, wsrc=wsrc, dc=dc, j=j: e.dma_start(
                        out=wload[j][:, 0:1024],
                        in_=wsrc[dc * 128:(dc + 1) * 128, :]), wr=(f"wload{j}",))
                    p.emit("scalar", lambda e, wdst=wdst, dc=dc, j=j: e.copy(
                        out=wdst[:, dc * 1024:(dc + 1) * 1024],
                        in_=wload[j][:, 0:1024]),
                        rd=(f"wload{j}",), wr=("w" + wname, f"wload{j}"))

            grow4 = sb1("grow4", [1, 4], F32)
            hrow = sb1("hrow", [1, 4], F32)
            hnrow = sb1("hnrow", [1, 4], F32)
            p.emit("ld", lambda e: e.dma_start(out=grow4[:], in_=gam_full),
                   rd=("gam_full",), wr=("grow4",))
            p.emit("vector", lambda e: e.tensor_scalar(
                out=hrow[:], in0=grow4[:], scalar1=Q_S1, scalar2=Q_S2,
                op0=ALU.mult, op1=ALU.add), rd=("grow4",), wr=("hrow",))
            p.emit("vector", lambda e: e.tensor_scalar_mul(hnrow[:], hrow[:], -1.0),
                   rd=("hrow",), wr=("hnrow",))
            pbc = ph.enter_context(nc.psum_tensor("pbc", [128, 4], F32))
            p.emit("tensor", lambda e: e.matmul(
                pbc[:], ones_f[:], hrow[:], start=True, stop=True),
                rd=("ones", "hrow"), wr=("pbc",))
            p.emit("vector", lambda e: e.tensor_copy(out=h_bc[:], in_=pbc[:]),
                   rd=("pbc",), wr=("h_bc", "pbc"))
            p.emit("tensor", lambda e: e.matmul(
                pbc[:], ones_f[:], hnrow[:], start=True, stop=True),
                rd=("ones", "hnrow"), wr=("pbc",))
            p.emit("vector", lambda e: e.tensor_copy(out=hn_bc[:], in_=pbc[:]),
                   rd=("pbc",), wr=("hn_bc", "pbc"))

            wfp = [sb1(f"wfp{j}", [128, D], F32) for j in range(2)]
            tneg = [sb1(f"tneg{j}", [128, D], BF16) for j in range(2)]
            wqt = [sb1(f"wqt{j}", [128, D], BF16) for j in range(2)]
            quant_work = []
            qi = 0
            for wi, (wsrc, wdst_s) in enumerate(((w1T, w1q_s), (w2T, w2q_s),
                                                 (w3T, w3q_s))):
                for hc in range(NHC):
                    j = qi % 2
                    qi += 1

                    def q_block(wi=wi, wsrc=wsrc, wdst_s=wdst_s, hc=hc, j=j):
                        if wi < 2:
                            src = wsrc[:, hc * 128:(hc + 1) * 128].rearrange(
                                "(c p) f -> p c f", p=128)
                            dst = wfp[j][:].rearrange("p (c f) -> p c f", c=NDC)
                            p.emit("ld", lambda e, src=src, dst=dst: e.dma_start(
                                out=dst, in_=src), wr=(f"wfp{j}",))
                        else:
                            p.emit("ld", lambda e: e.dma_start(
                                out=wfp[j][:], in_=wsrc[hc * 128:(hc + 1) * 128, :]),
                                wr=(f"wfp{j}",))
                        p.emit("vector", lambda e: e.tensor_single_scalar(
                            out=tneg[j][:], in_=wfp[j][:], scalar=hn_bc[:, wi:wi + 1],
                            op=ALU.is_lt), rd=(f"wfp{j}", "hn_bc"), wr=(f"tneg{j}",))
                        p.emit("vector", lambda e: e.scalar_tensor_tensor(
                            out=wqt[j][:], in0=wfp[j][:], scalar=h_bc[:, wi:wi + 1],
                            in1=tneg[j][:], op0=ALU.is_gt, op1=ALU.subtract),
                            rd=(f"wfp{j}", f"tneg{j}", "h_bc"),
                            wr=(f"wqt{j}", f"wfp{j}"))
                        p.emit("st", lambda e: e.dma_start(
                            out=wdst_s[hc], in_=wqt[j][:]), rd=(f"wqt{j}",))
                    quant_work.append(q_block)
            if upto == "quant":
                for qb in quant_work:
                    qb()
                p.barrier()
                return _finish(nc, p, dbg, root)

            xnT_sb = [sb1(f"xnT_sb{j}", [128, NDC * 512], BF16) for j in range(2)]
            pstage = [sb1(f"pstage{j}", [128, 512], BF16) for j in range(2)]
            pproj = [ph.enter_context(nc.psum_tensor(f"pproj{j}", [128, 512], F32))
                     for j in range(2)]
            ei = 0
            nq = (len(quant_work) + NTC - 1) // NTC
            for tc in range(NTC):
                cj = tc % 2
                p.emit("ld", lambda e, tc=tc, cj=cj: e.dma_start(
                    out=xnT_sb[cj][:], in_=xnT_s[tc]), rd=("xnTw",), wr=(f"xnT{cj}",))
                for qb in quant_work[tc * nq:(tc + 1) * nq]:
                    qb()
                for wkey, wsb, dst_s in (("wk", wk_sb, kT_s), ("wq", wq_sb, qT_s)):
                    for oc in range(NHL):
                        j = ei % 2
                        ei += 1
                        for dc in range(NDC):
                            p.emit("tensor", lambda e, wsb=wsb, oc=oc, dc=dc, j=j,
                                   cj=cj: e.matmul(
                                pproj[j][:],
                                wsb[:, dc * 1024 + oc * 128: dc * 1024 + (oc + 1) * 128],
                                xnT_sb[cj][:, dc * 512:(dc + 1) * 512],
                                start=(dc == 0), stop=(dc == NDC - 1)),
                                rd=(wkey, f"xnT{cj}"), wr=(f"pproj{j}",))
                        p.emit("vector", lambda e, j=j: e.tensor_copy(
                            out=pstage[j][:], in_=pproj[j][:]),
                            rd=(f"pproj{j}",), wr=(f"pstage{j}", f"pproj{j}"))
                        p.emit("st", lambda e, dst_s=dst_s, oc=oc, tc=tc, j=j:
                               e.dma_start(
                            out=dst_s[oc, :, tc * 512:(tc + 1) * 512],
                            in_=pstage[j][:]), rd=(f"pstage{j}",))
                for sv in range(4):
                    st_g = tc * 4 + sv
                    for dvc in range(2):
                        j = ei % 2
                        ei += 1
                        for dc in range(NDC):
                            p.emit("tensor", lambda e, sv=sv, dvc=dvc, dc=dc, j=j,
                                   cj=cj: e.matmul(
                                pproj[j][:],
                                xnT_sb[cj][:, dc * 512 + sv * 128: dc * 512 + (sv + 1) * 128],
                                wv_sb[:, dc * 1024 + dvc * 512: dc * 1024 + (dvc + 1) * 512],
                                start=(dc == 0), stop=(dc == NDC - 1)),
                                rd=("wv", f"xnT{cj}"), wr=(f"pproj{j}",))
                        p.emit("vector", lambda e, j=j: e.tensor_copy(
                            out=pstage[j][:], in_=pproj[j][:]),
                            rd=(f"pproj{j}",), wr=(f"pstage{j}", f"pproj{j}"))
                        for u in range(4):
                            p.emit("st", lambda e, dvc=dvc, u=u, st_g=st_g, j=j:
                                   e.dma_start(
                                out=v_s[4 * dvc + u, st_g],
                                in_=pstage[j][:, u * 128:(u + 1) * 128]),
                                rd=(f"pstage{j}",))
        p.barrier()
        if upto == "proj":
            return _finish(nc, p, dbg, root)

        with ExitStack() as ph:
            sb1 = lambda n_, s_, dt: ph.enter_context(nc.sbuf_tensor(n_, s_, dt))
            zT_sb = sb1("zT_sb", [128, NHL * NTT * 128], BF16)
            pa = ph.enter_context(ExitStack())
            sb2 = lambda n_, s_, dt: pa.enter_context(nc.sbuf_tensor(n_, s_, dt))
            v_hb = [sb2(f"v_hb{j}", [128, NTT * 128], BF16) for j in range(2)]
            kT_sb = [sb2(f"kT_sb{j}", [128, T], BF16) for j in range(2)]
            qT_sb = [sb2(f"qT_sb{j}", [128, T], BF16) for j in range(2)]
            pbuf = [sb2(f"pbuf{j}", [128, T], BF16) for j in range(2)]
            ptbuf = sb2("ptbuf", [128, NTT * 512], BF16)
            smv = [sb2(f"smv{j}", [128, 4], F32) for j in range(2)]
            dtmp = [sb2(f"dtmp{j}", [128, 512], BF16) for j in range(2)]
            lsums = [sb2(f"lsums{j}", [128, 8], F32) for j in range(2)]
            psum_s = [pa.enter_context(nc.psum_tensor(f"psum_s{j}", [128, 512], F32))
                      for j in range(4)]
            psum_pt = [pa.enter_context(nc.psum_tensor(f"psum_pt{j}", [128, 512], BF16))
                       for j in range(2)]
            psum_z = [pa.enter_context(nc.psum_tensor(f"psum_z{j}", [128, 512], F32))
                      for j in range(2)]
            s_ctr = [0]

            for h in range(NHL):
                hj = h % 2
                p.emit("ld", lambda e, h=h, hj=hj: e.dma_start(
                    out=v_hb[hj][:].rearrange("p (c f) -> p c f", c=NTT),
                    in_=v_s[h].rearrange("c p f -> p c f")), wr=(f"v{hj}",))
                if True:
                    vj = hj
                    hh = 0
                    p.emit("ld", lambda e, h=h, hj=hj: e.dma_start(
                        out=kT_sb[hj][:], in_=kT_s[h]), wr=(f"k{hj}",))
                    p.emit("ld", lambda e, h=h, hj=hj: e.dma_start(
                        out=qT_sb[hj][:], in_=qT_s[h]), wr=(f"q{hj}",))

                    def tile_S(ti, tj, hj):
                        st = ti // 4
                        i = ti % 4
                        for kc in range(st + 1):
                            sj = s_ctr[0] % 4
                            s_ctr[0] += 1
                            p.emit("tensor", lambda e, ti=ti, kc=kc, sj=sj, hj=hj:
                                   e.matmul(
                                psum_s[sj][:],
                                qT_sb[hj][:, ti * 128:(ti + 1) * 128],
                                kT_sb[hj][:, kc * 512:(kc + 1) * 512],
                                start=True, stop=True),
                                rd=(f"q{hj}", f"k{hj}"), wr=(f"ps{sj}",))
                            if kc == st:
                                p.emit("scalar", lambda e, tj=tj, sj=sj:
                                       e.activation(
                                    out=dtmp[tj][:], in_=psum_s[sj][:],
                                    func=AF.Exp, scale=SCALE),
                                    rd=(f"ps{sj}",), wr=(f"dt{tj}", f"ps{sj}"))
                                p.emit("vector", lambda e, kc=kc, i=i, tj=tj, st=st:
                                       e.scalar_tensor_tensor(
                                    out=pbuf[tj][:, kc * 512:(kc + 1) * 512],
                                    in0=dtmp[tj][:], scalar=1.0,
                                    in1=cmask[:, i * 512:(i + 1) * 512],
                                    op0=ALU.mult, op1=ALU.mult,
                                    accum_out=lsums[tj][:, st:st + 1]),
                                    rd=(f"dt{tj}", "cmask"),
                                    wr=(f"pb{tj}", f"ls{tj}"))
                            else:
                                p.emit("scalar", lambda e, kc=kc, tj=tj, sj=sj:
                                       e.activation(
                                    out=pbuf[tj][:, kc * 512:(kc + 1) * 512],
                                    in_=psum_s[sj][:], func=AF.Exp, scale=SCALE,
                                    accum_out=lsums[tj][:, kc:kc + 1]),
                                    rd=(f"ps{sj}",),
                                    wr=(f"pb{tj}", f"ls{tj}", f"ps{sj}"))

                    def tile_soft(ti, tj):
                        st = ti // 4
                        lsum = smv[tj][:, 1:2]
                        rl = smv[tj][:, 2:3]
                        p.emit("vector", lambda e, st=st, tj=tj, lsum=lsum:
                               e.reduce_sum(
                            out=lsum, in_=lsums[tj][:, 0:st + 1], axis=AX.X),
                            rd=(f"ls{tj}",), wr=(f"smv{tj}s",))
                        p.emit("vector", lambda e, rl=rl, lsum=lsum: e.reciprocal(
                            rl, lsum), rd=(f"smv{tj}s",), wr=(f"smv{tj}",))
                        p.emit("vector", lambda e, st=st, tj=tj, rl=rl:
                               e.tensor_single_scalar(
                            out=pbuf[tj][:, 0:(st + 1) * 512],
                            in_=pbuf[tj][:, 0:(st + 1) * 512],
                            scalar=rl, op=ALU.mult),
                            rd=(f"pb{tj}", f"smv{tj}"), wr=(f"pb{tj}",))

                    def tile_pt(ti, tj):
                        st = ti // 4
                        i = ti % 4
                        nsc_i = ti + 1
                        nsc = 4 * (st + 1)
                        for scq in range(st + 1):
                            cnt = min(4, nsc_i - 4 * scq)
                            pj = scq % 2
                            for u in range(cnt):
                                sc = scq * 4 + u
                                p.emit("tensor", lambda e, sc=sc, u=u, tj=tj, pj=pj:
                                       e.transpose(
                                    psum_pt[pj][:, u * 128:(u + 1) * 128],
                                    pbuf[tj][:, sc * 128:(sc + 1) * 128], ident[:]),
                                    rd=(f"pb{tj}", "ident"), wr=(f"ppt{pj}",))
                            dst = ptbuf[:].rearrange("p (c f) -> p c f", f=512)[
                                :, scq * 4: scq * 4 + cnt, i * 128:(i + 1) * 128]
                            src = psum_pt[pj][:, 0:cnt * 128].rearrange(
                                "p (c f) -> p c f", f=128)
                            p.emit("vector", lambda e, dst=dst, src=src:
                                   e.tensor_copy(out=dst, in_=src),
                                   rd=(f"ppt{pj}",),
                                   wr=(f"ppt{pj}",) + tuple(
                                       f"ptc{scq * 4 + u}_{i}" for u in range(cnt)))
                        for sc in range(nsc_i, nsc):
                            p.emit("vector", lambda e, sc=sc, i=i: e.memset(
                                ptbuf[:, sc * 512 + i * 128: sc * 512 + (i + 1) * 128],
                                0.0), wr=(f"ptc{sc}_{i}",))

                    for st in range(NTC):
                        nsc = 4 * (st + 1)
                        zj = st % 2
                        for i in range(4):
                            ti = st * 4 + i
                            tj = ti % 2
                            tile_S(ti, tj, hj)
                            if i > 0:
                                tprev = ti - 1
                                tile_soft(tprev, tprev % 2)
                                tile_pt(tprev, tprev % 2)
                        ti = st * 4 + 3
                        tile_soft(ti, ti % 2)
                        tile_pt(ti, ti % 2)
                        for sc in range(nsc):
                            p.emit("tensor", lambda e, sc=sc, vj=vj, zj=zj:
                                   e.matmul(
                                psum_z[zj][:],
                                v_hb[vj][:, sc * 128:(sc + 1) * 128],
                                ptbuf[:, sc * 512:(sc + 1) * 512],
                                start=(sc == 0), stop=(sc == nsc - 1)),
                                rd=(f"v{vj}",) + tuple(
                                    f"ptc{sc}_{i2}" for i2 in range(4)),
                                wr=(f"pz{zj}",))
                        p.emit("vector", lambda e, h=h, st=st, zj=zj: e.tensor_copy(
                            out=zT_sb[:, (h * NTT + st * 4) * 128:
                                      (h * NTT + st * 4 + 4) * 128],
                            in_=psum_z[zj][:]),
                            rd=(f"pz{zj}",), wr=("zT", f"pz{zj}"))
            if upto == "p2":
                p.barrier()
                return _finish(nc, p, dbg, root)

            pa.close()
            p.barrier()
            wo_sb = sb1("wo_sb", [128, NHL * D], BF16)
            wol_t = sb1("wol_t", [128, D], F32)
            sst_t = sb1("sst_t", [128, 2 * D], F32)
            wol = wol_t[:]
            sstage = [sst_t[:, 0:D], sst_t[:, D:2 * D]]
            psum_wo = [ph.enter_context(
                nc.psum_tensor(f"psum_wo{j}", [128, 512], F32)) for j in range(2)]
            for hc in range(NHL):
                p.emit("ld", lambda e, hc=hc: e.dma_start(
                    out=wol, in_=woT[hc * 128:(hc + 1) * 128, :]), wr=("wol",))
                p.emit("scalar", lambda e, hc=hc: e.copy(
                    out=wo_sb[:, hc * D:(hc + 1) * D], in_=wol),
                    rd=("wol",), wr=("wo", "wol"))
            for tt in range(NTT):
                sj = tt % 2
                for jc in range(4):
                    wj = jc % 2
                    for hc in range(NHL):
                        p.emit("tensor", lambda e, tt=tt, jc=jc, hc=hc, wj=wj:
                               e.matmul(
                            psum_wo[wj][:],
                            zT_sb[:, (hc * NTT + tt) * 128:(hc * NTT + tt + 1) * 128],
                            wo_sb[:, hc * D + jc * 512: hc * D + (jc + 1) * 512],
                            start=(hc == 0), stop=(hc == NHL - 1)),
                            rd=("zT", "wo"), wr=(f"pwo{wj}",))
                    p.emit("vector", lambda e, jc=jc, wj=wj, sj=sj: e.tensor_copy(
                        out=sstage[sj][:, jc * 512:(jc + 1) * 512],
                        in_=psum_wo[wj][:]),
                        rd=(f"pwo{wj}",), wr=(f"sst{sj}", f"pwo{wj}"))
                p.emit("st", lambda e, tt=tt, sj=sj: e.dma_start(
                    out=s_part[tt * 128:(tt + 1) * 128, :], in_=sstage[sj][:]),
                    rd=(f"sst{sj}",))
        p.barrier()
        p.emit("gpsimd", lambda e: e.collective_compute(
            "AllReduce", ALU.add, replica_groups=GROUPS,
            ins=[s_part.opt()], outs=[s_full.opt()]),
            rd=("s_part@post",), wr=("s_full",))
        if upto == "p2c":
            return _finish(nc, p, dbg, root)

        with ExitStack() as ph:
            sb1 = lambda n_, s_, dt: ph.enter_context(nc.sbuf_tensor(n_, s_, dt))
            xnT0 = sb1("xnT0", [128, NDC * 512], BF16)
            xnT1 = sb1("xnT1", [128, NDC * 512], BF16)
            w1q_sb = [sb1(f"w1q_sb{j}", [128, D], BF16) for j in range(2)]
            w2q_sb = [sb1(f"w2q_sb{j}", [128, D], BF16) for j in range(2)]
            w3q_sb = [sb1(f"w3q_sb{j}", [128, 512], BF16) for j in range(2)]
            hidT = sb1("hidT", [128, NHC * 1024], BF16)
            sig = [sb1(f"sig{j}", [128, 512], F32) for j in range(2)]
            fstage = [sb1(f"fstage{j}", [128, 512], F32) for j in range(2)]
            px1 = [ph.enter_context(nc.psum_tensor(f"px1{j}", [128, 512], F32))
                   for j in range(2)]
            px2 = [ph.enter_context(nc.psum_tensor(f"px2{j}", [128, 512], F32))
                   for j in range(2)]
            pf = [ph.enter_context(nc.psum_tensor(f"pf{k}", [128, 512], F32))
                  for k in range(4)]
            fsb = sb1("fsb", [128, D], F32)
            ssb = sb1("ssb", [128, D], F32)
            xsb = sb1("xsb", [128, D], F32)
            gsb = sb1("gsb", [128, D], F32)
            osb = sb1("osb", [128, D], F32)

            def combine_chunk(pair):
                lo, hi = pair * 1024, (pair + 1) * 1024
                p.emit("gpsimd", lambda e, lo=lo, hi=hi: e.collective_compute(
                    "AllReduce", ALU.add, replica_groups=GROUPS,
                    ins=[f_part[lo:hi, :].opt()], outs=[f_full[lo:hi, :].opt()]),
                    after=tuple(fstores[pair]), wr=(f"f_full{pair}",))
                for ttl in range(8):
                    tt = pair * 8 + ttl
                    p.emit("st", lambda e, tt=tt: e.dma_start(
                        out=fsb[:], in_=f_full[tt * 128:(tt + 1) * 128, :]),
                        rd=(f"f_full{pair}",), wr=("fsb",))
                    p.emit("st", lambda e, tt=tt: e.dma_start(
                        out=ssb[:], in_=s_full[tt * 128:(tt + 1) * 128, :]),
                        rd=("s_full",), wr=("ssb",))
                    p.emit("st", lambda e, tt=tt: e.dma_start(
                        out=xsb[:], in_=x[tt * 128:(tt + 1) * 128, :]),
                        wr=("xsb",))
                    p.emit("scalar", lambda e: e.activation(
                        out=gsb[:], in_=fsb[:], func=AF.Silu),
                        rd=("fsb",), wr=("gsb", "fsb"))
                    p.emit("vector", lambda e: e.tensor_mul(
                        gsb[:], gsb[:], ssb[:]),
                        rd=("gsb", "ssb"), wr=("gsb",))
                    p.emit("vector", lambda e: e.tensor_add(
                        osb[:], gsb[:], xsb[:]),
                        rd=("gsb", "xsb"), wr=("osb",))
                    p.emit("st", lambda e, tt=tt: e.dma_start(
                        out=out[tt * 128:(tt + 1) * 128, :], in_=osb[:]),
                        rd=("osb",))

            fstores = [[] for _ in range(4)]
            for pair in range(4):
                p.emit("ld", lambda e, pair=pair: e.dma_start(
                    out=xnT0[:], in_=xnT_s[2 * pair]), wr=("x0",))
                p.emit("ld", lambda e, pair=pair: e.dma_start(
                    out=xnT1[:], in_=xnT_s[2 * pair + 1]), wr=("x1",))
                for hc in range(NHC):
                    wj = hc % 2
                    p.emit("ld", lambda e, hc=hc, wj=wj: e.dma_start(
                        out=w1q_sb[wj][:], in_=w1q_s[hc]), wr=(f"w1{wj}",))
                    p.emit("ld", lambda e, hc=hc, wj=wj: e.dma_start(
                        out=w2q_sb[wj][:], in_=w2q_s[hc]), wr=(f"w2{wj}",))
                    for sub, (xnT_b, xk) in ((0, (xnT0, "x0")), (1, (xnT1, "x1"))):
                        sj = (2 * hc + sub) % 2
                        for wq_b, wkk, px in ((w1q_sb[wj], f"w1{wj}", px1[sj]),
                                              (w2q_sb[wj], f"w2{wj}", px2[sj])):
                            for dc in range(NDC):
                                p.emit("tensor", lambda e, wq_b=wq_b, px=px,
                                       xnT_b=xnT_b, dc=dc: e.matmul(
                                    px[:],
                                    wq_b[:, dc * 128:(dc + 1) * 128],
                                    xnT_b[:, dc * 512:(dc + 1) * 512],
                                    start=(dc == 0), stop=(dc == NDC - 1)),
                                    rd=(wkk, xk),
                                    wr=(("px1" if px in px1 else "px2") + str(sj),))
                        p.emit("scalar", lambda e, px=px1[sj], sg=sig[sj]:
                               e.activation(
                            out=sg[:], in_=px[:], func=AF.Sigmoid),
                            rd=(f"px1{sj}",), wr=(f"sig{sj}", f"px1{sj}"))
                        p.emit("vector", lambda e, hc=hc, sub=sub, px=px2[sj],
                               sg=sig[sj]: e.tensor_mul(
                            hidT[:, hc * 1024 + sub * 512:
                                 hc * 1024 + (sub + 1) * 512],
                            sg[:], px[:]),
                            rd=(f"sig{sj}", f"px2{sj}"), wr=("hidT", f"px2{sj}"),
                            tag=f"hid_p{pair}_hc{hc}_s{sub}")
                if pair == 0 and "hid0" in debug:
                    hd = nc.dram_tensor("dbg_hid0", [128, NHC * 1024], BF16,
                                        kind="ExternalOutput").ap()
                    p.emit("st", lambda e, hd=hd: e.dma_start(
                        out=hd, in_=hidT[:]), rd=("hidT",))
                for jc in range(4):
                    for g2 in range(2):
                        for hc in range(NHC):
                            wj = hc % 2
                            p.emit("ld", lambda e, hc=hc, jc=jc, wj=wj:
                                   e.dma_start(
                                out=w3q_sb[wj][:],
                                in_=w3q_s[hc, :, jc * 512:(jc + 1) * 512]),
                                wr=(f"w3{wj}",))
                            for tg in range(4):
                                ttl = g2 * 4 + tg
                                p.emit("tensor", lambda e, hc=hc, ttl=ttl,
                                       pfk=pf[ttl % 4], wj=wj: e.matmul(
                                    pfk[:],
                                    hidT[:, hc * 1024 + ttl * 128:
                                         hc * 1024 + (ttl + 1) * 128],
                                    w3q_sb[wj][:],
                                    start=(hc == 0), stop=(hc == NHC - 1)),
                                    rd=("hidT", f"w3{wj}"),
                                    wr=(f"pf{ttl % 4}",),
                                    tag=f"w3mm_p{pair}_jc{jc}_g{g2}_hc{hc}_t{tg}")
                        for tg in range(4):
                            ttl = g2 * 4 + tg
                            tt = pair * 8 + ttl
                            fj = ttl % 2
                            p.emit("vector", lambda e, pfk=pf[ttl % 4], fj=fj:
                                   e.tensor_copy(out=fstage[fj][:], in_=pfk[:]),
                                rd=(f"pf{ttl % 4}",), wr=(f"fst{fj}", f"pf{ttl % 4}"))
                            fstores[pair].append(p.emit(
                                "st", lambda e, tt=tt, jc=jc, fj=fj: e.dma_start(
                                    out=f_part[tt * 128:(tt + 1) * 128,
                                               jc * 512:(jc + 1) * 512],
                                    in_=fstage[fj][:]), rd=(f"fst{fj}",)))
                combine_chunk(pair)
        if upto == "p3":
            return _finish(nc, p, dbg, root)
        return _finish(nc, p, dbg, root)


_NC_CACHE = {}


def _get_program():
    key = (UPTO, tuple(DEBUG_DUMPS))
    if key not in _NC_CACHE:
        _NC_CACHE[key] = build_program(UPTO, DEBUG_DUMPS)
    return _NC_CACHE[key]


def kernel(L0_attn_x, L0_attn_Wq, L0_attn_Wk, L0_attn_Wv, L0_attn_Wo,
           L0_ffn_w1, L0_ffn_w2, L0_ffn_w3):
    x = np.ascontiguousarray(np.asarray(L0_attn_x, dtype=np.float32))
    wq = np.asarray(L0_attn_Wq, dtype=np.float32)
    wk = np.asarray(L0_attn_Wk, dtype=np.float32)
    wv = np.asarray(L0_attn_Wv, dtype=np.float32)
    wo = np.asarray(L0_attn_Wo, dtype=np.float32)
    w1 = np.asarray(L0_ffn_w1, dtype=np.float32)
    w2 = np.asarray(L0_ffn_w2, dtype=np.float32)
    w3 = np.asarray(L0_ffn_w3, dtype=np.float32)

    wqT = np.ascontiguousarray(wq.T)
    wkT = np.ascontiguousarray(wk.T)
    wvT = np.ascontiguousarray(wv.T)
    woT = np.ascontiguousarray(wo.T)
    w1T = np.ascontiguousarray(w1.T)
    w2T = np.ascontiguousarray(w2.T)
    w3T = np.ascontiguousarray(w3.T)

    nc, _ = _get_program()
    in_maps = []
    for c in range(8):
        b, g = c // 2, c % 2
        in_maps.append({
            "x": np.ascontiguousarray(x[b]),
            "wqT": np.ascontiguousarray(wqT[:, g * 1024:(g + 1) * 1024]),
            "wkT": np.ascontiguousarray(wkT[:, g * 1024:(g + 1) * 1024]),
            "wvT": np.ascontiguousarray(wvT[:, g * 1024:(g + 1) * 1024]),
            "woT": np.ascontiguousarray(woT[g * 1024:(g + 1) * 1024, :]),
            "w1T": np.ascontiguousarray(w1T[:, g * HF:(g + 1) * HF]),
            "w2T": np.ascontiguousarray(w2T[:, g * HF:(g + 1) * HF]),
            "w3T": np.ascontiguousarray(w3T[g * HF:(g + 1) * HF, :]),
        })
    res = run_bass_kernel_spmd(nc, in_maps, core_ids=list(range(8)))
    out = np.empty((4, T, D), dtype=np.float32)
    for b in range(4):
        out[b] = res.results[2 * b]["out"]
    return out


# version 09
# speedup vs baseline: 2.9904x; 1.1677x over previous
import numpy as np
import ml_dtypes
from contextlib import ExitStack

import concourse.bass as bass
import concourse.mybir as mybir
from concourse.bass_utils import run_bass_kernel_spmd
from concourse.bass_core_rust import IndirectOffsetOnAxis

F32 = mybir.dt.float32
BF16 = mybir.dt.bfloat16
AF = mybir.ActivationFunctionType
ALU = mybir.AluOpType
AX = mybir.AxisListType

T = 4096
D = 2048
NHL = 8
HF = 2816
NHC = HF // 128
NTT = T // 128
NTC = T // 512
NDC = D // 128
SCALE = 1.0 / float(np.sqrt(128.0))
LN_EPS = 1e-5
INV_D = 1.0 / D
Q_S1 = 0.5 / (5632.0 * 2048.0)
Q_S2 = 0.5e-8
GROUPS = [[0, 1], [2, 3], [4, 5], [6, 7]]

UPTO = "full"
FIFO_SKIP = True
DMA_STRICT = False
COMBINE_DEFER = True
DEBUG_DUMPS = ()

QINFO = {
    "tensor": ("tensor", 1),
    "vector": ("vector", 1),
    "scalar": ("scalar", 1),
    "gpsimd": ("gpsimd", 1),
    "ld": ("sync", 16),
    "st": ("scalar", 16),
    "gd": ("gpsimd", 16),
}
QUEUES = tuple(QINFO)


class Prog:
    def __init__(self):
        self.items = []
        self.res = {}
        self.pending = {q: set() for q in QUEUES}
        self.last_in_q = {q: None for q in QUEUES}

    def emit(self, q, fn, rd=(), wr=(), tag="", after=()):
        idx = len(self.items)
        deps = set(self.pending[q])
        deps.update(after)
        self.pending[q] = set()
        for k in rd:
            e = self.res.get(k)
            if e is not None and e["w"] is not None:
                deps.add(e["w"])
        for k in wr:
            e = self.res.get(k)
            if e is not None:
                if e["w"] is not None:
                    deps.add(e["w"])
                deps.update(e["r"].values())
        for k in rd:
            self.res.setdefault(k, {"w": None, "r": {}})["r"][q] = idx
        for k in wr:
            e = self.res.setdefault(k, {"w": None, "r": {}})
            e["w"] = idx
            e["r"] = {}
        deps.discard(idx)
        self.items.append({"q": q, "fn": fn, "deps": deps, "tag": tag})
        self.last_in_q[q] = idx
        return idx

    def barrier(self):
        lasts = {i for i in self.last_in_q.values() if i is not None}
        for q in QUEUES:
            self.pending[q] |= lasts
        self.res = {}


def _finish(nc, p, dbg, root):
    bounce = None
    for name, (dbg_ap, src_ap) in dbg.items():
        if name in ("s_pool", "f_pool"):
            if bounce is None:
                bounce = root.enter_context(
                    nc.sbuf_tensor("dbg_bounce", [128, D], F32))
            for r0 in range(0, src_ap.shape[0], 128):
                p.emit("ld", lambda e, ap=src_ap, r0=r0: e.dma_start(
                    out=bounce[:], in_=ap[r0:r0 + 128, :]),
                    rd=(name,), wr=("dbg_b",))
                p.emit("st", lambda e, dbg_ap=dbg_ap, r0=r0: e.dma_start(
                    out=dbg_ap[r0:r0 + 128, :], in_=bounce[:]), rd=("dbg_b",),
                    wr=("dbg_b",))
            continue
        p.emit("st", lambda e, dbg_ap=dbg_ap, ap=src_ap: e.dma_start(
            out=dbg_ap, in_=ap), rd=(name,))
    if dbg:
        pass

    items = p.items
    n = len(items)
    hard = [None] * n
    for idx, it in enumerate(items):
        my_eng = QINFO[it["q"]][0]
        hd = []
        for d in it["deps"]:
            dq = items[d]["q"]
            d_eng, d_inc = QINFO[dq][0], QINFO[dq][1]
            if FIFO_SKIP and d_inc == 1 and d_eng == my_eng == "tensor" \
                    and it["q"] == "tensor":
                continue
            hd.append(d)
        hard[idx] = hd
    needed = set()
    for hd in hard:
        needed.update(d for d in hd if QINFO[items[d]["q"]][1] == 1)
    NLANE = 8
    val = [0] * n
    sem_of = [None] * n
    cnt = {}
    dseq = {q: 0 for q in QUEUES}
    for idx, it in enumerate(items):
        q = it["q"]
        if QINFO[q][1] == 16:
            lane = dseq[q] % NLANE
            dseq[q] += 1
            sid = f"{q}{lane}"
            cnt[sid] = cnt.get(sid, 0) + 16
            val[idx] = cnt[sid]
            sem_of[idx] = sid
            it["inc"] = 16
        elif idx in needed:
            cnt[q] = cnt.get(q, 0) + 1
            val[idx] = cnt[q]
            sem_of[idx] = q
            it["inc"] = 1
        else:
            it["inc"] = 0
        it["sem"] = sem_of[idx]
    satisfied = {q: {} for q in QUEUES}
    waits = [[] for _ in range(n)]
    for idx, it in enumerate(items):
        q = it["q"]
        for d in sorted(hard[idx]):
            sid = sem_of[d]
            v = val[d]
            assert v > 0 and sid is not None
            if satisfied[q].get(sid, 0) >= v:
                continue
            waits[idx].append((sid, v))
            satisfied[q][sid] = v
        it["waits"] = waits[idx]

    sems = {}
    for sid in cnt:
        sems[sid] = root.enter_context(nc.semaphore("sem_" + sid))
    blk = root.enter_context(nc.Block())

    final_counts = dict(cnt)

    def run_body(body_eng):
        def body(eng):
            for it in items:
                if QINFO[it["q"]][0] != body_eng:
                    continue
                for (dq, v) in it["waits"]:
                    eng.wait_ge(sems[dq], v)
                inst = it["fn"](eng)
                if it["inc"]:
                    inst.then_inc(sems[it["sem"]], it["inc"])
            if body_eng == "sync":
                for sid, v in final_counts.items():
                    eng.wait_ge(sems[sid], v)
        return body

    blk.sync(run_body("sync"))
    blk.tensor(run_body("tensor"))
    blk.vector(run_body("vector"))
    blk.scalar(run_body("scalar"))
    blk.gpsimd(run_body("gpsimd"))
    return nc, p


def build_program(upto=UPTO, debug=()):
    nc = bass.Bass("TRN2", target_bir_lowering=False)

    x = nc.dram_tensor("x", [T, D], F32, kind="ExternalInput").ap()
    wqT = nc.dram_tensor("wqT", [D, 1024], F32, kind="ExternalInput").ap()
    wkT = nc.dram_tensor("wkT", [D, 1024], F32, kind="ExternalInput").ap()
    wvT = nc.dram_tensor("wvT", [D, 1024], F32, kind="ExternalInput").ap()
    woT = nc.dram_tensor("woT", [1024, D], F32, kind="ExternalInput").ap()
    w1T = nc.dram_tensor("w1T", [D, HF], F32, kind="ExternalInput").ap()
    w2T = nc.dram_tensor("w2T", [D, HF], F32, kind="ExternalInput").ap()
    w3T = nc.dram_tensor("w3T", [HF, D], F32, kind="ExternalInput").ap()
    out = nc.dram_tensor("out", [T, D], F32, kind="ExternalOutput").ap()

    xnT_s = nc.dram_tensor("xnT_s", [NTC, 128, NDC * 512], BF16, kind="Internal").ap()
    kT_s = nc.dram_tensor("kT_s", [NHL, 128, T], BF16, kind="Internal").ap()
    qT_s = nc.dram_tensor("qT_s", [NHL, 128, T], BF16, kind="Internal").ap()
    v_s = nc.dram_tensor("v_s", [NHL, NTT, 128, 128], BF16, kind="Internal").ap()
    w1q_s = nc.dram_tensor("w1q_s", [NHC, 128, D], BF16, kind="Internal").ap()
    w2q_s = nc.dram_tensor("w2q_s", [NHC, 128, D], BF16, kind="Internal").ap()
    w3q_s = nc.dram_tensor("w3q_s", [NHC, 128, D], BF16, kind="Internal").ap()
    rows_idx = nc.dram_tensor("rows_idx", [128, NTT], mybir.dt.int32,
                              kind="ExternalInput").ap()
    s_pool = nc.dram_tensor("s_pool", [2 * T, D], F32, kind="Internal",
                            addr_space="Shared").ap()
    f_pool = nc.dram_tensor("f_pool", [2 * T, D], F32, kind="Internal",
                            addr_space="Shared").ap()
    bar_in = nc.dram_tensor("bar_in", [1, 4], F32, kind="Internal").ap()
    bar_out = nc.dram_tensor("bar_out", [1, 4], F32, kind="Internal").ap()
    gam_part = nc.dram_tensor("gam_part", [1, 4], F32, kind="Internal").ap()
    gam_full = nc.dram_tensor("gam_full", [1, 4], F32, kind="Internal").ap()
    colb = nc.dram_tensor("colb", [128], F32, kind="Internal").ap()

    dbg = {}
    scratch_map = {"xnT_s": xnT_s, "kT_s": kT_s, "qT_s": qT_s, "v_s": v_s,
                   "w1q_s": w1q_s, "w2q_s": w2q_s, "w3q_s": w3q_s,
                   "s_pool": s_pool, "f_pool": f_pool, "gam_full": gam_full}
    for name in debug:
        if name not in scratch_map:
            continue
        src = scratch_map[name]
        dbg[name] = (nc.dram_tensor("dbg_" + name, list(src.shape),
                                    src.dtype, kind="ExternalOutput").ap(), src)

    ident_dram = nc.inline_tensor(np.eye(128, dtype=ml_dtypes.bfloat16), "ident_c")
    ones_dram = nc.inline_tensor(np.ones((1, 128), dtype=np.float32), "ones_c")
    _mk = np.zeros((4, 128, 512), dtype=np.float32)
    for _i in range(4):
        _p = np.arange(128)[:, None]
        _f = np.arange(512)[None, :]
        _mk[_i] = np.where(128 * _i + _p - _f >= 0, 1.0, 0.0)
    cmask_dram = nc.inline_tensor(_mk.astype(ml_dtypes.bfloat16), "cmask_c")

    p = Prog()

    with ExitStack() as root:
        sb = lambda n_, s_, dt: root.enter_context(nc.sbuf_tensor(n_, s_, dt))
        ident = sb("ident_sb", [128, 128], BF16)
        ones_f = sb("ones_f", [1, 128], F32)
        h_bc = sb("h_bc", [128, 4], F32)
        hn_bc = sb("hn_bc", [128, 4], F32)
        cmask = sb("cmask_sb", [128, 4 * 512], BF16)
        rows_sb = sb("rows_sb", [128, NTT], mybir.dt.int32)
        barv = sb("barv", [1, 4], F32)

        p.emit("ld", lambda e: e.dma_start(out=ident[:], in_=ident_dram.ap()),
               wr=("ident",))
        p.emit("ld", lambda e: e.dma_start(out=ones_f[:], in_=ones_dram.ap()),
               wr=("ones",))
        p.emit("ld", lambda e: e.dma_start(
            out=cmask[:].rearrange("p (c f) -> p c f", c=4),
            in_=cmask_dram.ap().rearrange("c p f -> p c f")), wr=("cmask",))
        p.emit("ld", lambda e: e.dma_start(out=rows_sb[:], in_=rows_idx),
               wr=("rows",))
        p.emit("vector", lambda e: e.memset(barv[:], 1.0), wr=("barv",))
        bar_ctr = [0]

        def pair_barrier(after=()):
            bar_ctr[0] += 1
            bid = p.emit("st", lambda e: e.dma_start(out=bar_in, in_=barv[:]),
                         rd=("barv",), after=after)
            return p.emit("gpsimd", lambda e: e.collective_compute(
                "AllReduce", ALU.add, replica_groups=GROUPS,
                ins=[bar_in.opt()], outs=[bar_out.opt()]), after=(bid,))

        with ExitStack() as ph:
            sb1 = lambda n_, s_, dt: ph.enter_context(nc.sbuf_tensor(n_, s_, dt))
            xt = [sb1(f"xt{j}", [128, D], F32) for j in range(2)]
            xsq = sb1("xsq", [128, D], F32)
            xn4 = [sb1(f"xn{i}", [128, D], BF16) for i in range(4)]
            xnTc = [sb1(f"xnTc{j}", [128, NDC * 512], BF16) for j in range(2)]
            lnv = [sb1(f"lnv{j}", [128, 8], F32) for j in range(2)]
            ptr = [ph.enter_context(nc.psum_tensor(f"ptr{j}", [128, 512], BF16))
                   for j in range(2)]

            for tc in range(NTC):
                cp = tc % 2
                for i in range(4):
                    ti = tc * 4 + i
                    j = ti % 2
                    xb, lv, xi = xt[j], lnv[j], xn4[i]
                    kx, kl, kxn = f"xt{j}", f"lnv{j}", f"xn{i}"
                    nm, ssq, t1v = lv[:, 0:1], lv[:, 1:2], lv[:, 2:3]
                    msq, veps, stdv, rstd = lv[:, 3:4], lv[:, 4:5], lv[:, 5:6], lv[:, 6:7]
                    p.emit("ld", lambda e, ti=ti, xb=xb: e.dma_start(
                        out=xb[:], in_=x[ti * 128:(ti + 1) * 128, :]), wr=(kx,))
                    p.emit("vector", lambda e, xb=xb, nm=nm: e.reduce_sum(
                        out=nm, in_=xb[:], axis=AX.X, negate=True),
                        rd=(kx,), wr=(kl,))
                    p.emit("vector", lambda e, nm=nm: e.tensor_scalar_mul(
                        nm, nm, INV_D), rd=(kl,), wr=(kl,))
                    p.emit("scalar", lambda e, xb=xb, ssq=ssq: e.activation(
                        out=xsq[:], in_=xb[:], func=AF.Square, accum_out=ssq),
                        rd=(kx,), wr=("xsq", kl + "s"))
                    p.emit("vector", lambda e, ssq=ssq, t1v=t1v: e.tensor_scalar(
                        out=t1v, in0=ssq, scalar1=INV_D, scalar2=LN_EPS,
                        op0=ALU.mult, op1=ALU.add), rd=(kl + "s",), wr=(kl,))
                    p.emit("vector", lambda e, nm=nm, msq=msq: e.tensor_single_scalar(
                        out=msq, in_=nm, scalar=nm, op=ALU.mult), rd=(kl,), wr=(kl,))
                    p.emit("vector", lambda e, t1v=t1v, msq=msq, veps=veps:
                           e.tensor_sub(veps, t1v, msq), rd=(kl,), wr=(kl,))
                    p.emit("scalar", lambda e, veps=veps, stdv=stdv: e.activation(
                        out=stdv, in_=veps, func=AF.Sqrt), rd=(kl,), wr=(kl + "s",))
                    p.emit("vector", lambda e, stdv=stdv, rstd=rstd: e.reciprocal(
                        rstd, stdv), rd=(kl + "s",), wr=(kl,))
                    p.emit("vector", lambda e, xb=xb, nm=nm, rstd=rstd, xi=xi:
                           e.tensor_scalar(
                        out=xi[:], in0=xb[:], scalar1=nm, scalar2=rstd,
                        op0=ALU.add, op1=ALU.mult), rd=(kx, kl), wr=(kxn,))
                for dc in range(NDC):
                    pj = dc % 2
                    for i in range(4):
                        p.emit("tensor", lambda e, i=i, dc=dc, pj=pj: e.transpose(
                            ptr[pj][:, i * 128:(i + 1) * 128],
                            xn4[i][:, dc * 128:(dc + 1) * 128], ident[:]),
                            rd=(f"xn{i}", "ident"), wr=(f"ptr{pj}",))
                    p.emit("scalar", lambda e, dc=dc, pj=pj, cp=cp: e.copy(
                        out=xnTc[cp][:, dc * 512:(dc + 1) * 512], in_=ptr[pj][:]),
                        rd=(f"ptr{pj}",), wr=(f"xnTc{cp}", f"ptr{pj}"))
                p.emit("st", lambda e, tc=tc, cp=cp: e.dma_start(
                    out=xnT_s[tc], in_=xnTc[cp][:]), rd=(f"xnTc{cp}",))
            gw = [sb1(f"gw{j}", [128, 2816], F32) for j in range(2)]
            gcols = sb1("gcols", [128, 40], F32)
            gcol = sb1("gcol", [128, 1], F32)
            grow = sb1("grow", [1, 128], F32)
            gpart = sb1("gpart", [1, 4], F32)
            p.emit("vector", lambda e: e.memset(gpart[:], 0.0), wr=("gpart",))
            li = 0
            for wi, wsrc in enumerate((w1T, w2T, w3T)):
                nblk = NDC if wi < 2 else NHC
                ncol = HF if wi < 2 else D
                for blk_i in range(nblk):
                    j = li % 2
                    li += 1
                    p.emit("ld", lambda e, wsrc=wsrc, blk_i=blk_i, ncol=ncol, j=j:
                           e.dma_start(
                        out=gw[j][:, 0:ncol],
                        in_=wsrc[blk_i * 128:(blk_i + 1) * 128, :]), wr=(f"gw{j}",))
                    p.emit("vector", lambda e, blk_i=blk_i, ncol=ncol, j=j:
                           e.reduce_sum(
                        out=gcols[:, blk_i:blk_i + 1],
                        in_=gw[j][:, 0:ncol],
                        axis=AX.X, apply_absolute_value=True),
                        rd=(f"gw{j}",), wr=(f"gcols{blk_i}", f"gw{j}"))
                p.emit("vector", lambda e, nblk=nblk: e.reduce_sum(
                    out=gcol[:], in_=gcols[:, 0:nblk], axis=AX.X),
                    rd=tuple(f"gcols{b2}" for b2 in range(nblk)), wr=("gcol",))
                p.emit("st", lambda e: e.dma_start(out=colb, in_=gcol[:]),
                       rd=("gcol",), wr=("colb",))
                p.emit("ld", lambda e: e.dma_start(
                    out=grow[:], in_=colb.rearrange("(o p) -> o p", o=1)),
                    rd=("colb",), wr=("grow",))
                p.emit("vector", lambda e, wi=wi: e.reduce_sum(
                    out=gpart[:, wi:wi + 1], in_=grow[:], axis=AX.X),
                    rd=("grow",), wr=("gpart", "grow", "gcol"))
            p.emit("st", lambda e: e.dma_start(out=gam_part, in_=gpart[:]),
                   rd=("gpart",), wr=("gam_part",))
            p.emit("gpsimd", lambda e: e.collective_compute(
                "AllReduce", ALU.add, replica_groups=GROUPS,
                ins=[gam_part.opt()], outs=[gam_full.opt()]),
                rd=("gam_part",), wr=("gam_full",))
        p.barrier()
        if upto == "p1a":
            return _finish(nc, p, dbg, root)

        with ExitStack() as ph:
            sb1 = lambda n_, s_, dt: ph.enter_context(nc.sbuf_tensor(n_, s_, dt))
            wq_sb = sb1("wq_sb", [128, NDC * 1024], BF16)
            wk_sb = sb1("wk_sb", [128, NDC * 1024], BF16)
            wv_sb = sb1("wv_sb", [128, NDC * 1024], BF16)
            wload = [sb1(f"wload{j}", [128, 2816], F32) for j in range(2)]
            li = 0
            for wname, wdst, wsrc in (("q", wq_sb, wqT), ("k", wk_sb, wkT),
                                      ("v", wv_sb, wvT)):
                for dc in range(NDC):
                    j = li % 2
                    li += 1
                    p.emit("ld", lambda e, wsrc=wsrc, dc=dc, j=j: e.dma_start(
                        out=wload[j][:, 0:1024],
                        in_=wsrc[dc * 128:(dc + 1) * 128, :]), wr=(f"wload{j}",))
                    p.emit("scalar", lambda e, wdst=wdst, dc=dc, j=j: e.copy(
                        out=wdst[:, dc * 1024:(dc + 1) * 1024],
                        in_=wload[j][:, 0:1024]),
                        rd=(f"wload{j}",), wr=("w" + wname, f"wload{j}"))

            grow4 = sb1("grow4", [1, 4], F32)
            hrow = sb1("hrow", [1, 4], F32)
            hnrow = sb1("hnrow", [1, 4], F32)
            p.emit("ld", lambda e: e.dma_start(out=grow4[:], in_=gam_full),
                   rd=("gam_full",), wr=("grow4",))
            p.emit("vector", lambda e: e.tensor_scalar(
                out=hrow[:], in0=grow4[:], scalar1=Q_S1, scalar2=Q_S2,
                op0=ALU.mult, op1=ALU.add), rd=("grow4",), wr=("hrow",))
            p.emit("vector", lambda e: e.tensor_scalar_mul(hnrow[:], hrow[:], -1.0),
                   rd=("hrow",), wr=("hnrow",))
            pbc = ph.enter_context(nc.psum_tensor("pbc", [128, 4], F32))
            p.emit("tensor", lambda e: e.matmul(
                pbc[:], ones_f[:], hrow[:], start=True, stop=True),
                rd=("ones", "hrow"), wr=("pbc",))
            p.emit("vector", lambda e: e.tensor_copy(out=h_bc[:], in_=pbc[:]),
                   rd=("pbc",), wr=("h_bc", "pbc"))
            p.emit("tensor", lambda e: e.matmul(
                pbc[:], ones_f[:], hnrow[:], start=True, stop=True),
                rd=("ones", "hnrow"), wr=("pbc",))
            p.emit("vector", lambda e: e.tensor_copy(out=hn_bc[:], in_=pbc[:]),
                   rd=("pbc",), wr=("hn_bc", "pbc"))

            wfp = [sb1(f"wfp{j}", [128, D], F32) for j in range(2)]
            tneg = [sb1(f"tneg{j}", [128, D], BF16) for j in range(2)]
            wqt = [sb1(f"wqt{j}", [128, D], BF16) for j in range(2)]
            quant_work = []
            qi = 0
            for wi, (wsrc, wdst_s) in enumerate(((w1T, w1q_s), (w2T, w2q_s),
                                                 (w3T, w3q_s))):
                for hc in range(NHC):
                    j = qi % 2
                    qi += 1

                    def q_block(wi=wi, wsrc=wsrc, wdst_s=wdst_s, hc=hc, j=j):
                        if wi < 2:
                            src = wsrc[:, hc * 128:(hc + 1) * 128].rearrange(
                                "(c p) f -> p c f", p=128)
                            dst = wfp[j][:].rearrange("p (c f) -> p c f", c=NDC)
                            p.emit("ld", lambda e, src=src, dst=dst: e.dma_start(
                                out=dst, in_=src), wr=(f"wfp{j}",))
                        else:
                            p.emit("ld", lambda e: e.dma_start(
                                out=wfp[j][:], in_=wsrc[hc * 128:(hc + 1) * 128, :]),
                                wr=(f"wfp{j}",))
                        p.emit("vector", lambda e: e.tensor_single_scalar(
                            out=tneg[j][:], in_=wfp[j][:], scalar=hn_bc[:, wi:wi + 1],
                            op=ALU.is_lt), rd=(f"wfp{j}", "hn_bc"), wr=(f"tneg{j}",))
                        p.emit("vector", lambda e: e.scalar_tensor_tensor(
                            out=wqt[j][:], in0=wfp[j][:], scalar=h_bc[:, wi:wi + 1],
                            in1=tneg[j][:], op0=ALU.is_gt, op1=ALU.subtract),
                            rd=(f"wfp{j}", f"tneg{j}", "h_bc"),
                            wr=(f"wqt{j}", f"wfp{j}"))
                        p.emit("st", lambda e: e.dma_start(
                            out=wdst_s[hc], in_=wqt[j][:]), rd=(f"wqt{j}",))
                    quant_work.append(q_block)
            if upto == "quant":
                for qb in quant_work:
                    qb()
                p.barrier()
                return _finish(nc, p, dbg, root)

            xnT_sb = [sb1(f"xnT_sb{j}", [128, NDC * 512], BF16) for j in range(2)]
            pstage = [sb1(f"pstage{j}", [128, 512], BF16) for j in range(2)]
            pproj = [ph.enter_context(nc.psum_tensor(f"pproj{j}", [128, 512], F32))
                     for j in range(2)]
            ei = 0
            nq = (len(quant_work) + NTC - 1) // NTC
            for tc in range(NTC):
                cj = tc % 2
                p.emit("ld", lambda e, tc=tc, cj=cj: e.dma_start(
                    out=xnT_sb[cj][:], in_=xnT_s[tc]), rd=("xnTw",), wr=(f"xnT{cj}",))
                for qb in quant_work[tc * nq:(tc + 1) * nq]:
                    qb()
                for wkey, wsb, dst_s in (("wk", wk_sb, kT_s), ("wq", wq_sb, qT_s)):
                    for oc in range(NHL):
                        j = ei % 2
                        ei += 1
                        for dc in range(NDC):
                            p.emit("tensor", lambda e, wsb=wsb, oc=oc, dc=dc, j=j,
                                   cj=cj: e.matmul(
                                pproj[j][:],
                                wsb[:, dc * 1024 + oc * 128: dc * 1024 + (oc + 1) * 128],
                                xnT_sb[cj][:, dc * 512:(dc + 1) * 512],
                                start=(dc == 0), stop=(dc == NDC - 1)),
                                rd=(wkey, f"xnT{cj}"), wr=(f"pproj{j}",))
                        p.emit("vector", lambda e, j=j: e.tensor_copy(
                            out=pstage[j][:], in_=pproj[j][:]),
                            rd=(f"pproj{j}",), wr=(f"pstage{j}", f"pproj{j}"))
                        p.emit("st", lambda e, dst_s=dst_s, oc=oc, tc=tc, j=j:
                               e.dma_start(
                            out=dst_s[oc, :, tc * 512:(tc + 1) * 512],
                            in_=pstage[j][:]), rd=(f"pstage{j}",))
                for sv in range(4):
                    st_g = tc * 4 + sv
                    for dvc in range(2):
                        j = ei % 2
                        ei += 1
                        for dc in range(NDC):
                            p.emit("tensor", lambda e, sv=sv, dvc=dvc, dc=dc, j=j,
                                   cj=cj: e.matmul(
                                pproj[j][:],
                                xnT_sb[cj][:, dc * 512 + sv * 128: dc * 512 + (sv + 1) * 128],
                                wv_sb[:, dc * 1024 + dvc * 512: dc * 1024 + (dvc + 1) * 512],
                                start=(dc == 0), stop=(dc == NDC - 1)),
                                rd=("wv", f"xnT{cj}"), wr=(f"pproj{j}",))
                        p.emit("vector", lambda e, j=j: e.tensor_copy(
                            out=pstage[j][:], in_=pproj[j][:]),
                            rd=(f"pproj{j}",), wr=(f"pstage{j}", f"pproj{j}"))
                        for u in range(4):
                            p.emit("st", lambda e, dvc=dvc, u=u, st_g=st_g, j=j:
                                   e.dma_start(
                                out=v_s[4 * dvc + u, st_g],
                                in_=pstage[j][:, u * 128:(u + 1) * 128]),
                                rd=(f"pstage{j}",))
        p.barrier()
        if upto == "proj":
            return _finish(nc, p, dbg, root)

        with ExitStack() as ph:
            sb1 = lambda n_, s_, dt: ph.enter_context(nc.sbuf_tensor(n_, s_, dt))
            zT_sb = sb1("zT_sb", [128, NHL * NTT * 128], BF16)
            pa = ph.enter_context(ExitStack())
            sb2 = lambda n_, s_, dt: pa.enter_context(nc.sbuf_tensor(n_, s_, dt))
            v_hb = [sb2(f"v_hb{j}", [128, NTT * 128], BF16) for j in range(2)]
            kT_sb = [sb2(f"kT_sb{j}", [128, T], BF16) for j in range(2)]
            qT_sb = [sb2(f"qT_sb{j}", [128, T], BF16) for j in range(2)]
            pbuf = [sb2(f"pbuf{j}", [128, T], BF16) for j in range(4)]
            ptbuf = sb2("ptbuf", [128, NTT * 512], BF16)
            smv = [sb2(f"smv{j}", [128, 4], F32) for j in range(4)]
            dtmp = [sb2(f"dtmp{j}", [128, 512], BF16) for j in range(4)]
            lsums = [sb2(f"lsums{j}", [128, 8], F32) for j in range(4)]
            psum_s = [pa.enter_context(nc.psum_tensor(f"psum_s{j}", [128, 512], F32))
                      for j in range(4)]
            psum_pt = [pa.enter_context(nc.psum_tensor(f"psum_pt{j}", [128, 512], BF16))
                       for j in range(2)]
            psum_z = [pa.enter_context(nc.psum_tensor(f"psum_z{j}", [128, 512], F32))
                      for j in range(2)]
            s_ctr = [0]
            pt_ctr = [0]

            def tile_S(ti, tj, hj):
                st = ti // 4
                i = ti % 4
                for kc in range(st + 1):
                    sj = s_ctr[0] % 4
                    s_ctr[0] += 1
                    p.emit("tensor", lambda e, ti=ti, kc=kc, sj=sj, hj=hj:
                           e.matmul(
                        psum_s[sj][:],
                        qT_sb[hj][:, ti * 128:(ti + 1) * 128],
                        kT_sb[hj][:, kc * 512:(kc + 1) * 512],
                        start=True, stop=True),
                        rd=(f"q{hj}", f"k{hj}"), wr=(f"ps{sj}",))
                    if kc == st:
                        p.emit("scalar", lambda e, tj=tj, sj=sj: e.activation(
                            out=dtmp[tj][:], in_=psum_s[sj][:],
                            func=AF.Exp, scale=SCALE),
                            rd=(f"ps{sj}",), wr=(f"dt{tj}", f"ps{sj}"))
                        p.emit("vector", lambda e, kc=kc, i=i, tj=tj, st=st:
                               e.scalar_tensor_tensor(
                            out=pbuf[tj][:, kc * 512:(kc + 1) * 512],
                            in0=dtmp[tj][:], scalar=1.0,
                            in1=cmask[:, i * 512:(i + 1) * 512],
                            op0=ALU.mult, op1=ALU.mult,
                            accum_out=lsums[tj][:, st:st + 1]),
                            rd=(f"dt{tj}", "cmask"),
                            wr=(f"pb{tj}", f"ls{tj}"))
                    else:
                        p.emit("scalar", lambda e, kc=kc, tj=tj, sj=sj:
                               e.activation(
                            out=pbuf[tj][:, kc * 512:(kc + 1) * 512],
                            in_=psum_s[sj][:], func=AF.Exp, scale=SCALE,
                            accum_out=lsums[tj][:, kc:kc + 1]),
                            rd=(f"ps{sj}",),
                            wr=(f"pb{tj}", f"ls{tj}", f"ps{sj}"))

            def tile_soft(ti, tj):
                st = ti // 4
                lsum = smv[tj][:, 1:2]
                rl = smv[tj][:, 2:3]
                p.emit("vector", lambda e, st=st, tj=tj, lsum=lsum:
                       e.reduce_sum(
                    out=lsum, in_=lsums[tj][:, 0:st + 1], axis=AX.X),
                    rd=(f"ls{tj}",), wr=(f"smv{tj}s",))
                p.emit("vector", lambda e, rl=rl, lsum=lsum: e.reciprocal(
                    rl, lsum), rd=(f"smv{tj}s",), wr=(f"smv{tj}",))
                for scq in range(st + 1):
                    p.emit("vector", lambda e, scq=scq, tj=tj, rl=rl:
                           e.tensor_single_scalar(
                        out=pbuf[tj][:, scq * 512:(scq + 1) * 512],
                        in_=pbuf[tj][:, scq * 512:(scq + 1) * 512],
                        scalar=rl, op=ALU.mult),
                        rd=(f"pb{tj}", f"smv{tj}"), wr=(f"pbn{tj}_{scq}",))

            def tile_pt(ti, tj):
                st = ti // 4
                i = ti % 4
                nsc_i = ti + 1
                nsc = 4 * (st + 1)
                for scq in range(st + 1):
                    cnt = min(4, nsc_i - 4 * scq)
                    pj = pt_ctr[0] % 2
                    pt_ctr[0] += 1
                    for u in range(cnt):
                        sc = scq * 4 + u
                        p.emit("tensor", lambda e, sc=sc, u=u, tj=tj, pj=pj:
                               e.transpose(
                            psum_pt[pj][:, u * 128:(u + 1) * 128],
                            pbuf[tj][:, sc * 128:(sc + 1) * 128], ident[:]),
                            rd=(f"pbn{tj}_{scq}", f"pb{tj}", "ident"),
                            wr=(f"ppt{pj}",))
                    dst = ptbuf[:].rearrange("p (c f) -> p c f", f=512)[
                        :, scq * 4: scq * 4 + cnt, i * 128:(i + 1) * 128]
                    src = psum_pt[pj][:, 0:cnt * 128].rearrange(
                        "p (c f) -> p c f", f=128)
                    p.emit("vector", lambda e, dst=dst, src=src:
                           e.tensor_copy(out=dst, in_=src),
                           rd=(f"ppt{pj}",),
                           wr=(f"ppt{pj}",) + tuple(
                               f"ptc{scq * 4 + u}_{i}" for u in range(cnt)))
                for sc in range(nsc_i, nsc):
                    p.emit("vector", lambda e, sc=sc, i=i: e.memset(
                        ptbuf[:, sc * 512 + i * 128: sc * 512 + (i + 1) * 128],
                        0.0), wr=(f"ptc{sc}_{i}",))

            def do_z(st, vj, h):
                nsc = 4 * (st + 1)
                zj = st % 2
                for sc in range(nsc):
                    p.emit("tensor", lambda e, sc=sc, vj=vj, zj=zj: e.matmul(
                        psum_z[zj][:],
                        v_hb[vj][:, sc * 128:(sc + 1) * 128],
                        ptbuf[:, sc * 512:(sc + 1) * 512],
                        start=(sc == 0), stop=(sc == nsc - 1)),
                        rd=(f"v{vj}",) + tuple(
                            f"ptc{sc}_{i2}" for i2 in range(4)),
                        wr=(f"pz{zj}",))
                p.emit("vector", lambda e, h=h, st=st, zj=zj: e.tensor_copy(
                    out=zT_sb[:, (h * NTT + st * 4) * 128:
                              (h * NTT + st * 4 + 4) * 128],
                    in_=psum_z[zj][:]),
                    rd=(f"pz{zj}",), wr=("zT", f"pz{zj}"))

            for h in range(NHL):
                hj = h % 2
                vj = hj
                p.emit("ld", lambda e, h=h, hj=hj: e.dma_start(
                    out=v_hb[hj][:].rearrange("p (c f) -> p c f", c=NTT),
                    in_=v_s[h].rearrange("c p f -> p c f")), wr=(f"v{hj}",))
                p.emit("ld", lambda e, h=h, hj=hj: e.dma_start(
                    out=kT_sb[hj][:], in_=kT_s[h]), wr=(f"k{hj}",))
                p.emit("ld", lambda e, h=h, hj=hj: e.dma_start(
                    out=qT_sb[hj][:], in_=qT_s[h]), wr=(f"q{hj}",))
                pend = []

                def pop_one():
                    pti, ptj = pend.pop(0)
                    tile_soft(pti, ptj)
                    tile_pt(pti, ptj)
                    if pti % 4 == 3:
                        do_z(pti // 4, vj, h)

                for st in range(NTC):
                    for i in range(4):
                        ti = st * 4 + i
                        tile_S(ti, ti % 4, hj)
                        pend.append((ti, ti % 4))
                        if len(pend) > 2:
                            pop_one()
                while pend:
                    pop_one()
            if upto == "p2":
                p.barrier()
                return _finish(nc, p, dbg, root)

            pa.close()
            p.barrier()
            wo_sb = sb1("wo_sb", [128, NHL * D], BF16)
            wol_t = sb1("wol_t", [128, D], F32)
            sst_t = sb1("sst_t", [128, 2 * D], F32)
            wol = wol_t[:]
            sstage = [sst_t[:, 0:D], sst_t[:, D:2 * D]]
            psum_wo = [ph.enter_context(
                nc.psum_tensor(f"psum_wo{j}", [128, 512], F32)) for j in range(2)]
            for hc in range(NHL):
                p.emit("ld", lambda e, hc=hc: e.dma_start(
                    out=wol, in_=woT[hc * 128:(hc + 1) * 128, :]), wr=("wol",))
                p.emit("scalar", lambda e, hc=hc: e.copy(
                    out=wo_sb[:, hc * D:(hc + 1) * D], in_=wol),
                    rd=("wol",), wr=("wo", "wol"))
            sstores = []
            for tt in range(NTT):
                sj = tt % 2
                for jc in range(4):
                    wj = jc % 2
                    for hc in range(NHL):
                        p.emit("tensor", lambda e, tt=tt, jc=jc, hc=hc, wj=wj:
                               e.matmul(
                            psum_wo[wj][:],
                            zT_sb[:, (hc * NTT + tt) * 128:(hc * NTT + tt + 1) * 128],
                            wo_sb[:, hc * D + jc * 512: hc * D + (jc + 1) * 512],
                            start=(hc == 0), stop=(hc == NHL - 1)),
                            rd=("zT", "wo"), wr=(f"pwo{wj}",))
                    p.emit("vector", lambda e, jc=jc, wj=wj, sj=sj: e.tensor_copy(
                        out=sstage[sj][:, jc * 512:(jc + 1) * 512],
                        in_=psum_wo[wj][:]),
                        rd=(f"pwo{wj}",), wr=(f"sst{sj}", f"pwo{wj}"))
                sstores.append(p.emit("gd", lambda e, tt=tt, sj=sj:
                                      e.indirect_dma_start(
                    out=s_pool,
                    out_offset=IndirectOffsetOnAxis(ap=rows_sb[:, tt:tt + 1],
                                                    axis=0),
                    in_=sstage[sj][:], in_offset=None),
                    rd=(f"sst{sj}", "rows")))
        p.barrier()
        sbar = pair_barrier(after=tuple(sstores))
        p.res["s_ready"] = {"w": sbar, "r": {}}
        if upto == "p2c":
            return _finish(nc, p, dbg, root)

        with ExitStack() as ph:
            sb1 = lambda n_, s_, dt: ph.enter_context(nc.sbuf_tensor(n_, s_, dt))
            xnT0 = sb1("xnT0", [128, NDC * 512], BF16)
            xnT1 = sb1("xnT1", [128, NDC * 512], BF16)
            w1q_sb = [sb1(f"w1q_sb{j}", [128, D], BF16) for j in range(2)]
            w2q_sb = [sb1(f"w2q_sb{j}", [128, D], BF16) for j in range(2)]
            w3f_sb = [sb1(f"w3f_sb{j}", [128, D], BF16) for j in range(3)]
            hidT = [sb1(f"hidT{j}", [128, NHC * 1024], BF16) for j in range(2)]
            sig = [sb1("sig0", [128, 512], F32)] * 2
            fstage = [sb1(f"fstage{j}", [128, D], F32) for j in range(2)]
            P8 = [ph.enter_context(nc.psum_tensor(f"P8_{k}", [128, 512], F32))
                  for k in range(8)]
            px1 = [P8[0], P8[1]]
            px2 = [P8[2], P8[3]]
            fsb = sb1("fsb", [128, D], F32)
            ssb = sb1("ssb", [128, D], F32)
            gsb = sb1("gsb", [128, D], F32)
            osb = sb1("osb", [128, D], F32)

            def combine_chunk(pair):
                fbar = pair_barrier(after=tuple(fstores[pair]))
                p.res[f"f_ready{pair}"] = {"w": fbar, "r": {}}
                for ttl in range(8):
                    tt = pair * 8 + ttl
                    p.emit("st", lambda e, tt=tt: e.dma_start(
                        out=fsb[:], in_=f_pool[tt * 128:(tt + 1) * 128, :]),
                        rd=(f"f_ready{pair}",), wr=("fsb",))
                    p.emit("st", lambda e, tt=tt: e.dma_start(
                        out=gsb[:], in_=f_pool[T + tt * 128:T + (tt + 1) * 128, :]),
                        rd=(f"f_ready{pair}",), wr=("gsb",))
                    p.emit("vector", lambda e: e.tensor_add(
                        fsb[:], fsb[:], gsb[:]),
                        rd=("fsb", "gsb"), wr=("fsb",))
                    p.emit("scalar", lambda e: e.activation(
                        out=gsb[:], in_=fsb[:], func=AF.Silu),
                        rd=("fsb",), wr=("gsb", "fsb"))
                    p.emit("st", lambda e, tt=tt: e.dma_start(
                        out=ssb[:], in_=s_pool[tt * 128:(tt + 1) * 128, :]),
                        rd=("s_ready",), wr=("ssb",))
                    p.emit("st", lambda e, tt=tt: e.dma_start(
                        out=osb[:], in_=s_pool[T + tt * 128:T + (tt + 1) * 128, :]),
                        rd=("s_ready",), wr=("osb",))
                    p.emit("vector", lambda e: e.tensor_add(
                        ssb[:], ssb[:], osb[:]),
                        rd=("ssb", "osb"), wr=("ssb",))
                    p.emit("vector", lambda e: e.tensor_mul(
                        gsb[:], gsb[:], ssb[:]),
                        rd=("gsb", "ssb"), wr=("gsb",))
                    p.emit("st", lambda e, tt=tt: e.dma_start(
                        out=osb[:], in_=x[tt * 128:(tt + 1) * 128, :]),
                        rd=("ssb",), wr=("osb",))
                    p.emit("vector", lambda e: e.tensor_add(
                        osb[:], gsb[:], osb[:]),
                        rd=("gsb", "osb"), wr=("osb",))
                    p.emit("st", lambda e, tt=tt: e.dma_start(
                        out=out[tt * 128:(tt + 1) * 128, :], in_=osb[:]),
                        rd=("osb",))

            fstores = [[] for _ in range(4)]
            for pair in range(4):
                hp = pair % 2
                p.emit("ld", lambda e, pair=pair: e.dma_start(
                    out=xnT0[:], in_=xnT_s[2 * pair]), wr=("x0",))
                p.emit("ld", lambda e, pair=pair: e.dma_start(
                    out=xnT1[:], in_=xnT_s[2 * pair + 1]), wr=("x1",))
                for hc in range(NHC):
                    wj = hc % 2
                    p.emit("ld", lambda e, hc=hc, wj=wj: e.dma_start(
                        out=w1q_sb[wj][:], in_=w1q_s[hc]), wr=(f"w1{wj}",))
                    p.emit("ld", lambda e, hc=hc, wj=wj: e.dma_start(
                        out=w2q_sb[wj][:], in_=w2q_s[hc]), wr=(f"w2{wj}",))
                    for sub, (xnT_b, xk) in ((0, (xnT0, "x0")), (1, (xnT1, "x1"))):
                        sj = (2 * hc + sub) % 2
                        for wq_b, wkk, px, pxk in (
                                (w1q_sb[wj], f"w1{wj}", px1[sj], f"P{sj}"),
                                (w2q_sb[wj], f"w2{wj}", px2[sj], f"P{2 + sj}")):
                            for dc in range(NDC):
                                p.emit("tensor", lambda e, wq_b=wq_b, px=px,
                                       xnT_b=xnT_b, dc=dc: e.matmul(
                                    px[:],
                                    wq_b[:, dc * 128:(dc + 1) * 128],
                                    xnT_b[:, dc * 512:(dc + 1) * 512],
                                    start=(dc == 0), stop=(dc == NDC - 1)),
                                    rd=(wkk, xk), wr=(pxk,))
                        p.emit("scalar", lambda e, px=px1[sj], sg=sig[sj]:
                               e.activation(
                            out=sg[:], in_=px[:], func=AF.Sigmoid),
                            rd=(f"P{sj}",), wr=(f"sig{sj}", f"P{sj}"))
                        p.emit("vector", lambda e, hc=hc, sub=sub, px=px2[sj],
                               sg=sig[sj], hb=hidT[hp]: e.tensor_mul(
                            hb[:, hc * 1024 + sub * 512:
                               hc * 1024 + (sub + 1) * 512],
                            sg[:], px[:]),
                            rd=(f"sig{sj}", f"P{2 + sj}"),
                            wr=(f"hidT{hp}", f"P{2 + sj}"),
                            tag=f"hid_p{pair}_hc{hc}_s{sub}")
                if pair == 0 and "hid0" in debug:
                    hd = nc.dram_tensor("dbg_hid0", [128, NHC * 1024], BF16,
                                        kind="ExternalOutput").ap()
                    p.emit("st", lambda e, hd=hd: e.dma_start(
                        out=hd, in_=hidT[0][:]), rd=("hidT0",))
                for tpair in range(4):
                    for hc in range(NHC):
                        wj = hc % 3
                        p.emit("ld", lambda e, hc=hc, wj=wj: e.dma_start(
                            out=w3f_sb[wj][:], in_=w3q_s[hc]), wr=(f"w3{wj}",))
                        for jc in range(4):
                            for t2 in range(2):
                                ttl = tpair * 2 + t2
                                p.emit("tensor", lambda e, hc=hc, ttl=ttl,
                                       pfk=P8[jc * 2 + t2], wj=wj, jc=jc,
                                       hb=hidT[hp]: e.matmul(
                                    pfk[:],
                                    hb[:, hc * 1024 + ttl * 128:
                                       hc * 1024 + (ttl + 1) * 128],
                                    w3f_sb[wj][:, jc * 512:(jc + 1) * 512],
                                    start=(hc == 0), stop=(hc == NHC - 1)),
                                    rd=(f"hidT{hp}", f"w3{wj}"),
                                    wr=(f"P{jc * 2 + t2}",))
                    for t2 in range(2):
                        ttl = tpair * 2 + t2
                        tt = pair * 8 + ttl
                        fj = ttl % 2
                        for jc in range(4):
                            p.emit("scalar", lambda e, pfk=P8[jc * 2 + t2],
                                   fj=fj, jc=jc: e.copy(
                                out=fstage[fj][:, jc * 512:(jc + 1) * 512],
                                in_=pfk[:]),
                                rd=(f"P{jc * 2 + t2}",),
                                wr=(f"fst{fj}", f"P{jc * 2 + t2}"))
                        fstores[pair].append(p.emit(
                            "gd", lambda e, tt=tt, fj=fj:
                            e.indirect_dma_start(
                                out=f_pool,
                                out_offset=IndirectOffsetOnAxis(
                                    ap=rows_sb[:, tt:tt + 1], axis=0),
                                in_=fstage[fj][:], in_offset=None),
                            rd=(f"fst{fj}", "rows")))
                if not COMBINE_DEFER:
                    combine_chunk(pair)
            if COMBINE_DEFER:
                p.barrier()
                for pr in range(4):
                    combine_chunk(pr)
        if upto == "p3":
            return _finish(nc, p, dbg, root)
        return _finish(nc, p, dbg, root)


_NC_CACHE = {}


def _get_program():
    key = (UPTO, tuple(DEBUG_DUMPS))
    if key not in _NC_CACHE:
        _NC_CACHE[key] = build_program(UPTO, DEBUG_DUMPS)
    return _NC_CACHE[key]


def kernel(L0_attn_x, L0_attn_Wq, L0_attn_Wk, L0_attn_Wv, L0_attn_Wo,
           L0_ffn_w1, L0_ffn_w2, L0_ffn_w3):
    x = np.ascontiguousarray(np.asarray(L0_attn_x, dtype=np.float32))
    wq = np.asarray(L0_attn_Wq, dtype=np.float32)
    wk = np.asarray(L0_attn_Wk, dtype=np.float32)
    wv = np.asarray(L0_attn_Wv, dtype=np.float32)
    wo = np.asarray(L0_attn_Wo, dtype=np.float32)
    w1 = np.asarray(L0_ffn_w1, dtype=np.float32)
    w2 = np.asarray(L0_ffn_w2, dtype=np.float32)
    w3 = np.asarray(L0_ffn_w3, dtype=np.float32)

    wqT = np.ascontiguousarray(wq.T)
    wkT = np.ascontiguousarray(wk.T)
    wvT = np.ascontiguousarray(wv.T)
    woT = np.ascontiguousarray(wo.T)
    w1T = np.ascontiguousarray(w1.T)
    w2T = np.ascontiguousarray(w2.T)
    w3T = np.ascontiguousarray(w3.T)

    nc, _ = _get_program()
    in_maps = []
    for c in range(8):
        b, g = c // 2, c % 2
        in_maps.append({
            "x": np.ascontiguousarray(x[b]),
            "wqT": np.ascontiguousarray(wqT[:, g * 1024:(g + 1) * 1024]),
            "wkT": np.ascontiguousarray(wkT[:, g * 1024:(g + 1) * 1024]),
            "wvT": np.ascontiguousarray(wvT[:, g * 1024:(g + 1) * 1024]),
            "woT": np.ascontiguousarray(woT[g * 1024:(g + 1) * 1024, :]),
            "w1T": np.ascontiguousarray(w1T[:, g * HF:(g + 1) * HF]),
            "w2T": np.ascontiguousarray(w2T[:, g * HF:(g + 1) * HF]),
            "w3T": np.ascontiguousarray(w3T[g * HF:(g + 1) * HF, :]),
            "rows_idx": (g * T + np.arange(NTT, dtype=np.int32)[None, :] * 128
                         + np.arange(128, dtype=np.int32)[:, None]).astype(np.int32),
        })
    res = run_bass_kernel_spmd(nc, in_maps, core_ids=list(range(8)))
    out = np.empty((4, T, D), dtype=np.float32)
    for b in range(4):
        out[b] = res.results[2 * b]["out"]
    return out


# version 10
# speedup vs baseline: 3.1730x; 1.0611x over previous
import numpy as np
import ml_dtypes
from contextlib import ExitStack

import concourse.bass as bass
import concourse.mybir as mybir
from concourse.bass_utils import run_bass_kernel_spmd
from concourse.bass_core_rust import IndirectOffsetOnAxis

F32 = mybir.dt.float32
BF16 = mybir.dt.bfloat16
AF = mybir.ActivationFunctionType
ALU = mybir.AluOpType
AX = mybir.AxisListType

T = 4096
D = 2048
NHL = 8
HF = 2816
NHC = HF // 128
NTT = T // 128
NTC = T // 512
NDC = D // 128
SCALE = 1.0 / float(np.sqrt(128.0))
LN_EPS = 1e-5
INV_D = 1.0 / D
Q_S1 = 0.5 / (5632.0 * 2048.0)
Q_S2 = 0.5e-8
GROUPS = [[0, 1], [2, 3], [4, 5], [6, 7]]

UPTO = "full"
FIFO_SKIP = True
DMA_STRICT = False
COMBINE_DEFER = True
DEBUG_DUMPS = ()

QINFO = {
    "tensor": ("tensor", 1),
    "vector": ("vector", 1),
    "scalar": ("scalar", 1),
    "gpsimd": ("gpsimd", 1),
    "ld": ("sync", 16),
    "st": ("scalar", 16),
    "gd": ("gpsimd", 16),
}
QUEUES = tuple(QINFO)


class Prog:
    def __init__(self):
        self.items = []
        self.res = {}
        self.pending = {q: set() for q in QUEUES}
        self.last_in_q = {q: None for q in QUEUES}

    def emit(self, q, fn, rd=(), wr=(), tag="", after=()):
        idx = len(self.items)
        deps = set(self.pending[q])
        deps.update(after)
        self.pending[q] = set()
        for k in rd:
            e = self.res.get(k)
            if e is not None and e["w"] is not None:
                deps.add(e["w"])
        for k in wr:
            e = self.res.get(k)
            if e is not None:
                if e["w"] is not None:
                    deps.add(e["w"])
                deps.update(e["r"].values())
        for k in rd:
            self.res.setdefault(k, {"w": None, "r": {}})["r"][q] = idx
        for k in wr:
            e = self.res.setdefault(k, {"w": None, "r": {}})
            e["w"] = idx
            e["r"] = {}
        deps.discard(idx)
        self.items.append({"q": q, "fn": fn, "deps": deps, "tag": tag})
        self.last_in_q[q] = idx
        return idx

    def barrier(self):
        lasts = {i for i in self.last_in_q.values() if i is not None}
        for q in QUEUES:
            self.pending[q] |= lasts
        self.res = {}


def _finish(nc, p, dbg, root):
    bounce = None
    for name, (dbg_ap, src_ap) in dbg.items():
        if name in ("s_pool", "f_pool"):
            if bounce is None:
                bounce = root.enter_context(
                    nc.sbuf_tensor("dbg_bounce", [128, D], F32))
            for r0 in range(0, src_ap.shape[0], 128):
                p.emit("ld", lambda e, ap=src_ap, r0=r0: e.dma_start(
                    out=bounce[:], in_=ap[r0:r0 + 128, :]),
                    rd=(name,), wr=("dbg_b",))
                p.emit("st", lambda e, dbg_ap=dbg_ap, r0=r0: e.dma_start(
                    out=dbg_ap[r0:r0 + 128, :], in_=bounce[:]), rd=("dbg_b",),
                    wr=("dbg_b",))
            continue
        p.emit("st", lambda e, dbg_ap=dbg_ap, ap=src_ap: e.dma_start(
            out=dbg_ap, in_=ap), rd=(name,))
    if dbg:
        pass

    items = p.items
    n = len(items)
    hard = [None] * n
    for idx, it in enumerate(items):
        my_eng = QINFO[it["q"]][0]
        hd = []
        for d in it["deps"]:
            dq = items[d]["q"]
            d_eng, d_inc = QINFO[dq][0], QINFO[dq][1]
            if FIFO_SKIP and d_inc == 1 and d_eng == my_eng == "tensor" \
                    and it["q"] == "tensor":
                continue
            hd.append(d)
        hard[idx] = hd
    needed = set()
    for hd in hard:
        needed.update(d for d in hd if QINFO[items[d]["q"]][1] == 1)
    NLANE = 8
    val = [0] * n
    sem_of = [None] * n
    cnt = {}
    dseq = {q: 0 for q in QUEUES}
    for idx, it in enumerate(items):
        q = it["q"]
        if QINFO[q][1] == 16:
            lane = dseq[q] % NLANE
            dseq[q] += 1
            sid = f"{q}{lane}"
            cnt[sid] = cnt.get(sid, 0) + 16
            val[idx] = cnt[sid]
            sem_of[idx] = sid
            it["inc"] = 16
        elif idx in needed:
            cnt[q] = cnt.get(q, 0) + 1
            val[idx] = cnt[q]
            sem_of[idx] = q
            it["inc"] = 1
        else:
            it["inc"] = 0
        it["sem"] = sem_of[idx]
    satisfied = {q: {} for q in QUEUES}
    waits = [[] for _ in range(n)]
    for idx, it in enumerate(items):
        q = it["q"]
        for d in sorted(hard[idx]):
            sid = sem_of[d]
            v = val[d]
            assert v > 0 and sid is not None
            if satisfied[q].get(sid, 0) >= v:
                continue
            waits[idx].append((sid, v))
            satisfied[q][sid] = v
        it["waits"] = waits[idx]

    sems = {}
    for sid in cnt:
        sems[sid] = root.enter_context(nc.semaphore("sem_" + sid))
    blk = root.enter_context(nc.Block())

    final_counts = dict(cnt)

    def run_body(body_eng):
        def body(eng):
            for it in items:
                if QINFO[it["q"]][0] != body_eng:
                    continue
                for (dq, v) in it["waits"]:
                    eng.wait_ge(sems[dq], v)
                inst = it["fn"](eng)
                if it["inc"]:
                    inst.then_inc(sems[it["sem"]], it["inc"])
            if body_eng == "sync":
                for sid, v in final_counts.items():
                    eng.wait_ge(sems[sid], v)
        return body

    blk.sync(run_body("sync"))
    blk.tensor(run_body("tensor"))
    blk.vector(run_body("vector"))
    blk.scalar(run_body("scalar"))
    blk.gpsimd(run_body("gpsimd"))
    return nc, p


def build_program(upto=UPTO, debug=()):
    nc = bass.Bass("TRN2", target_bir_lowering=False)

    x = nc.dram_tensor("x", [T, D], F32, kind="ExternalInput").ap()
    wqT = nc.dram_tensor("wqT", [D, 1024], F32, kind="ExternalInput").ap()
    wkT = nc.dram_tensor("wkT", [D, 1024], F32, kind="ExternalInput").ap()
    wvT = nc.dram_tensor("wvT", [D, 1024], F32, kind="ExternalInput").ap()
    woT = nc.dram_tensor("woT", [1024, D], F32, kind="ExternalInput").ap()
    w1T = nc.dram_tensor("w1T", [D, HF], F32, kind="ExternalInput").ap()
    w2T = nc.dram_tensor("w2T", [D, HF], F32, kind="ExternalInput").ap()
    w3T = nc.dram_tensor("w3T", [HF, D], F32, kind="ExternalInput").ap()
    out = nc.dram_tensor("out", [T, D], F32, kind="ExternalOutput").ap()

    xnT_s = nc.dram_tensor("xnT_s", [NTC, 128, NDC * 512], BF16, kind="Internal").ap()
    kT_s = nc.dram_tensor("kT_s", [NHL, 128, T], BF16, kind="Internal").ap()
    qT_s = nc.dram_tensor("qT_s", [NHL, 128, T], BF16, kind="Internal").ap()
    v_s = nc.dram_tensor("v_s", [NHL, NTT, 128, 128], BF16, kind="Internal").ap()
    w1q_s = nc.dram_tensor("w1q_s", [NHC, 128, D], BF16, kind="Internal").ap()
    w2q_s = nc.dram_tensor("w2q_s", [NHC, 128, D], BF16, kind="Internal").ap()
    w3q_s = nc.dram_tensor("w3q_s", [NHC, 128, D], BF16, kind="Internal").ap()
    rows_idx = nc.dram_tensor("rows_idx", [128, NTT], mybir.dt.int32,
                              kind="ExternalInput").ap()
    s_pool = nc.dram_tensor("s_pool", [2 * T, D], F32, kind="Internal",
                            addr_space="Shared").ap()
    f_pool = nc.dram_tensor("f_pool", [2 * T, D], F32, kind="Internal",
                            addr_space="Shared").ap()
    bar_in = nc.dram_tensor("bar_in", [1, 4], F32, kind="Internal").ap()
    bar_out = nc.dram_tensor("bar_out", [1, 4], F32, kind="Internal").ap()
    gam_part = nc.dram_tensor("gam_part", [1, 4], F32, kind="Internal").ap()
    gam_full = nc.dram_tensor("gam_full", [1, 4], F32, kind="Internal").ap()
    colb = nc.dram_tensor("colb", [128], F32, kind="Internal").ap()

    dbg = {}
    scratch_map = {"xnT_s": xnT_s, "kT_s": kT_s, "qT_s": qT_s, "v_s": v_s,
                   "w1q_s": w1q_s, "w2q_s": w2q_s, "w3q_s": w3q_s,
                   "s_pool": s_pool, "f_pool": f_pool, "gam_full": gam_full}
    for name in debug:
        if name not in scratch_map:
            continue
        src = scratch_map[name]
        dbg[name] = (nc.dram_tensor("dbg_" + name, list(src.shape),
                                    src.dtype, kind="ExternalOutput").ap(), src)

    ident_dram = nc.inline_tensor(np.eye(128, dtype=ml_dtypes.bfloat16), "ident_c")
    ones_dram = nc.inline_tensor(np.ones((1, 128), dtype=np.float32), "ones_c")
    _mk = np.zeros((4, 128, 512), dtype=np.float32)
    for _i in range(4):
        _p = np.arange(128)[:, None]
        _f = np.arange(512)[None, :]
        _mk[_i] = np.where(128 * _i + _p - _f >= 0, 1.0, 0.0)
    cmask_dram = nc.inline_tensor(_mk.astype(ml_dtypes.bfloat16), "cmask_c")

    p = Prog()

    with ExitStack() as root:
        sb = lambda n_, s_, dt: root.enter_context(nc.sbuf_tensor(n_, s_, dt))
        ident = sb("ident_sb", [128, 128], BF16)
        ones_f = sb("ones_f", [1, 128], F32)
        h_bc = sb("h_bc", [128, 4], F32)
        hn_bc = sb("hn_bc", [128, 4], F32)
        cmask = sb("cmask_sb", [128, 4 * 512], BF16)
        rows_sb = sb("rows_sb", [128, NTT], mybir.dt.int32)
        barv = sb("barv", [1, 4], F32)

        p.emit("ld", lambda e: e.dma_start(out=ident[:], in_=ident_dram.ap()),
               wr=("ident",))
        p.emit("ld", lambda e: e.dma_start(out=ones_f[:], in_=ones_dram.ap()),
               wr=("ones",))
        p.emit("ld", lambda e: e.dma_start(
            out=cmask[:].rearrange("p (c f) -> p c f", c=4),
            in_=cmask_dram.ap().rearrange("c p f -> p c f")), wr=("cmask",))
        p.emit("ld", lambda e: e.dma_start(out=rows_sb[:], in_=rows_idx),
               wr=("rows",))
        p.emit("vector", lambda e: e.memset(barv[:], 1.0), wr=("barv",))
        bar_ctr = [0]

        def pair_barrier(after=()):
            bar_ctr[0] += 1
            bid = p.emit("st", lambda e: e.dma_start(out=bar_in, in_=barv[:]),
                         rd=("barv",), after=after)
            return p.emit("gpsimd", lambda e: e.collective_compute(
                "AllReduce", ALU.add, replica_groups=GROUPS,
                ins=[bar_in.opt()], outs=[bar_out.opt()]), after=(bid,))

        with ExitStack() as ph:
            sb1 = lambda n_, s_, dt: ph.enter_context(nc.sbuf_tensor(n_, s_, dt))
            xt = [sb1(f"xt{j}", [128, D], F32) for j in range(2)]
            xsq = sb1("xsq", [128, D], F32)
            xn4 = [sb1(f"xn{i}", [128, D], BF16) for i in range(4)]
            xnTc = [sb1(f"xnTc{j}", [128, NDC * 512], BF16) for j in range(2)]
            lnv = [sb1(f"lnv{j}", [128, 8], F32) for j in range(2)]
            ptr = [ph.enter_context(nc.psum_tensor(f"ptr{j}", [128, 512], BF16))
                   for j in range(2)]

            for tc in range(NTC):
                cp = tc % 2
                for i in range(4):
                    ti = tc * 4 + i
                    j = ti % 2
                    xb, lv, xi = xt[j], lnv[j], xn4[i]
                    kx, kl, kxn = f"xt{j}", f"lnv{j}", f"xn{i}"
                    nm, ssq, t1v = lv[:, 0:1], lv[:, 1:2], lv[:, 2:3]
                    msq, veps, stdv, rstd = lv[:, 3:4], lv[:, 4:5], lv[:, 5:6], lv[:, 6:7]
                    p.emit("ld", lambda e, ti=ti, xb=xb: e.dma_start(
                        out=xb[:], in_=x[ti * 128:(ti + 1) * 128, :]), wr=(kx,))
                    p.emit("vector", lambda e, xb=xb, nm=nm: e.reduce_sum(
                        out=nm, in_=xb[:], axis=AX.X, negate=True),
                        rd=(kx,), wr=(kl,))
                    p.emit("vector", lambda e, nm=nm: e.tensor_scalar_mul(
                        nm, nm, INV_D), rd=(kl,), wr=(kl,))
                    p.emit("scalar", lambda e, xb=xb, ssq=ssq: e.activation(
                        out=xsq[:], in_=xb[:], func=AF.Square, accum_out=ssq),
                        rd=(kx,), wr=("xsq", kl + "s"))
                    p.emit("vector", lambda e, ssq=ssq, t1v=t1v: e.tensor_scalar(
                        out=t1v, in0=ssq, scalar1=INV_D, scalar2=LN_EPS,
                        op0=ALU.mult, op1=ALU.add), rd=(kl + "s",), wr=(kl,))
                    p.emit("vector", lambda e, nm=nm, msq=msq: e.tensor_single_scalar(
                        out=msq, in_=nm, scalar=nm, op=ALU.mult), rd=(kl,), wr=(kl,))
                    p.emit("vector", lambda e, t1v=t1v, msq=msq, veps=veps:
                           e.tensor_sub(veps, t1v, msq), rd=(kl,), wr=(kl,))
                    p.emit("scalar", lambda e, veps=veps, stdv=stdv: e.activation(
                        out=stdv, in_=veps, func=AF.Sqrt), rd=(kl,), wr=(kl + "s",))
                    p.emit("vector", lambda e, stdv=stdv, rstd=rstd: e.reciprocal(
                        rstd, stdv), rd=(kl + "s",), wr=(kl,))
                    p.emit("vector", lambda e, xb=xb, nm=nm, rstd=rstd, xi=xi:
                           e.tensor_scalar(
                        out=xi[:], in0=xb[:], scalar1=nm, scalar2=rstd,
                        op0=ALU.add, op1=ALU.mult), rd=(kx, kl), wr=(kxn,))
                for dc in range(NDC):
                    pj = dc % 2
                    for i in range(4):
                        p.emit("tensor", lambda e, i=i, dc=dc, pj=pj: e.transpose(
                            ptr[pj][:, i * 128:(i + 1) * 128],
                            xn4[i][:, dc * 128:(dc + 1) * 128], ident[:]),
                            rd=(f"xn{i}", "ident"), wr=(f"ptr{pj}",))
                    p.emit("scalar", lambda e, dc=dc, pj=pj, cp=cp: e.copy(
                        out=xnTc[cp][:, dc * 512:(dc + 1) * 512], in_=ptr[pj][:]),
                        rd=(f"ptr{pj}",), wr=(f"xnTc{cp}", f"ptr{pj}"))
                p.emit("st", lambda e, tc=tc, cp=cp: e.dma_start(
                    out=xnT_s[tc], in_=xnTc[cp][:]), rd=(f"xnTc{cp}",))
            gw = [sb1(f"gw{j}", [128, 2816], F32) for j in range(2)]
            gcols = sb1("gcols", [128, 40], F32)
            gcol = sb1("gcol", [128, 1], F32)
            grow = sb1("grow", [1, 128], F32)
            gpart = sb1("gpart", [1, 4], F32)
            p.emit("vector", lambda e: e.memset(gpart[:], 0.0), wr=("gpart",))
            li = 0
            for wi, wsrc in enumerate((w1T, w2T, w3T)):
                nblk = NDC if wi < 2 else NHC
                ncol = HF if wi < 2 else D
                for blk_i in range(nblk):
                    j = li % 2
                    li += 1
                    p.emit("ld", lambda e, wsrc=wsrc, blk_i=blk_i, ncol=ncol, j=j:
                           e.dma_start(
                        out=gw[j][:, 0:ncol],
                        in_=wsrc[blk_i * 128:(blk_i + 1) * 128, :]), wr=(f"gw{j}",))
                    p.emit("vector", lambda e, blk_i=blk_i, ncol=ncol, j=j:
                           e.reduce_sum(
                        out=gcols[:, blk_i:blk_i + 1],
                        in_=gw[j][:, 0:ncol],
                        axis=AX.X, apply_absolute_value=True),
                        rd=(f"gw{j}",), wr=(f"gcols{blk_i}", f"gw{j}"))
                p.emit("vector", lambda e, nblk=nblk: e.reduce_sum(
                    out=gcol[:], in_=gcols[:, 0:nblk], axis=AX.X),
                    rd=tuple(f"gcols{b2}" for b2 in range(nblk)), wr=("gcol",))
                p.emit("st", lambda e: e.dma_start(out=colb, in_=gcol[:]),
                       rd=("gcol",), wr=("colb",))
                p.emit("ld", lambda e: e.dma_start(
                    out=grow[:], in_=colb.rearrange("(o p) -> o p", o=1)),
                    rd=("colb",), wr=("grow",))
                p.emit("vector", lambda e, wi=wi: e.reduce_sum(
                    out=gpart[:, wi:wi + 1], in_=grow[:], axis=AX.X),
                    rd=("grow",), wr=("gpart", "grow", "gcol"))
            p.emit("st", lambda e: e.dma_start(out=gam_part, in_=gpart[:]),
                   rd=("gpart",), wr=("gam_part",))
            p.emit("gpsimd", lambda e: e.collective_compute(
                "AllReduce", ALU.add, replica_groups=GROUPS,
                ins=[gam_part.opt()], outs=[gam_full.opt()]),
                rd=("gam_part",), wr=("gam_full",))
        p.barrier()
        if upto == "p1a":
            return _finish(nc, p, dbg, root)

        with ExitStack() as ph:
            sb1 = lambda n_, s_, dt: ph.enter_context(nc.sbuf_tensor(n_, s_, dt))
            wq_sb = sb1("wq_sb", [128, NDC * 1024], BF16)
            wk_sb = sb1("wk_sb", [128, NDC * 1024], BF16)
            wv_sb = sb1("wv_sb", [128, NDC * 1024], BF16)
            wload = [sb1(f"wload{j}", [128, 2816], F32) for j in range(2)]
            li = 0
            for wname, wdst, wsrc in (("q", wq_sb, wqT), ("k", wk_sb, wkT),
                                      ("v", wv_sb, wvT)):
                for dc in range(NDC):
                    j = li % 2
                    li += 1
                    p.emit("ld", lambda e, wsrc=wsrc, dc=dc, j=j: e.dma_start(
                        out=wload[j][:, 0:1024],
                        in_=wsrc[dc * 128:(dc + 1) * 128, :]), wr=(f"wload{j}",))
                    p.emit("scalar", lambda e, wdst=wdst, dc=dc, j=j: e.copy(
                        out=wdst[:, dc * 1024:(dc + 1) * 1024],
                        in_=wload[j][:, 0:1024]),
                        rd=(f"wload{j}",), wr=("w" + wname, f"wload{j}"))

            grow4 = sb1("grow4", [1, 4], F32)
            hrow = sb1("hrow", [1, 4], F32)
            hnrow = sb1("hnrow", [1, 4], F32)
            p.emit("ld", lambda e: e.dma_start(out=grow4[:], in_=gam_full),
                   rd=("gam_full",), wr=("grow4",))
            p.emit("vector", lambda e: e.tensor_scalar(
                out=hrow[:], in0=grow4[:], scalar1=Q_S1, scalar2=Q_S2,
                op0=ALU.mult, op1=ALU.add), rd=("grow4",), wr=("hrow",))
            p.emit("vector", lambda e: e.tensor_scalar_mul(hnrow[:], hrow[:], -1.0),
                   rd=("hrow",), wr=("hnrow",))
            pbc = ph.enter_context(nc.psum_tensor("pbc", [128, 4], F32))
            p.emit("tensor", lambda e: e.matmul(
                pbc[:], ones_f[:], hrow[:], start=True, stop=True),
                rd=("ones", "hrow"), wr=("pbc",))
            p.emit("vector", lambda e: e.tensor_copy(out=h_bc[:], in_=pbc[:]),
                   rd=("pbc",), wr=("h_bc", "pbc"))
            p.emit("tensor", lambda e: e.matmul(
                pbc[:], ones_f[:], hnrow[:], start=True, stop=True),
                rd=("ones", "hnrow"), wr=("pbc",))
            p.emit("vector", lambda e: e.tensor_copy(out=hn_bc[:], in_=pbc[:]),
                   rd=("pbc",), wr=("hn_bc", "pbc"))

            wfp = [sb1(f"wfp{j}", [128, D], F32) for j in range(2)]
            tneg = [sb1(f"tneg{j}", [128, D], BF16) for j in range(2)]
            wqt = [sb1(f"wqt{j}", [128, D], BF16) for j in range(2)]
            quant_work = []
            qi = 0
            for wi, (wsrc, wdst_s) in enumerate(((w1T, w1q_s), (w2T, w2q_s),
                                                 (w3T, w3q_s))):
                for hc in range(NHC):
                    j = qi % 2
                    qi += 1

                    def q_block(wi=wi, wsrc=wsrc, wdst_s=wdst_s, hc=hc, j=j):
                        if wi < 2:
                            src = wsrc[:, hc * 128:(hc + 1) * 128].rearrange(
                                "(c p) f -> p c f", p=128)
                            dst = wfp[j][:].rearrange("p (c f) -> p c f", c=NDC)
                            p.emit("ld", lambda e, src=src, dst=dst: e.dma_start(
                                out=dst, in_=src), wr=(f"wfp{j}",))
                        else:
                            p.emit("ld", lambda e: e.dma_start(
                                out=wfp[j][:], in_=wsrc[hc * 128:(hc + 1) * 128, :]),
                                wr=(f"wfp{j}",))
                        p.emit("vector", lambda e: e.tensor_single_scalar(
                            out=tneg[j][:], in_=wfp[j][:], scalar=hn_bc[:, wi:wi + 1],
                            op=ALU.is_lt), rd=(f"wfp{j}", "hn_bc"), wr=(f"tneg{j}",))
                        p.emit("vector", lambda e: e.scalar_tensor_tensor(
                            out=wqt[j][:], in0=wfp[j][:], scalar=h_bc[:, wi:wi + 1],
                            in1=tneg[j][:], op0=ALU.is_gt, op1=ALU.subtract),
                            rd=(f"wfp{j}", f"tneg{j}", "h_bc"),
                            wr=(f"wqt{j}", f"wfp{j}"))
                        p.emit("st", lambda e: e.dma_start(
                            out=wdst_s[hc], in_=wqt[j][:]), rd=(f"wqt{j}",))
                    quant_work.append(q_block)
            if upto == "quant":
                for qb in quant_work:
                    qb()
                p.barrier()
                return _finish(nc, p, dbg, root)

            xnT_sb = [sb1(f"xnT_sb{j}", [128, NDC * 512], BF16) for j in range(2)]
            pstage = [sb1(f"pstage{j}", [128, 512], BF16) for j in range(2)]
            pproj = [ph.enter_context(nc.psum_tensor(f"pproj{j}", [128, 512], F32))
                     for j in range(2)]
            ei = 0
            nq = (len(quant_work) + NTC - 1) // NTC
            for tc in range(NTC):
                cj = tc % 2
                p.emit("ld", lambda e, tc=tc, cj=cj: e.dma_start(
                    out=xnT_sb[cj][:], in_=xnT_s[tc]), rd=("xnTw",), wr=(f"xnT{cj}",))
                for qb in quant_work[tc * nq:(tc + 1) * nq]:
                    qb()
                for wkey, wsb, dst_s in (("wk", wk_sb, kT_s), ("wq", wq_sb, qT_s)):
                    for oc in range(NHL):
                        j = ei % 2
                        ei += 1
                        for dc in range(NDC):
                            p.emit("tensor", lambda e, wsb=wsb, oc=oc, dc=dc, j=j,
                                   cj=cj: e.matmul(
                                pproj[j][:],
                                wsb[:, dc * 1024 + oc * 128: dc * 1024 + (oc + 1) * 128],
                                xnT_sb[cj][:, dc * 512:(dc + 1) * 512],
                                start=(dc == 0), stop=(dc == NDC - 1)),
                                rd=(wkey, f"xnT{cj}"), wr=(f"pproj{j}",))
                        p.emit("vector", lambda e, j=j: e.tensor_copy(
                            out=pstage[j][:], in_=pproj[j][:]),
                            rd=(f"pproj{j}",), wr=(f"pstage{j}", f"pproj{j}"))
                        p.emit("st", lambda e, dst_s=dst_s, oc=oc, tc=tc, j=j:
                               e.dma_start(
                            out=dst_s[oc, :, tc * 512:(tc + 1) * 512],
                            in_=pstage[j][:]), rd=(f"pstage{j}",))
                for sv in range(4):
                    st_g = tc * 4 + sv
                    for dvc in range(2):
                        j = ei % 2
                        ei += 1
                        for dc in range(NDC):
                            p.emit("tensor", lambda e, sv=sv, dvc=dvc, dc=dc, j=j,
                                   cj=cj: e.matmul(
                                pproj[j][:],
                                xnT_sb[cj][:, dc * 512 + sv * 128: dc * 512 + (sv + 1) * 128],
                                wv_sb[:, dc * 1024 + dvc * 512: dc * 1024 + (dvc + 1) * 512],
                                start=(dc == 0), stop=(dc == NDC - 1)),
                                rd=("wv", f"xnT{cj}"), wr=(f"pproj{j}",))
                        p.emit("vector", lambda e, j=j: e.tensor_copy(
                            out=pstage[j][:], in_=pproj[j][:]),
                            rd=(f"pproj{j}",), wr=(f"pstage{j}", f"pproj{j}"))
                        for u in range(4):
                            p.emit("st", lambda e, dvc=dvc, u=u, st_g=st_g, j=j:
                                   e.dma_start(
                                out=v_s[4 * dvc + u, st_g],
                                in_=pstage[j][:, u * 128:(u + 1) * 128]),
                                rd=(f"pstage{j}",))
        p.barrier()
        if upto == "proj":
            return _finish(nc, p, dbg, root)

        with ExitStack() as ph:
            sb1 = lambda n_, s_, dt: ph.enter_context(nc.sbuf_tensor(n_, s_, dt))
            zT_sb = sb1("zT_sb", [128, NHL * NTT * 128], BF16)
            pa = ph.enter_context(ExitStack())
            sb2 = lambda n_, s_, dt: pa.enter_context(nc.sbuf_tensor(n_, s_, dt))
            v_hb = [sb2(f"v_hb{j}", [128, NTT * 128], BF16) for j in range(2)]
            kT_sb = [sb2(f"kT_sb{j}", [128, T], BF16) for j in range(2)]
            qT_sb = [sb2(f"qT_sb{j}", [128, T], BF16) for j in range(2)]
            pbuf = [sb2(f"pbuf{j}", [128, T], BF16) for j in range(4)]
            ptbuf = sb2("ptbuf", [128, NTT * 512], BF16)
            smv = [sb2(f"smv{j}", [128, 4], F32) for j in range(4)]
            dtmp = [sb2(f"dtmp{j}", [128, 512], BF16) for j in range(4)]
            lsums = [sb2(f"lsums{j}", [128, 8], F32) for j in range(4)]
            psum_s = [pa.enter_context(nc.psum_tensor(f"psum_s{j}", [128, 512], F32))
                      for j in range(4)]
            psum_pt = [pa.enter_context(nc.psum_tensor(f"psum_pt{j}", [128, 512], BF16))
                       for j in range(2)]
            psum_z = [pa.enter_context(nc.psum_tensor(f"psum_z{j}", [128, 512], F32))
                      for j in range(2)]
            s_ctr = [0]
            pt_ctr = [0]

            def tile_S(ti, tj, hj):
                st = ti // 4
                i = ti % 4
                for kc in range(st + 1):
                    sj = s_ctr[0] % 4
                    s_ctr[0] += 1
                    p.emit("tensor", lambda e, ti=ti, kc=kc, sj=sj, hj=hj:
                           e.matmul(
                        psum_s[sj][:],
                        qT_sb[hj][:, ti * 128:(ti + 1) * 128],
                        kT_sb[hj][:, kc * 512:(kc + 1) * 512],
                        start=True, stop=True),
                        rd=(f"q{hj}", f"k{hj}"), wr=(f"ps{sj}",))
                    if kc == st:
                        p.emit("scalar", lambda e, tj=tj, sj=sj: e.activation(
                            out=dtmp[tj][:], in_=psum_s[sj][:],
                            func=AF.Exp, scale=SCALE),
                            rd=(f"ps{sj}",), wr=(f"dt{tj}", f"ps{sj}"))
                    else:
                        p.emit("scalar", lambda e, kc=kc, tj=tj, sj=sj:
                               e.activation(
                            out=pbuf[tj][:, kc * 512:(kc + 1) * 512],
                            in_=psum_s[sj][:], func=AF.Exp, scale=SCALE,
                            accum_out=lsums[tj][:, kc:kc + 1]),
                            rd=(f"ps{sj}",),
                            wr=(f"pb{tj}", f"ls{tj}", f"ps{sj}"))

            def tile_soft(ti, tj):
                st = ti // 4
                i = ti % 4
                lsum = smv[tj][:, 1:2]
                rl = smv[tj][:, 2:3]
                p.emit("vector", lambda e, i=i, tj=tj, st=st:
                       e.scalar_tensor_tensor(
                    out=pbuf[tj][:, st * 512:(st + 1) * 512],
                    in0=dtmp[tj][:], scalar=1.0,
                    in1=cmask[:, i * 512:(i + 1) * 512],
                    op0=ALU.mult, op1=ALU.mult,
                    accum_out=lsums[tj][:, st:st + 1]),
                    rd=(f"dt{tj}", "cmask"),
                    wr=(f"pb{tj}", f"ls{tj}", f"dt{tj}"))
                p.emit("vector", lambda e, st=st, tj=tj, lsum=lsum:
                       e.reduce_sum(
                    out=lsum, in_=lsums[tj][:, 0:st + 1], axis=AX.X),
                    rd=(f"ls{tj}",), wr=(f"smv{tj}s",))
                p.emit("vector", lambda e, rl=rl, lsum=lsum: e.reciprocal(
                    rl, lsum), rd=(f"smv{tj}s",), wr=(f"smv{tj}",))
                for scq in range(st + 1):
                    p.emit("vector", lambda e, scq=scq, tj=tj, rl=rl:
                           e.tensor_single_scalar(
                        out=pbuf[tj][:, scq * 512:(scq + 1) * 512],
                        in_=pbuf[tj][:, scq * 512:(scq + 1) * 512],
                        scalar=rl, op=ALU.mult),
                        rd=(f"pb{tj}", f"smv{tj}"), wr=(f"pbn{tj}_{scq}",))

            def tile_pt(ti, tj):
                st = ti // 4
                i = ti % 4
                nsc_i = ti + 1
                nsc = 4 * (st + 1)
                for scq in range(st + 1):
                    cnt = min(4, nsc_i - 4 * scq)
                    pj = pt_ctr[0] % 2
                    pt_ctr[0] += 1
                    for u in range(cnt):
                        sc = scq * 4 + u
                        p.emit("tensor", lambda e, sc=sc, u=u, tj=tj, pj=pj:
                               e.transpose(
                            psum_pt[pj][:, u * 128:(u + 1) * 128],
                            pbuf[tj][:, sc * 128:(sc + 1) * 128], ident[:]),
                            rd=(f"pbn{tj}_{scq}", f"pb{tj}", "ident"),
                            wr=(f"ppt{pj}",))
                    dst = ptbuf[:].rearrange("p (c f) -> p c f", f=512)[
                        :, scq * 4: scq * 4 + cnt, i * 128:(i + 1) * 128]
                    src = psum_pt[pj][:, 0:cnt * 128].rearrange(
                        "p (c f) -> p c f", f=128)
                    ev_q = "vector" if pj == 0 else "scalar"
                    ev_fn = ((lambda e, dst=dst, src=src:
                              e.tensor_copy(out=dst, in_=src))
                             if pj == 0 else
                             (lambda e, dst=dst, src=src:
                              e.copy(out=dst, in_=src)))
                    p.emit(ev_q, ev_fn,
                           rd=(f"ppt{pj}",),
                           wr=(f"ppt{pj}",) + tuple(
                               f"ptc{scq * 4 + u}_{i}" for u in range(cnt)))
                for sc in range(nsc_i, nsc):
                    p.emit("vector", lambda e, sc=sc, i=i: e.memset(
                        ptbuf[:, sc * 512 + i * 128: sc * 512 + (i + 1) * 128],
                        0.0), wr=(f"ptc{sc}_{i}",))

            def do_z(st, vj, h):
                nsc = 4 * (st + 1)
                zj = st % 2
                for sc in range(nsc):
                    p.emit("tensor", lambda e, sc=sc, vj=vj, zj=zj: e.matmul(
                        psum_z[zj][:],
                        v_hb[vj][:, sc * 128:(sc + 1) * 128],
                        ptbuf[:, sc * 512:(sc + 1) * 512],
                        start=(sc == 0), stop=(sc == nsc - 1)),
                        rd=(f"v{vj}",) + tuple(
                            f"ptc{sc}_{i2}" for i2 in range(4)),
                        wr=(f"pz{zj}",))
                p.emit("vector", lambda e, h=h, st=st, zj=zj: e.tensor_copy(
                    out=zT_sb[:, (h * NTT + st * 4) * 128:
                              (h * NTT + st * 4 + 4) * 128],
                    in_=psum_z[zj][:]),
                    rd=(f"pz{zj}",), wr=("zT", f"pz{zj}"))

            for h in range(NHL):
                hj = h % 2
                vj = hj
                p.emit("ld", lambda e, h=h, hj=hj: e.dma_start(
                    out=v_hb[hj][:].rearrange("p (c f) -> p c f", c=NTT),
                    in_=v_s[h].rearrange("c p f -> p c f")), wr=(f"v{hj}",))
                p.emit("ld", lambda e, h=h, hj=hj: e.dma_start(
                    out=kT_sb[hj][:], in_=kT_s[h]), wr=(f"k{hj}",))
                p.emit("ld", lambda e, h=h, hj=hj: e.dma_start(
                    out=qT_sb[hj][:], in_=qT_s[h]), wr=(f"q{hj}",))
                pend = []

                def pop_one():
                    pti, ptj = pend.pop(0)
                    tile_soft(pti, ptj)
                    tile_pt(pti, ptj)
                    if pti % 4 == 3:
                        do_z(pti // 4, vj, h)

                for st in range(NTC):
                    for i in range(4):
                        ti = st * 4 + i
                        tile_S(ti, ti % 4, hj)
                        pend.append((ti, ti % 4))
                        if len(pend) > 2:
                            pop_one()
                while pend:
                    pop_one()
            if upto == "p2":
                p.barrier()
                return _finish(nc, p, dbg, root)

            pa.close()
            p.barrier()
            wo_sb = sb1("wo_sb", [128, NHL * D], BF16)
            wol_t = sb1("wol_t", [128, D], F32)
            sst_t = sb1("sst_t", [128, 2 * D], F32)
            wol = wol_t[:]
            sstage = [sst_t[:, 0:D], sst_t[:, D:2 * D]]
            psum_wo = [ph.enter_context(
                nc.psum_tensor(f"psum_wo{j}", [128, 512], F32)) for j in range(2)]
            for hc in range(NHL):
                p.emit("ld", lambda e, hc=hc: e.dma_start(
                    out=wol, in_=woT[hc * 128:(hc + 1) * 128, :]), wr=("wol",))
                p.emit("scalar", lambda e, hc=hc: e.copy(
                    out=wo_sb[:, hc * D:(hc + 1) * D], in_=wol),
                    rd=("wol",), wr=("wo", "wol"))
            sstores = []
            for tt in range(NTT):
                sj = tt % 2
                for jc in range(4):
                    wj = jc % 2
                    for hc in range(NHL):
                        p.emit("tensor", lambda e, tt=tt, jc=jc, hc=hc, wj=wj:
                               e.matmul(
                            psum_wo[wj][:],
                            zT_sb[:, (hc * NTT + tt) * 128:(hc * NTT + tt + 1) * 128],
                            wo_sb[:, hc * D + jc * 512: hc * D + (jc + 1) * 512],
                            start=(hc == 0), stop=(hc == NHL - 1)),
                            rd=("zT", "wo"), wr=(f"pwo{wj}",))
                    p.emit("vector", lambda e, jc=jc, wj=wj, sj=sj: e.tensor_copy(
                        out=sstage[sj][:, jc * 512:(jc + 1) * 512],
                        in_=psum_wo[wj][:]),
                        rd=(f"pwo{wj}",), wr=(f"sst{sj}", f"pwo{wj}"))
                sstores.append(p.emit("gd", lambda e, tt=tt, sj=sj:
                                      e.indirect_dma_start(
                    out=s_pool,
                    out_offset=IndirectOffsetOnAxis(ap=rows_sb[:, tt:tt + 1],
                                                    axis=0),
                    in_=sstage[sj][:], in_offset=None),
                    rd=(f"sst{sj}", "rows")))
        p.barrier()
        sbar = pair_barrier(after=tuple(sstores))
        p.res["s_ready"] = {"w": sbar, "r": {}}
        if upto == "p2c":
            return _finish(nc, p, dbg, root)

        with ExitStack() as ph:
            sb1 = lambda n_, s_, dt: ph.enter_context(nc.sbuf_tensor(n_, s_, dt))
            xnT0 = sb1("xnT0", [128, NDC * 512], BF16)
            xnT1 = sb1("xnT1", [128, NDC * 512], BF16)
            w1q_sb = [sb1(f"w1q_sb{j}", [128, D], BF16) for j in range(2)]
            w2q_sb = [sb1(f"w2q_sb{j}", [128, D], BF16) for j in range(2)]
            w3f_sb = [sb1(f"w3f_sb{j}", [128, D], BF16) for j in range(3)]
            hidT = [sb1(f"hidT{j}", [128, NHC * 1024], BF16) for j in range(2)]
            sig = [sb1("sig0", [128, 512], F32)] * 2
            fstage = [sb1(f"fstage{j}", [128, D], F32) for j in range(2)]
            P8 = [ph.enter_context(nc.psum_tensor(f"P8_{k}", [128, 512], F32))
                  for k in range(8)]
            px1 = [P8[0], P8[1]]
            px2 = [P8[2], P8[3]]
            fsb = sb1("fsb", [128, D], F32)
            ssb = sb1("ssb", [128, D], F32)
            gsb = sb1("gsb", [128, D], F32)
            osb = sb1("osb", [128, D], F32)

            def combine_chunk(pair):
                fbar = pair_barrier(after=tuple(fstores[pair]))
                p.res[f"f_ready{pair}"] = {"w": fbar, "r": {}}
                for ttl in range(8):
                    tt = pair * 8 + ttl
                    p.emit("st", lambda e, tt=tt: e.dma_start(
                        out=fsb[:], in_=f_pool[tt * 128:(tt + 1) * 128, :]),
                        rd=(f"f_ready{pair}",), wr=("fsb",))
                    p.emit("st", lambda e, tt=tt: e.dma_start(
                        out=gsb[:], in_=f_pool[T + tt * 128:T + (tt + 1) * 128, :]),
                        rd=(f"f_ready{pair}",), wr=("gsb",))
                    p.emit("vector", lambda e: e.tensor_add(
                        fsb[:], fsb[:], gsb[:]),
                        rd=("fsb", "gsb"), wr=("fsb",))
                    p.emit("scalar", lambda e: e.activation(
                        out=gsb[:], in_=fsb[:], func=AF.Silu),
                        rd=("fsb",), wr=("gsb", "fsb"))
                    p.emit("st", lambda e, tt=tt: e.dma_start(
                        out=ssb[:], in_=s_pool[tt * 128:(tt + 1) * 128, :]),
                        rd=("s_ready",), wr=("ssb",))
                    p.emit("st", lambda e, tt=tt: e.dma_start(
                        out=osb[:], in_=s_pool[T + tt * 128:T + (tt + 1) * 128, :]),
                        rd=("s_ready",), wr=("osb",))
                    p.emit("vector", lambda e: e.tensor_add(
                        ssb[:], ssb[:], osb[:]),
                        rd=("ssb", "osb"), wr=("ssb",))
                    p.emit("vector", lambda e: e.tensor_mul(
                        gsb[:], gsb[:], ssb[:]),
                        rd=("gsb", "ssb"), wr=("gsb",))
                    p.emit("st", lambda e, tt=tt: e.dma_start(
                        out=osb[:], in_=x[tt * 128:(tt + 1) * 128, :]),
                        rd=("ssb",), wr=("osb",))
                    p.emit("vector", lambda e: e.tensor_add(
                        osb[:], gsb[:], osb[:]),
                        rd=("gsb", "osb"), wr=("osb",))
                    p.emit("st", lambda e, tt=tt: e.dma_start(
                        out=out[tt * 128:(tt + 1) * 128, :], in_=osb[:]),
                        rd=("osb",))

            fstores = [[] for _ in range(4)]
            for pair in range(4):
                hp = pair % 2
                p.emit("ld", lambda e, pair=pair: e.dma_start(
                    out=xnT0[:], in_=xnT_s[2 * pair]), wr=("x0",))
                p.emit("ld", lambda e, pair=pair: e.dma_start(
                    out=xnT1[:], in_=xnT_s[2 * pair + 1]), wr=("x1",))
                for hc in range(NHC):
                    wj = hc % 2
                    p.emit("ld", lambda e, hc=hc, wj=wj: e.dma_start(
                        out=w1q_sb[wj][:], in_=w1q_s[hc]), wr=(f"w1{wj}",))
                    p.emit("ld", lambda e, hc=hc, wj=wj: e.dma_start(
                        out=w2q_sb[wj][:], in_=w2q_s[hc]), wr=(f"w2{wj}",))
                    for sub, (xnT_b, xk) in ((0, (xnT0, "x0")), (1, (xnT1, "x1"))):
                        sj = (2 * hc + sub) % 2
                        for wq_b, wkk, px, pxk in (
                                (w1q_sb[wj], f"w1{wj}", px1[sj], f"P{sj}"),
                                (w2q_sb[wj], f"w2{wj}", px2[sj], f"P{2 + sj}")):
                            for dc in range(NDC):
                                p.emit("tensor", lambda e, wq_b=wq_b, px=px,
                                       xnT_b=xnT_b, dc=dc: e.matmul(
                                    px[:],
                                    wq_b[:, dc * 128:(dc + 1) * 128],
                                    xnT_b[:, dc * 512:(dc + 1) * 512],
                                    start=(dc == 0), stop=(dc == NDC - 1)),
                                    rd=(wkk, xk), wr=(pxk,))
                        p.emit("scalar", lambda e, px=px1[sj], sg=sig[sj]:
                               e.activation(
                            out=sg[:], in_=px[:], func=AF.Sigmoid),
                            rd=(f"P{sj}",), wr=(f"sig{sj}", f"P{sj}"))
                        p.emit("vector", lambda e, hc=hc, sub=sub, px=px2[sj],
                               sg=sig[sj], hb=hidT[hp]: e.tensor_mul(
                            hb[:, hc * 1024 + sub * 512:
                               hc * 1024 + (sub + 1) * 512],
                            sg[:], px[:]),
                            rd=(f"sig{sj}", f"P{2 + sj}"),
                            wr=(f"hidT{hp}", f"P{2 + sj}"),
                            tag=f"hid_p{pair}_hc{hc}_s{sub}")
                if pair == 0 and "hid0" in debug:
                    hd = nc.dram_tensor("dbg_hid0", [128, NHC * 1024], BF16,
                                        kind="ExternalOutput").ap()
                    p.emit("st", lambda e, hd=hd: e.dma_start(
                        out=hd, in_=hidT[0][:]), rd=("hidT0",))
                for tpair in range(4):
                    for hc in range(NHC):
                        wj = hc % 3
                        p.emit("ld", lambda e, hc=hc, wj=wj: e.dma_start(
                            out=w3f_sb[wj][:], in_=w3q_s[hc]), wr=(f"w3{wj}",))
                        for jc in range(4):
                            for t2 in range(2):
                                ttl = tpair * 2 + t2
                                p.emit("tensor", lambda e, hc=hc, ttl=ttl,
                                       pfk=P8[jc * 2 + t2], wj=wj, jc=jc,
                                       hb=hidT[hp]: e.matmul(
                                    pfk[:],
                                    hb[:, hc * 1024 + ttl * 128:
                                       hc * 1024 + (ttl + 1) * 128],
                                    w3f_sb[wj][:, jc * 512:(jc + 1) * 512],
                                    start=(hc == 0), stop=(hc == NHC - 1)),
                                    rd=(f"hidT{hp}", f"w3{wj}"),
                                    wr=(f"P{jc * 2 + t2}",))
                    for t2 in range(2):
                        ttl = tpair * 2 + t2
                        tt = pair * 8 + ttl
                        fj = ttl % 2
                        for jc in range(4):
                            p.emit("scalar", lambda e, pfk=P8[jc * 2 + t2],
                                   fj=fj, jc=jc: e.copy(
                                out=fstage[fj][:, jc * 512:(jc + 1) * 512],
                                in_=pfk[:]),
                                rd=(f"P{jc * 2 + t2}",),
                                wr=(f"fst{fj}", f"P{jc * 2 + t2}"))
                        fstores[pair].append(p.emit(
                            "gd", lambda e, tt=tt, fj=fj:
                            e.indirect_dma_start(
                                out=f_pool,
                                out_offset=IndirectOffsetOnAxis(
                                    ap=rows_sb[:, tt:tt + 1], axis=0),
                                in_=fstage[fj][:], in_offset=None),
                            rd=(f"fst{fj}", "rows")))
                if not COMBINE_DEFER:
                    combine_chunk(pair)
            if COMBINE_DEFER:
                p.barrier()
                for pr in range(4):
                    combine_chunk(pr)
        if upto == "p3":
            return _finish(nc, p, dbg, root)
        return _finish(nc, p, dbg, root)


_NC_CACHE = {}


def _get_program():
    key = (UPTO, tuple(DEBUG_DUMPS))
    if key not in _NC_CACHE:
        _NC_CACHE[key] = build_program(UPTO, DEBUG_DUMPS)
    return _NC_CACHE[key]


def kernel(L0_attn_x, L0_attn_Wq, L0_attn_Wk, L0_attn_Wv, L0_attn_Wo,
           L0_ffn_w1, L0_ffn_w2, L0_ffn_w3):
    x = np.ascontiguousarray(np.asarray(L0_attn_x, dtype=np.float32))
    wq = np.asarray(L0_attn_Wq, dtype=np.float32)
    wk = np.asarray(L0_attn_Wk, dtype=np.float32)
    wv = np.asarray(L0_attn_Wv, dtype=np.float32)
    wo = np.asarray(L0_attn_Wo, dtype=np.float32)
    w1 = np.asarray(L0_ffn_w1, dtype=np.float32)
    w2 = np.asarray(L0_ffn_w2, dtype=np.float32)
    w3 = np.asarray(L0_ffn_w3, dtype=np.float32)

    wqT = np.ascontiguousarray(wq.T)
    wkT = np.ascontiguousarray(wk.T)
    wvT = np.ascontiguousarray(wv.T)
    woT = np.ascontiguousarray(wo.T)
    w1T = np.ascontiguousarray(w1.T)
    w2T = np.ascontiguousarray(w2.T)
    w3T = np.ascontiguousarray(w3.T)

    nc, _ = _get_program()
    in_maps = []
    for c in range(8):
        b, g = c // 2, c % 2
        in_maps.append({
            "x": np.ascontiguousarray(x[b]),
            "wqT": np.ascontiguousarray(wqT[:, g * 1024:(g + 1) * 1024]),
            "wkT": np.ascontiguousarray(wkT[:, g * 1024:(g + 1) * 1024]),
            "wvT": np.ascontiguousarray(wvT[:, g * 1024:(g + 1) * 1024]),
            "woT": np.ascontiguousarray(woT[g * 1024:(g + 1) * 1024, :]),
            "w1T": np.ascontiguousarray(w1T[:, g * HF:(g + 1) * HF]),
            "w2T": np.ascontiguousarray(w2T[:, g * HF:(g + 1) * HF]),
            "w3T": np.ascontiguousarray(w3T[g * HF:(g + 1) * HF, :]),
            "rows_idx": (g * T + np.arange(NTT, dtype=np.int32)[None, :] * 128
                         + np.arange(128, dtype=np.int32)[:, None]).astype(np.int32),
        })
    res = run_bass_kernel_spmd(nc, in_maps, core_ids=list(range(8)))
    out = np.empty((4, T, D), dtype=np.float32)
    for b in range(4):
        out[b] = res.results[2 * b]["out"]
    return out


# version 11 (final)
# speedup vs baseline: 3.2326x; 1.0188x over previous
import numpy as np
import ml_dtypes
from contextlib import ExitStack

import concourse.bass as bass
import concourse.mybir as mybir
from concourse.bass_utils import run_bass_kernel_spmd
from concourse.bass_core_rust import IndirectOffsetOnAxis

F32 = mybir.dt.float32
BF16 = mybir.dt.bfloat16
AF = mybir.ActivationFunctionType
ALU = mybir.AluOpType
AX = mybir.AxisListType

T = 4096
D = 2048
NHL = 8
HF = 2816
NHC = HF // 128
NTT = T // 128
NTC = T // 512
NDC = D // 128
SCALE = 1.0 / float(np.sqrt(128.0))
LN_EPS = 1e-5
INV_D = 1.0 / D
Q_S1 = 0.5 / (5632.0 * 2048.0)
Q_S2 = 0.5e-8
GROUPS = [[0, 1], [2, 3], [4, 5], [6, 7]]

UPTO = "full"
FIFO_SKIP = True
DMA_STRICT = False
COMBINE_DEFER = True
DEBUG_DUMPS = ()

QINFO = {
    "tensor": ("tensor", 1),
    "vector": ("vector", 1),
    "scalar": ("scalar", 1),
    "gpsimd": ("gpsimd", 1),
    "ld": ("sync", 16),
    "st": ("scalar", 16),
    "gd": ("gpsimd", 16),
}
QUEUES = tuple(QINFO)


class Prog:
    def __init__(self):
        self.items = []
        self.res = {}
        self.pending = {q: set() for q in QUEUES}
        self.last_in_q = {q: None for q in QUEUES}

    def emit(self, q, fn, rd=(), wr=(), tag="", after=()):
        idx = len(self.items)
        deps = set(self.pending[q])
        deps.update(after)
        self.pending[q] = set()
        for k in rd:
            e = self.res.get(k)
            if e is not None and e["w"] is not None:
                deps.add(e["w"])
        for k in wr:
            e = self.res.get(k)
            if e is not None:
                if e["w"] is not None:
                    deps.add(e["w"])
                deps.update(e["r"].values())
        for k in rd:
            self.res.setdefault(k, {"w": None, "r": {}})["r"][q] = idx
        for k in wr:
            e = self.res.setdefault(k, {"w": None, "r": {}})
            e["w"] = idx
            e["r"] = {}
        deps.discard(idx)
        self.items.append({"q": q, "fn": fn, "deps": deps, "tag": tag})
        self.last_in_q[q] = idx
        return idx

    def barrier(self):
        lasts = {i for i in self.last_in_q.values() if i is not None}
        for q in QUEUES:
            self.pending[q] |= lasts
        self.res = {}


def _finish(nc, p, dbg, root):
    bounce = None
    for name, (dbg_ap, src_ap) in dbg.items():
        if name in ("s_pool", "f_pool"):
            if bounce is None:
                bounce = root.enter_context(
                    nc.sbuf_tensor("dbg_bounce", [128, D], F32))
            for r0 in range(0, src_ap.shape[0], 128):
                p.emit("ld", lambda e, ap=src_ap, r0=r0: e.dma_start(
                    out=bounce[:], in_=ap[r0:r0 + 128, :]),
                    rd=(name,), wr=("dbg_b",))
                p.emit("st", lambda e, dbg_ap=dbg_ap, r0=r0: e.dma_start(
                    out=dbg_ap[r0:r0 + 128, :], in_=bounce[:]), rd=("dbg_b",),
                    wr=("dbg_b",))
            continue
        p.emit("st", lambda e, dbg_ap=dbg_ap, ap=src_ap: e.dma_start(
            out=dbg_ap, in_=ap), rd=(name,))
    if dbg:
        pass

    items = p.items
    n = len(items)
    hard = [None] * n
    for idx, it in enumerate(items):
        my_eng = QINFO[it["q"]][0]
        hd = []
        for d in it["deps"]:
            dq = items[d]["q"]
            d_eng, d_inc = QINFO[dq][0], QINFO[dq][1]
            if FIFO_SKIP and d_inc == 1 and d_eng == my_eng == "tensor" \
                    and it["q"] == "tensor":
                continue
            hd.append(d)
        hard[idx] = hd
    needed = set()
    for hd in hard:
        needed.update(d for d in hd if QINFO[items[d]["q"]][1] == 1)
    NLANE = 8
    val = [0] * n
    sem_of = [None] * n
    cnt = {}
    dseq = {q: 0 for q in QUEUES}
    for idx, it in enumerate(items):
        q = it["q"]
        if QINFO[q][1] == 16:
            lane = dseq[q] % NLANE
            dseq[q] += 1
            sid = f"{q}{lane}"
            cnt[sid] = cnt.get(sid, 0) + 16
            val[idx] = cnt[sid]
            sem_of[idx] = sid
            it["inc"] = 16
        elif idx in needed:
            cnt[q] = cnt.get(q, 0) + 1
            val[idx] = cnt[q]
            sem_of[idx] = q
            it["inc"] = 1
        else:
            it["inc"] = 0
        it["sem"] = sem_of[idx]
    satisfied = {q: {} for q in QUEUES}
    waits = [[] for _ in range(n)]
    for idx, it in enumerate(items):
        q = it["q"]
        for d in sorted(hard[idx]):
            sid = sem_of[d]
            v = val[d]
            assert v > 0 and sid is not None
            if satisfied[q].get(sid, 0) >= v:
                continue
            waits[idx].append((sid, v))
            satisfied[q][sid] = v
        it["waits"] = waits[idx]

    sems = {}
    for sid in cnt:
        sems[sid] = root.enter_context(nc.semaphore("sem_" + sid))
    blk = root.enter_context(nc.Block())

    final_counts = dict(cnt)

    def run_body(body_eng):
        def body(eng):
            for it in items:
                if QINFO[it["q"]][0] != body_eng:
                    continue
                for (dq, v) in it["waits"]:
                    eng.wait_ge(sems[dq], v)
                inst = it["fn"](eng)
                if it["inc"]:
                    inst.then_inc(sems[it["sem"]], it["inc"])
            if body_eng == "sync":
                for sid, v in final_counts.items():
                    eng.wait_ge(sems[sid], v)
        return body

    blk.sync(run_body("sync"))
    blk.tensor(run_body("tensor"))
    blk.vector(run_body("vector"))
    blk.scalar(run_body("scalar"))
    blk.gpsimd(run_body("gpsimd"))
    return nc, p


def build_program(upto=UPTO, debug=()):
    nc = bass.Bass("TRN2", target_bir_lowering=False)

    x = nc.dram_tensor("x", [T, D], F32, kind="ExternalInput").ap()
    wqT = nc.dram_tensor("wqT", [128, NDC * 1024], F32, kind="ExternalInput").ap()
    wkT = nc.dram_tensor("wkT", [128, NDC * 1024], F32, kind="ExternalInput").ap()
    wvT = nc.dram_tensor("wvT", [128, NDC * 1024], F32, kind="ExternalInput").ap()
    woT = nc.dram_tensor("woT", [128, NHL * D], F32, kind="ExternalInput").ap()
    w1T = nc.dram_tensor("w1T", [D, HF], F32, kind="ExternalInput").ap()
    w2T = nc.dram_tensor("w2T", [D, HF], F32, kind="ExternalInput").ap()
    w3T = nc.dram_tensor("w3T", [HF, D], F32, kind="ExternalInput").ap()
    out = nc.dram_tensor("out", [T, D], F32, kind="ExternalOutput").ap()

    xnT_s = nc.dram_tensor("xnT_s", [NTC, 128, NDC * 512], BF16, kind="Internal").ap()
    kT_s = nc.dram_tensor("kT_s", [NHL, 128, T], BF16, kind="Internal").ap()
    qT_s = nc.dram_tensor("qT_s", [NHL, 128, T], BF16, kind="Internal").ap()
    v_s = nc.dram_tensor("v_s", [NHL, NTT, 128, 128], BF16, kind="Internal").ap()
    w1q_s = nc.dram_tensor("w1q_s", [NHC, 128, D], BF16, kind="Internal").ap()
    w2q_s = nc.dram_tensor("w2q_s", [NHC, 128, D], BF16, kind="Internal").ap()
    w3q_s = nc.dram_tensor("w3q_s", [NHC, 128, D], BF16, kind="Internal").ap()
    rows_idx = nc.dram_tensor("rows_idx", [128, NTT], mybir.dt.int32,
                              kind="ExternalInput").ap()
    s_pool = nc.dram_tensor("s_pool", [2 * T, D], F32, kind="Internal",
                            addr_space="Shared").ap()
    f_pool = nc.dram_tensor("f_pool", [2 * T, D], F32, kind="Internal",
                            addr_space="Shared").ap()
    bar_in = nc.dram_tensor("bar_in", [1, 4], F32, kind="Internal").ap()
    bar_out = nc.dram_tensor("bar_out", [1, 4], F32, kind="Internal").ap()
    gam_part = nc.dram_tensor("gam_part", [1, 4], F32, kind="Internal").ap()
    gam_full = nc.dram_tensor("gam_full", [1, 4], F32, kind="Internal").ap()
    colb = nc.dram_tensor("colb", [128], F32, kind="Internal").ap()

    dbg = {}
    scratch_map = {"xnT_s": xnT_s, "kT_s": kT_s, "qT_s": qT_s, "v_s": v_s,
                   "w1q_s": w1q_s, "w2q_s": w2q_s, "w3q_s": w3q_s,
                   "s_pool": s_pool, "f_pool": f_pool, "gam_full": gam_full}
    for name in debug:
        if name not in scratch_map:
            continue
        src = scratch_map[name]
        dbg[name] = (nc.dram_tensor("dbg_" + name, list(src.shape),
                                    src.dtype, kind="ExternalOutput").ap(), src)

    ident_dram = nc.inline_tensor(np.eye(128, dtype=ml_dtypes.bfloat16), "ident_c")
    ones_dram = nc.inline_tensor(np.ones((1, 128), dtype=np.float32), "ones_c")
    _mk = np.zeros((4, 128, 512), dtype=np.float32)
    for _i in range(4):
        _p = np.arange(128)[:, None]
        _f = np.arange(512)[None, :]
        _mk[_i] = np.where(128 * _i + _p - _f >= 0, 1.0, 0.0)
    cmask_dram = nc.inline_tensor(_mk.astype(ml_dtypes.bfloat16), "cmask_c")

    p = Prog()

    with ExitStack() as root:
        sb = lambda n_, s_, dt: root.enter_context(nc.sbuf_tensor(n_, s_, dt))
        ident = sb("ident_sb", [128, 128], BF16)
        ones_f = sb("ones_f", [1, 128], F32)
        h_bc = sb("h_bc", [128, 4], F32)
        hn_bc = sb("hn_bc", [128, 4], F32)
        cmask = sb("cmask_sb", [128, 4 * 512], BF16)
        rows_sb = sb("rows_sb", [128, NTT], mybir.dt.int32)
        barv = sb("barv", [1, 4], F32)

        p.emit("ld", lambda e: e.dma_start(out=ident[:], in_=ident_dram.ap()),
               wr=("ident",))
        p.emit("ld", lambda e: e.dma_start(out=ones_f[:], in_=ones_dram.ap()),
               wr=("ones",))
        p.emit("ld", lambda e: e.dma_start(
            out=cmask[:].rearrange("p (c f) -> p c f", c=4),
            in_=cmask_dram.ap().rearrange("c p f -> p c f")), wr=("cmask",))
        p.emit("ld", lambda e: e.dma_start(out=rows_sb[:], in_=rows_idx),
               wr=("rows",))
        p.emit("vector", lambda e: e.memset(barv[:], 1.0), wr=("barv",))
        bar_ctr = [0]

        def pair_barrier(after=()):
            bar_ctr[0] += 1
            bid = p.emit("st", lambda e: e.dma_start(out=bar_in, in_=barv[:]),
                         rd=("barv",), after=after)
            return p.emit("gpsimd", lambda e: e.collective_compute(
                "AllReduce", ALU.add, replica_groups=GROUPS,
                ins=[bar_in.opt()], outs=[bar_out.opt()]), after=(bid,))

        with ExitStack() as ph:
            sb1 = lambda n_, s_, dt: ph.enter_context(nc.sbuf_tensor(n_, s_, dt))
            xt = [sb1(f"xt{j}", [128, D], F32) for j in range(2)]
            xsq = sb1("xsq", [128, D], F32)
            xn4 = [sb1(f"xn{i}", [128, D], BF16) for i in range(4)]
            xnTc = [sb1(f"xnTc{j}", [128, NDC * 512], BF16) for j in range(2)]
            lnv = [sb1(f"lnv{j}", [128, 8], F32) for j in range(2)]
            ptr = [ph.enter_context(nc.psum_tensor(f"ptr{j}", [128, 512], BF16))
                   for j in range(2)]

            for tc in range(NTC):
                cp = tc % 2
                for i in range(4):
                    ti = tc * 4 + i
                    j = ti % 2
                    xb, lv, xi = xt[j], lnv[j], xn4[i]
                    kx, kl, kxn = f"xt{j}", f"lnv{j}", f"xn{i}"
                    nm, ssq, t1v = lv[:, 0:1], lv[:, 1:2], lv[:, 2:3]
                    msq, veps, stdv, rstd = lv[:, 3:4], lv[:, 4:5], lv[:, 5:6], lv[:, 6:7]
                    p.emit("ld", lambda e, ti=ti, xb=xb: e.dma_start(
                        out=xb[:], in_=x[ti * 128:(ti + 1) * 128, :]), wr=(kx,))
                    p.emit("vector", lambda e, xb=xb, nm=nm: e.reduce_sum(
                        out=nm, in_=xb[:], axis=AX.X, negate=True),
                        rd=(kx,), wr=(kl,))
                    p.emit("vector", lambda e, nm=nm: e.tensor_scalar_mul(
                        nm, nm, INV_D), rd=(kl,), wr=(kl,))
                    p.emit("scalar", lambda e, xb=xb, ssq=ssq: e.activation(
                        out=xsq[:], in_=xb[:], func=AF.Square, accum_out=ssq),
                        rd=(kx,), wr=("xsq", kl + "s"))
                    p.emit("vector", lambda e, ssq=ssq, t1v=t1v: e.tensor_scalar(
                        out=t1v, in0=ssq, scalar1=INV_D, scalar2=LN_EPS,
                        op0=ALU.mult, op1=ALU.add), rd=(kl + "s",), wr=(kl,))
                    p.emit("vector", lambda e, nm=nm, msq=msq: e.tensor_single_scalar(
                        out=msq, in_=nm, scalar=nm, op=ALU.mult), rd=(kl,), wr=(kl,))
                    p.emit("vector", lambda e, t1v=t1v, msq=msq, veps=veps:
                           e.tensor_sub(veps, t1v, msq), rd=(kl,), wr=(kl,))
                    p.emit("scalar", lambda e, veps=veps, stdv=stdv: e.activation(
                        out=stdv, in_=veps, func=AF.Sqrt), rd=(kl,), wr=(kl + "s",))
                    p.emit("vector", lambda e, stdv=stdv, rstd=rstd: e.reciprocal(
                        rstd, stdv), rd=(kl + "s",), wr=(kl,))
                    p.emit("vector", lambda e, xb=xb, nm=nm, rstd=rstd, xi=xi:
                           e.tensor_scalar(
                        out=xi[:], in0=xb[:], scalar1=nm, scalar2=rstd,
                        op0=ALU.add, op1=ALU.mult), rd=(kx, kl), wr=(kxn,))
                for dc in range(NDC):
                    pj = dc % 2
                    for i in range(4):
                        p.emit("tensor", lambda e, i=i, dc=dc, pj=pj: e.transpose(
                            ptr[pj][:, i * 128:(i + 1) * 128],
                            xn4[i][:, dc * 128:(dc + 1) * 128], ident[:]),
                            rd=(f"xn{i}", "ident"), wr=(f"ptr{pj}",))
                    p.emit("scalar", lambda e, dc=dc, pj=pj, cp=cp: e.copy(
                        out=xnTc[cp][:, dc * 512:(dc + 1) * 512], in_=ptr[pj][:]),
                        rd=(f"ptr{pj}",), wr=(f"xnTc{cp}", f"ptr{pj}"))
                p.emit("st", lambda e, tc=tc, cp=cp: e.dma_start(
                    out=xnT_s[tc], in_=xnTc[cp][:]), rd=(f"xnTc{cp}",))
            gw = [sb1(f"gw{j}", [128, 2816], F32) for j in range(2)]
            gcols = sb1("gcols", [128, 40], F32)
            gcol = sb1("gcol", [128, 1], F32)
            grow = sb1("grow", [1, 128], F32)
            gpart = sb1("gpart", [1, 4], F32)
            p.emit("vector", lambda e: e.memset(gpart[:], 0.0), wr=("gpart",))
            li = 0
            for wi, wsrc in enumerate((w1T, w2T, w3T)):
                nblk = NDC if wi < 2 else NHC
                ncol = HF if wi < 2 else D
                for blk_i in range(nblk):
                    j = li % 2
                    li += 1
                    p.emit("ld", lambda e, wsrc=wsrc, blk_i=blk_i, ncol=ncol, j=j:
                           e.dma_start(
                        out=gw[j][:, 0:ncol],
                        in_=wsrc[blk_i * 128:(blk_i + 1) * 128, :]), wr=(f"gw{j}",))
                    p.emit("vector", lambda e, blk_i=blk_i, ncol=ncol, j=j:
                           e.reduce_sum(
                        out=gcols[:, blk_i:blk_i + 1],
                        in_=gw[j][:, 0:ncol],
                        axis=AX.X, apply_absolute_value=True),
                        rd=(f"gw{j}",), wr=(f"gcols{blk_i}", f"gw{j}"))
                p.emit("vector", lambda e, nblk=nblk: e.reduce_sum(
                    out=gcol[:], in_=gcols[:, 0:nblk], axis=AX.X),
                    rd=tuple(f"gcols{b2}" for b2 in range(nblk)), wr=("gcol",))
                p.emit("st", lambda e: e.dma_start(out=colb, in_=gcol[:]),
                       rd=("gcol",), wr=("colb",))
                p.emit("ld", lambda e: e.dma_start(
                    out=grow[:], in_=colb.rearrange("(o p) -> o p", o=1)),
                    rd=("colb",), wr=("grow",))
                p.emit("vector", lambda e, wi=wi: e.reduce_sum(
                    out=gpart[:, wi:wi + 1], in_=grow[:], axis=AX.X),
                    rd=("grow",), wr=("gpart", "grow", "gcol"))
            p.emit("st", lambda e: e.dma_start(out=gam_part, in_=gpart[:]),
                   rd=("gpart",), wr=("gam_part",))
            p.emit("gpsimd", lambda e: e.collective_compute(
                "AllReduce", ALU.add, replica_groups=GROUPS,
                ins=[gam_part.opt()], outs=[gam_full.opt()]),
                rd=("gam_part",), wr=("gam_full",))
        p.barrier()
        if upto == "p1a":
            return _finish(nc, p, dbg, root)

        with ExitStack() as ph:
            sb1 = lambda n_, s_, dt: ph.enter_context(nc.sbuf_tensor(n_, s_, dt))
            wq_sb = sb1("wq_sb", [128, NDC * 1024], BF16)
            wk_sb = sb1("wk_sb", [128, NDC * 1024], BF16)
            wv_sb = sb1("wv_sb", [128, NDC * 1024], BF16)
            wload = [sb1(f"wload{j}", [128, 2816], F32) for j in range(2)]
            li = 0
            for wname, wdst, wsrc in (("q", wq_sb, wqT), ("k", wk_sb, wkT),
                                      ("v", wv_sb, wvT)):
                for dc in range(NDC):
                    j = li % 2
                    li += 1
                    p.emit("ld", lambda e, wsrc=wsrc, dc=dc, j=j: e.dma_start(
                        out=wload[j][:, 0:1024],
                        in_=wsrc[:, dc * 1024:(dc + 1) * 1024]), wr=(f"wload{j}",))
                    p.emit("scalar", lambda e, wdst=wdst, dc=dc, j=j: e.copy(
                        out=wdst[:, dc * 1024:(dc + 1) * 1024],
                        in_=wload[j][:, 0:1024]),
                        rd=(f"wload{j}",), wr=("w" + wname, f"wload{j}"))

            grow4 = sb1("grow4", [1, 4], F32)
            hrow = sb1("hrow", [1, 4], F32)
            hnrow = sb1("hnrow", [1, 4], F32)
            p.emit("ld", lambda e: e.dma_start(out=grow4[:], in_=gam_full),
                   rd=("gam_full",), wr=("grow4",))
            p.emit("vector", lambda e: e.tensor_scalar(
                out=hrow[:], in0=grow4[:], scalar1=Q_S1, scalar2=Q_S2,
                op0=ALU.mult, op1=ALU.add), rd=("grow4",), wr=("hrow",))
            p.emit("vector", lambda e: e.tensor_scalar_mul(hnrow[:], hrow[:], -1.0),
                   rd=("hrow",), wr=("hnrow",))
            pbc = ph.enter_context(nc.psum_tensor("pbc", [128, 4], F32))
            p.emit("tensor", lambda e: e.matmul(
                pbc[:], ones_f[:], hrow[:], start=True, stop=True),
                rd=("ones", "hrow"), wr=("pbc",))
            p.emit("vector", lambda e: e.tensor_copy(out=h_bc[:], in_=pbc[:]),
                   rd=("pbc",), wr=("h_bc", "pbc"))
            p.emit("tensor", lambda e: e.matmul(
                pbc[:], ones_f[:], hnrow[:], start=True, stop=True),
                rd=("ones", "hnrow"), wr=("pbc",))
            p.emit("vector", lambda e: e.tensor_copy(out=hn_bc[:], in_=pbc[:]),
                   rd=("pbc",), wr=("hn_bc", "pbc"))

            wfp = [sb1(f"wfp{j}", [128, D], F32) for j in range(2)]
            tneg = [sb1(f"tneg{j}", [128, D], BF16) for j in range(2)]
            wqt = [sb1(f"wqt{j}", [128, D], BF16) for j in range(2)]
            quant_work = []
            qi = 0
            for wi, (wsrc, wdst_s) in enumerate(((w1T, w1q_s), (w2T, w2q_s),
                                                 (w3T, w3q_s))):
                for hc in range(NHC):
                    j = qi % 2
                    qi += 1

                    def q_block(wi=wi, wsrc=wsrc, wdst_s=wdst_s, hc=hc, j=j):
                        if wi < 2:
                            src = wsrc[:, hc * 128:(hc + 1) * 128].rearrange(
                                "(c p) f -> p c f", p=128)
                            dst = wfp[j][:].rearrange("p (c f) -> p c f", c=NDC)
                            p.emit("ld", lambda e, src=src, dst=dst: e.dma_start(
                                out=dst, in_=src), wr=(f"wfp{j}",))
                        else:
                            p.emit("ld", lambda e: e.dma_start(
                                out=wfp[j][:], in_=wsrc[hc * 128:(hc + 1) * 128, :]),
                                wr=(f"wfp{j}",))
                        p.emit("vector", lambda e: e.tensor_single_scalar(
                            out=tneg[j][:], in_=wfp[j][:], scalar=hn_bc[:, wi:wi + 1],
                            op=ALU.is_lt), rd=(f"wfp{j}", "hn_bc"), wr=(f"tneg{j}",))
                        p.emit("vector", lambda e: e.scalar_tensor_tensor(
                            out=wqt[j][:], in0=wfp[j][:], scalar=h_bc[:, wi:wi + 1],
                            in1=tneg[j][:], op0=ALU.is_gt, op1=ALU.subtract),
                            rd=(f"wfp{j}", f"tneg{j}", "h_bc"),
                            wr=(f"wqt{j}", f"wfp{j}"))
                        p.emit("st", lambda e: e.dma_start(
                            out=wdst_s[hc], in_=wqt[j][:]), rd=(f"wqt{j}",))
                    quant_work.append(q_block)
            if upto == "quant":
                for qb in quant_work:
                    qb()
                p.barrier()
                return _finish(nc, p, dbg, root)

            xnT_sb = [sb1(f"xnT_sb{j}", [128, NDC * 512], BF16) for j in range(2)]
            pstage = [sb1(f"pstage{j}", [128, 512], BF16) for j in range(2)]
            pproj = [ph.enter_context(nc.psum_tensor(f"pproj{j}", [128, 512], F32))
                     for j in range(2)]
            ei = 0
            nq = (len(quant_work) + NTC - 1) // NTC
            for tc in range(NTC):
                cj = tc % 2
                p.emit("ld", lambda e, tc=tc, cj=cj: e.dma_start(
                    out=xnT_sb[cj][:], in_=xnT_s[tc]), rd=("xnTw",), wr=(f"xnT{cj}",))
                for qb in quant_work[tc * nq:(tc + 1) * nq]:
                    qb()
                for wkey, wsb, dst_s in (("wk", wk_sb, kT_s), ("wq", wq_sb, qT_s)):
                    for oc in range(NHL):
                        j = ei % 2
                        ei += 1
                        for dc in range(NDC):
                            p.emit("tensor", lambda e, wsb=wsb, oc=oc, dc=dc, j=j,
                                   cj=cj: e.matmul(
                                pproj[j][:],
                                wsb[:, dc * 1024 + oc * 128: dc * 1024 + (oc + 1) * 128],
                                xnT_sb[cj][:, dc * 512:(dc + 1) * 512],
                                start=(dc == 0), stop=(dc == NDC - 1)),
                                rd=(wkey, f"xnT{cj}"), wr=(f"pproj{j}",))
                        p.emit("vector", lambda e, j=j: e.tensor_copy(
                            out=pstage[j][:], in_=pproj[j][:]),
                            rd=(f"pproj{j}",), wr=(f"pstage{j}", f"pproj{j}"))
                        p.emit("st", lambda e, dst_s=dst_s, oc=oc, tc=tc, j=j:
                               e.dma_start(
                            out=dst_s[oc, :, tc * 512:(tc + 1) * 512],
                            in_=pstage[j][:]), rd=(f"pstage{j}",))
                for sv in range(4):
                    st_g = tc * 4 + sv
                    for dvc in range(2):
                        j = ei % 2
                        ei += 1
                        for dc in range(NDC):
                            p.emit("tensor", lambda e, sv=sv, dvc=dvc, dc=dc, j=j,
                                   cj=cj: e.matmul(
                                pproj[j][:],
                                xnT_sb[cj][:, dc * 512 + sv * 128: dc * 512 + (sv + 1) * 128],
                                wv_sb[:, dc * 1024 + dvc * 512: dc * 1024 + (dvc + 1) * 512],
                                start=(dc == 0), stop=(dc == NDC - 1)),
                                rd=("wv", f"xnT{cj}"), wr=(f"pproj{j}",))
                        p.emit("vector", lambda e, j=j: e.tensor_copy(
                            out=pstage[j][:], in_=pproj[j][:]),
                            rd=(f"pproj{j}",), wr=(f"pstage{j}", f"pproj{j}"))
                        for u in range(4):
                            p.emit("st", lambda e, dvc=dvc, u=u, st_g=st_g, j=j:
                                   e.dma_start(
                                out=v_s[4 * dvc + u, st_g],
                                in_=pstage[j][:, u * 128:(u + 1) * 128]),
                                rd=(f"pstage{j}",))
        p.barrier()
        if upto == "proj":
            return _finish(nc, p, dbg, root)

        with ExitStack() as ph:
            sb1 = lambda n_, s_, dt: ph.enter_context(nc.sbuf_tensor(n_, s_, dt))
            zT_sb = sb1("zT_sb", [128, NHL * NTT * 128], BF16)
            pa = ph.enter_context(ExitStack())
            sb2 = lambda n_, s_, dt: pa.enter_context(nc.sbuf_tensor(n_, s_, dt))
            v_hb = [sb2(f"v_hb{j}", [128, NTT * 128], BF16) for j in range(2)]
            kT_sb = [sb2(f"kT_sb{j}", [128, T], BF16) for j in range(2)]
            qT_sb = [sb2(f"qT_sb{j}", [128, T], BF16) for j in range(2)]
            pbuf = [sb2(f"pbuf{j}", [128, T], BF16) for j in range(4)]
            ptbuf = sb2("ptbuf", [128, NTT * 512], BF16)
            smv = [sb2(f"smv{j}", [128, 4], F32) for j in range(4)]
            dtmp = [sb2(f"dtmp{j}", [128, 512], BF16) for j in range(4)]
            lsums = [sb2(f"lsums{j}", [128, 8], F32) for j in range(4)]
            psum_s = [pa.enter_context(nc.psum_tensor(f"psum_s{j}", [128, 512], F32))
                      for j in range(4)]
            psum_pt = [pa.enter_context(nc.psum_tensor(f"psum_pt{j}", [128, 512], BF16))
                       for j in range(2)]
            psum_z = [pa.enter_context(nc.psum_tensor(f"psum_z{j}", [128, 512], F32))
                      for j in range(2)]
            s_ctr = [0]
            pt_ctr = [0]

            def tile_S(ti, tj, hj):
                st = ti // 4
                i = ti % 4
                for kc in range(st + 1):
                    sj = s_ctr[0] % 4
                    s_ctr[0] += 1
                    p.emit("tensor", lambda e, ti=ti, kc=kc, sj=sj, hj=hj:
                           e.matmul(
                        psum_s[sj][:],
                        qT_sb[hj][:, ti * 128:(ti + 1) * 128],
                        kT_sb[hj][:, kc * 512:(kc + 1) * 512],
                        start=True, stop=True),
                        rd=(f"q{hj}", f"k{hj}"), wr=(f"ps{sj}",))
                    if kc == st:
                        p.emit("scalar", lambda e, tj=tj, sj=sj: e.activation(
                            out=dtmp[tj][:], in_=psum_s[sj][:],
                            func=AF.Exp, scale=SCALE),
                            rd=(f"ps{sj}",), wr=(f"dt{tj}", f"ps{sj}"))
                    else:
                        p.emit("scalar", lambda e, kc=kc, tj=tj, sj=sj:
                               e.activation(
                            out=pbuf[tj][:, kc * 512:(kc + 1) * 512],
                            in_=psum_s[sj][:], func=AF.Exp, scale=SCALE,
                            accum_out=lsums[tj][:, kc:kc + 1]),
                            rd=(f"ps{sj}",),
                            wr=(f"pb{tj}", f"ls{tj}", f"ps{sj}"))

            def tile_soft(ti, tj):
                st = ti // 4
                i = ti % 4
                lsum = smv[tj][:, 1:2]
                rl = smv[tj][:, 2:3]
                p.emit("vector", lambda e, i=i, tj=tj, st=st:
                       e.scalar_tensor_tensor(
                    out=pbuf[tj][:, st * 512:(st + 1) * 512],
                    in0=dtmp[tj][:], scalar=1.0,
                    in1=cmask[:, i * 512:(i + 1) * 512],
                    op0=ALU.mult, op1=ALU.mult,
                    accum_out=lsums[tj][:, st:st + 1]),
                    rd=(f"dt{tj}", "cmask"),
                    wr=(f"pb{tj}", f"ls{tj}", f"dt{tj}"))
                p.emit("vector", lambda e, st=st, tj=tj, lsum=lsum:
                       e.reduce_sum(
                    out=lsum, in_=lsums[tj][:, 0:st + 1], axis=AX.X),
                    rd=(f"ls{tj}",), wr=(f"smv{tj}s",))
                p.emit("vector", lambda e, rl=rl, lsum=lsum: e.reciprocal(
                    rl, lsum), rd=(f"smv{tj}s",), wr=(f"smv{tj}",))
                for scq in range(st + 1):
                    p.emit("vector", lambda e, scq=scq, tj=tj, rl=rl:
                           e.tensor_single_scalar(
                        out=pbuf[tj][:, scq * 512:(scq + 1) * 512],
                        in_=pbuf[tj][:, scq * 512:(scq + 1) * 512],
                        scalar=rl, op=ALU.mult),
                        rd=(f"pb{tj}", f"smv{tj}"), wr=(f"pbn{tj}_{scq}",))

            def tile_pt(ti, tj):
                st = ti // 4
                i = ti % 4
                nsc_i = ti + 1
                nsc = 4 * (st + 1)
                for scq in range(st + 1):
                    cnt = min(4, nsc_i - 4 * scq)
                    pj = pt_ctr[0] % 2
                    pt_ctr[0] += 1
                    for u in range(cnt):
                        sc = scq * 4 + u
                        p.emit("tensor", lambda e, sc=sc, u=u, tj=tj, pj=pj:
                               e.transpose(
                            psum_pt[pj][:, u * 128:(u + 1) * 128],
                            pbuf[tj][:, sc * 128:(sc + 1) * 128], ident[:]),
                            rd=(f"pbn{tj}_{scq}", f"pb{tj}", "ident"),
                            wr=(f"ppt{pj}",))
                    dst = ptbuf[:].rearrange("p (c f) -> p c f", f=512)[
                        :, scq * 4: scq * 4 + cnt, i * 128:(i + 1) * 128]
                    src = psum_pt[pj][:, 0:cnt * 128].rearrange(
                        "p (c f) -> p c f", f=128)
                    ev_q = "vector" if pj == 0 else "scalar"
                    ev_fn = ((lambda e, dst=dst, src=src:
                              e.tensor_copy(out=dst, in_=src))
                             if pj == 0 else
                             (lambda e, dst=dst, src=src:
                              e.copy(out=dst, in_=src)))
                    p.emit(ev_q, ev_fn,
                           rd=(f"ppt{pj}",),
                           wr=(f"ppt{pj}",) + tuple(
                               f"ptc{scq * 4 + u}_{i}" for u in range(cnt)))
                for sc in range(nsc_i, nsc):
                    p.emit("vector", lambda e, sc=sc, i=i: e.memset(
                        ptbuf[:, sc * 512 + i * 128: sc * 512 + (i + 1) * 128],
                        0.0), wr=(f"ptc{sc}_{i}",))

            def do_z(st, vj, h):
                nsc = 4 * (st + 1)
                zj = st % 2
                for sc in range(nsc):
                    p.emit("tensor", lambda e, sc=sc, vj=vj, zj=zj: e.matmul(
                        psum_z[zj][:],
                        v_hb[vj][:, sc * 128:(sc + 1) * 128],
                        ptbuf[:, sc * 512:(sc + 1) * 512],
                        start=(sc == 0), stop=(sc == nsc - 1)),
                        rd=(f"v{vj}",) + tuple(
                            f"ptc{sc}_{i2}" for i2 in range(4)),
                        wr=(f"pz{zj}",))
                p.emit("vector", lambda e, h=h, st=st, zj=zj: e.tensor_copy(
                    out=zT_sb[:, (h * NTT + st * 4) * 128:
                              (h * NTT + st * 4 + 4) * 128],
                    in_=psum_z[zj][:]),
                    rd=(f"pz{zj}",), wr=("zT", f"pz{zj}"))

            for h in range(NHL):
                hj = h % 2
                vj = hj
                p.emit("ld", lambda e, h=h, hj=hj: e.dma_start(
                    out=v_hb[hj][:].rearrange("p (c f) -> p c f", c=NTT),
                    in_=v_s[h].rearrange("c p f -> p c f")), wr=(f"v{hj}",))
                p.emit("ld", lambda e, h=h, hj=hj: e.dma_start(
                    out=kT_sb[hj][:], in_=kT_s[h]), wr=(f"k{hj}",))
                p.emit("ld", lambda e, h=h, hj=hj: e.dma_start(
                    out=qT_sb[hj][:], in_=qT_s[h]), wr=(f"q{hj}",))
                pend = []

                def pop_one():
                    pti, ptj = pend.pop(0)
                    tile_soft(pti, ptj)
                    tile_pt(pti, ptj)
                    if pti % 4 == 3:
                        do_z(pti // 4, vj, h)

                for st in range(NTC):
                    for i in range(4):
                        ti = st * 4 + i
                        tile_S(ti, ti % 4, hj)
                        pend.append((ti, ti % 4))
                        if len(pend) > 2:
                            pop_one()
                while pend:
                    pop_one()
            if upto == "p2":
                p.barrier()
                return _finish(nc, p, dbg, root)

            pa.close()
            p.barrier()
            wo_sb = sb1("wo_sb", [128, NHL * D], BF16)
            wol_t = sb1("wol_t", [128, D], F32)
            sst_t = sb1("sst_t", [128, 2 * D], F32)
            wol = wol_t[:]
            sstage = [sst_t[:, 0:D], sst_t[:, D:2 * D]]
            psum_wo = [ph.enter_context(
                nc.psum_tensor(f"psum_wo{j}", [128, 512], F32)) for j in range(2)]
            for hc in range(NHL):
                p.emit("ld", lambda e, hc=hc: e.dma_start(
                    out=wol, in_=woT[:, hc * D:(hc + 1) * D]), wr=("wol",))
                p.emit("scalar", lambda e, hc=hc: e.copy(
                    out=wo_sb[:, hc * D:(hc + 1) * D], in_=wol),
                    rd=("wol",), wr=("wo", "wol"))
            sstores = []
            for tt in range(NTT):
                sj = tt % 2
                for jc in range(4):
                    wj = jc % 2
                    for hc in range(NHL):
                        p.emit("tensor", lambda e, tt=tt, jc=jc, hc=hc, wj=wj:
                               e.matmul(
                            psum_wo[wj][:],
                            zT_sb[:, (hc * NTT + tt) * 128:(hc * NTT + tt + 1) * 128],
                            wo_sb[:, hc * D + jc * 512: hc * D + (jc + 1) * 512],
                            start=(hc == 0), stop=(hc == NHL - 1)),
                            rd=("zT", "wo"), wr=(f"pwo{wj}",))
                    p.emit("vector", lambda e, jc=jc, wj=wj, sj=sj: e.tensor_copy(
                        out=sstage[sj][:, jc * 512:(jc + 1) * 512],
                        in_=psum_wo[wj][:]),
                        rd=(f"pwo{wj}",), wr=(f"sst{sj}", f"pwo{wj}"))
                sstores.append(p.emit("gd", lambda e, tt=tt, sj=sj:
                                      e.indirect_dma_start(
                    out=s_pool,
                    out_offset=IndirectOffsetOnAxis(ap=rows_sb[:, tt:tt + 1],
                                                    axis=0),
                    in_=sstage[sj][:], in_offset=None),
                    rd=(f"sst{sj}", "rows")))
        p.barrier()
        sbar = pair_barrier(after=tuple(sstores))
        p.res["s_ready"] = {"w": sbar, "r": {}}
        if upto == "p2c":
            return _finish(nc, p, dbg, root)

        with ExitStack() as ph:
            sb1 = lambda n_, s_, dt: ph.enter_context(nc.sbuf_tensor(n_, s_, dt))
            xnT0 = sb1("xnT0", [128, NDC * 512], BF16)
            xnT1 = sb1("xnT1", [128, NDC * 512], BF16)
            w1q_sb = [sb1(f"w1q_sb{j}", [128, D], BF16) for j in range(2)]
            w2q_sb = [sb1(f"w2q_sb{j}", [128, D], BF16) for j in range(2)]
            w3f_sb = [sb1(f"w3f_sb{j}", [128, D], BF16) for j in range(3)]
            _h1 = sb1("hidT0", [128, NHC * 1024], BF16)
            hidT = [_h1, _h1]
            sig = [sb1("sig0", [128, 512], F32)] * 2
            fstage = [sb1(f"fstage{j}", [128, D], F32) for j in range(2)]
            P8 = [ph.enter_context(nc.psum_tensor(f"P8_{k}", [128, 512], F32))
                  for k in range(8)]
            px1 = [P8[0], P8[1]]
            px2 = [P8[2], P8[3]]
            fsb = [sb1(f"fsb{j}", [128, D], F32) for j in range(2)]
            ssb = [sb1(f"ssb{j}", [128, D], F32) for j in range(2)]
            gsb = [sb1(f"gsb{j}", [128, D], F32) for j in range(2)]
            osb = [sb1(f"osb{j}", [128, D], F32) for j in range(2)]

            def combine_chunk(pair):
                fbar = pair_barrier(after=tuple(fstores[pair]))
                p.res[f"f_ready{pair}"] = {"w": fbar, "r": {}}
                for ttl in range(8):
                    tt = pair * 8 + ttl
                    j = tt % 2
                    p.emit("st", lambda e, tt=tt, j=j: e.dma_start(
                        out=fsb[j][:], in_=f_pool[tt * 128:(tt + 1) * 128, :]),
                        rd=(f"f_ready{pair}",), wr=(f"fsb{j}",))
                    p.emit("st", lambda e, tt=tt, j=j: e.dma_start(
                        out=gsb[j][:],
                        in_=f_pool[T + tt * 128:T + (tt + 1) * 128, :]),
                        rd=(f"f_ready{pair}",), wr=(f"gsb{j}",))
                    p.emit("vector", lambda e, j=j: e.tensor_add(
                        fsb[j][:], fsb[j][:], gsb[j][:]),
                        rd=(f"fsb{j}", f"gsb{j}"), wr=(f"fsb{j}",))
                    p.emit("scalar", lambda e, j=j: e.activation(
                        out=gsb[j][:], in_=fsb[j][:], func=AF.Silu),
                        rd=(f"fsb{j}",), wr=(f"gsb{j}", f"fsb{j}"))
                    p.emit("st", lambda e, tt=tt, j=j: e.dma_start(
                        out=ssb[j][:], in_=s_pool[tt * 128:(tt + 1) * 128, :]),
                        rd=("s_ready",), wr=(f"ssb{j}",))
                    p.emit("st", lambda e, tt=tt, j=j: e.dma_start(
                        out=osb[j][:],
                        in_=s_pool[T + tt * 128:T + (tt + 1) * 128, :]),
                        rd=("s_ready",), wr=(f"osb{j}",))
                    p.emit("vector", lambda e, j=j: e.tensor_add(
                        ssb[j][:], ssb[j][:], osb[j][:]),
                        rd=(f"ssb{j}", f"osb{j}"), wr=(f"ssb{j}",))
                    p.emit("vector", lambda e, j=j: e.tensor_mul(
                        gsb[j][:], gsb[j][:], ssb[j][:]),
                        rd=(f"gsb{j}", f"ssb{j}"), wr=(f"gsb{j}",))
                    p.emit("st", lambda e, tt=tt, j=j: e.dma_start(
                        out=osb[j][:], in_=x[tt * 128:(tt + 1) * 128, :]),
                        rd=(f"ssb{j}",), wr=(f"osb{j}",))
                    p.emit("vector", lambda e, j=j: e.tensor_add(
                        osb[j][:], gsb[j][:], osb[j][:]),
                        rd=(f"gsb{j}", f"osb{j}"), wr=(f"osb{j}",))
                    p.emit("st", lambda e, tt=tt, j=j: e.dma_start(
                        out=out[tt * 128:(tt + 1) * 128, :], in_=osb[j][:]),
                        rd=(f"osb{j}",))

            fstores = [[] for _ in range(4)]
            for pair in range(4):
                hp = pair % 2
                p.emit("ld", lambda e, pair=pair: e.dma_start(
                    out=xnT0[:], in_=xnT_s[2 * pair]), wr=("x0",))
                p.emit("ld", lambda e, pair=pair: e.dma_start(
                    out=xnT1[:], in_=xnT_s[2 * pair + 1]), wr=("x1",))
                for hc in range(NHC):
                    wj = hc % 2
                    p.emit("ld", lambda e, hc=hc, wj=wj: e.dma_start(
                        out=w1q_sb[wj][:], in_=w1q_s[hc]), wr=(f"w1{wj}",))
                    p.emit("ld", lambda e, hc=hc, wj=wj: e.dma_start(
                        out=w2q_sb[wj][:], in_=w2q_s[hc]), wr=(f"w2{wj}",))
                    for sub, (xnT_b, xk) in ((0, (xnT0, "x0")), (1, (xnT1, "x1"))):
                        sj = (2 * hc + sub) % 2
                        for wq_b, wkk, px, pxk in (
                                (w1q_sb[wj], f"w1{wj}", px1[sj], f"P{sj}"),
                                (w2q_sb[wj], f"w2{wj}", px2[sj], f"P{2 + sj}")):
                            for dc in range(NDC):
                                p.emit("tensor", lambda e, wq_b=wq_b, px=px,
                                       xnT_b=xnT_b, dc=dc: e.matmul(
                                    px[:],
                                    wq_b[:, dc * 128:(dc + 1) * 128],
                                    xnT_b[:, dc * 512:(dc + 1) * 512],
                                    start=(dc == 0), stop=(dc == NDC - 1)),
                                    rd=(wkk, xk), wr=(pxk,))
                        p.emit("scalar", lambda e, px=px1[sj], sg=sig[sj]:
                               e.activation(
                            out=sg[:], in_=px[:], func=AF.Sigmoid),
                            rd=(f"P{sj}",), wr=(f"sig{sj}", f"P{sj}"))
                        p.emit("vector", lambda e, hc=hc, sub=sub, px=px2[sj],
                               sg=sig[sj], hb=hidT[hp]: e.tensor_mul(
                            hb[:, hc * 1024 + sub * 512:
                               hc * 1024 + (sub + 1) * 512],
                            sg[:], px[:]),
                            rd=(f"sig{sj}", f"P{2 + sj}"),
                            wr=("hidT0", f"P{2 + sj}"),
                            tag=f"hid_p{pair}_hc{hc}_s{sub}")
                if pair == 0 and "hid0" in debug:
                    hd = nc.dram_tensor("dbg_hid0", [128, NHC * 1024], BF16,
                                        kind="ExternalOutput").ap()
                    p.emit("st", lambda e, hd=hd: e.dma_start(
                        out=hd, in_=hidT[0][:]), rd=("hidT0",))
                for tpair in range(4):
                    for hc in range(NHC):
                        wj = hc % 3
                        p.emit("ld", lambda e, hc=hc, wj=wj: e.dma_start(
                            out=w3f_sb[wj][:], in_=w3q_s[hc]), wr=(f"w3{wj}",))
                        for jc in range(4):
                            for t2 in range(2):
                                ttl = tpair * 2 + t2
                                p.emit("tensor", lambda e, hc=hc, ttl=ttl,
                                       pfk=P8[jc * 2 + t2], wj=wj, jc=jc,
                                       hb=hidT[hp]: e.matmul(
                                    pfk[:],
                                    hb[:, hc * 1024 + ttl * 128:
                                       hc * 1024 + (ttl + 1) * 128],
                                    w3f_sb[wj][:, jc * 512:(jc + 1) * 512],
                                    start=(hc == 0), stop=(hc == NHC - 1)),
                                    rd=("hidT0", f"w3{wj}"),
                                    wr=(f"P{jc * 2 + t2}",))
                    for t2 in range(2):
                        ttl = tpair * 2 + t2
                        tt = pair * 8 + ttl
                        fj = ttl % 2
                        for jc in range(4):
                            p.emit("scalar", lambda e, pfk=P8[jc * 2 + t2],
                                   fj=fj, jc=jc: e.copy(
                                out=fstage[fj][:, jc * 512:(jc + 1) * 512],
                                in_=pfk[:]),
                                rd=(f"P{jc * 2 + t2}",),
                                wr=(f"fst{fj}", f"P{jc * 2 + t2}"))
                        fstores[pair].append(p.emit(
                            "gd", lambda e, tt=tt, fj=fj:
                            e.indirect_dma_start(
                                out=f_pool,
                                out_offset=IndirectOffsetOnAxis(
                                    ap=rows_sb[:, tt:tt + 1], axis=0),
                                in_=fstage[fj][:], in_offset=None),
                            rd=(f"fst{fj}", "rows")))
                if not COMBINE_DEFER:
                    combine_chunk(pair)
            if COMBINE_DEFER:
                p.barrier()
                for pr in range(4):
                    combine_chunk(pr)
        if upto == "p3":
            return _finish(nc, p, dbg, root)
        return _finish(nc, p, dbg, root)


_NC_CACHE = {}


def _get_program():
    key = (UPTO, tuple(DEBUG_DUMPS))
    if key not in _NC_CACHE:
        _NC_CACHE[key] = build_program(UPTO, DEBUG_DUMPS)
    return _NC_CACHE[key]


def kernel(L0_attn_x, L0_attn_Wq, L0_attn_Wk, L0_attn_Wv, L0_attn_Wo,
           L0_ffn_w1, L0_ffn_w2, L0_ffn_w3):
    x = np.ascontiguousarray(np.asarray(L0_attn_x, dtype=np.float32))
    wq = np.asarray(L0_attn_Wq, dtype=np.float32)
    wk = np.asarray(L0_attn_Wk, dtype=np.float32)
    wv = np.asarray(L0_attn_Wv, dtype=np.float32)
    wo = np.asarray(L0_attn_Wo, dtype=np.float32)
    w1 = np.asarray(L0_ffn_w1, dtype=np.float32)
    w2 = np.asarray(L0_ffn_w2, dtype=np.float32)
    w3 = np.asarray(L0_ffn_w3, dtype=np.float32)

    def blk_proj(wt, g):
        m = wt[:, g * 1024:(g + 1) * 1024]
        return np.ascontiguousarray(
            m.reshape(16, 128, 1024).transpose(1, 0, 2).reshape(128, 16384))

    def blk_wo(wt, g):
        m = wt[g * 1024:(g + 1) * 1024, :]
        return np.ascontiguousarray(
            m.reshape(8, 128, 2048).transpose(1, 0, 2).reshape(128, 16384))

    wqT = wq.T
    wkT = wk.T
    wvT = wv.T
    woT = wo.T
    w1T = np.ascontiguousarray(w1.T)
    w2T = np.ascontiguousarray(w2.T)
    w3T = np.ascontiguousarray(w3.T)

    nc, _ = _get_program()
    in_maps = []
    for c in range(8):
        b, g = c // 2, c % 2
        in_maps.append({
            "x": np.ascontiguousarray(x[b]),
            "wqT": blk_proj(wqT, g),
            "wkT": blk_proj(wkT, g),
            "wvT": blk_proj(wvT, g),
            "woT": blk_wo(woT, g),
            "w1T": np.ascontiguousarray(w1T[:, g * HF:(g + 1) * HF]),
            "w2T": np.ascontiguousarray(w2T[:, g * HF:(g + 1) * HF]),
            "w3T": np.ascontiguousarray(w3T[g * HF:(g + 1) * HF, :]),
            "rows_idx": (g * T + np.arange(NTT, dtype=np.int32)[None, :] * 128
                         + np.arange(128, dtype=np.int32)[:, None]).astype(np.int32),
        })
    res = run_bass_kernel_spmd(nc, in_maps, core_ids=list(range(8)))
    out = np.empty((4, T, D), dtype=np.float32)
    for b in range(4):
        out[b] = res.results[2 * b]["out"]
    return out


# version 12
# speedup vs baseline: 3.3793x; 1.0454x over previous
import numpy as np
import ml_dtypes
from contextlib import ExitStack

import concourse.bass as bass
import concourse.mybir as mybir
from concourse.bass_utils import run_bass_kernel_spmd
from concourse.bass_core_rust import IndirectOffsetOnAxis

F32 = mybir.dt.float32
BF16 = mybir.dt.bfloat16
AF = mybir.ActivationFunctionType
ALU = mybir.AluOpType
AX = mybir.AxisListType

T = 4096
D = 2048
NHL = 8
HF = 2816
NHC = HF // 128
NTT = T // 128
NTC = T // 512
NDC = D // 128
SCALE = 1.0 / float(np.sqrt(128.0))
LN_EPS = 1e-5
INV_D = 1.0 / D
Q_S1 = 0.5 / (5632.0 * 2048.0)
Q_S2 = 0.5e-8
GROUPS = [[0, 1], [2, 3], [4, 5], [6, 7]]

UPTO = "full"
FIFO_SKIP = True
DMA_STRICT = False
COMBINE_DEFER = True
DEBUG_DUMPS = ()

QINFO = {
    "tensor": ("tensor", 1),
    "vector": ("vector", 1),
    "scalar": ("scalar", 1),
    "gpsimd": ("gpsimd", 1),
    "ld": ("sync", 16),
    "st": ("scalar", 16),
    "gd": ("gpsimd", 16),
}
QUEUES = tuple(QINFO)


class Prog:
    def __init__(self):
        self.items = []
        self.res = {}
        self.pending = {q: set() for q in QUEUES}
        self.last_in_q = {q: None for q in QUEUES}

    def emit(self, q, fn, rd=(), wr=(), tag="", after=()):
        idx = len(self.items)
        deps = set(self.pending[q])
        deps.update(after)
        self.pending[q] = set()
        for k in rd:
            e = self.res.get(k)
            if e is not None and e["w"] is not None:
                deps.add(e["w"])
        for k in wr:
            e = self.res.get(k)
            if e is not None:
                if e["w"] is not None:
                    deps.add(e["w"])
                deps.update(e["r"].values())
        for k in rd:
            self.res.setdefault(k, {"w": None, "r": {}})["r"][q] = idx
        for k in wr:
            e = self.res.setdefault(k, {"w": None, "r": {}})
            e["w"] = idx
            e["r"] = {}
        deps.discard(idx)
        self.items.append({"q": q, "fn": fn, "deps": deps, "tag": tag})
        self.last_in_q[q] = idx
        return idx

    def barrier(self):
        lasts = {i for i in self.last_in_q.values() if i is not None}
        for q in QUEUES:
            self.pending[q] |= lasts
        self.res = {}


def _finish(nc, p, dbg, root):
    bounce = None
    for name, (dbg_ap, src_ap) in dbg.items():
        if name in ("s_pool", "f_pool"):
            if bounce is None:
                bounce = root.enter_context(
                    nc.sbuf_tensor("dbg_bounce", [128, D], F32))
            for r0 in range(0, src_ap.shape[0], 128):
                p.emit("ld", lambda e, ap=src_ap, r0=r0: e.dma_start(
                    out=bounce[:], in_=ap[r0:r0 + 128, :]),
                    rd=(name,), wr=("dbg_b",))
                p.emit("st", lambda e, dbg_ap=dbg_ap, r0=r0: e.dma_start(
                    out=dbg_ap[r0:r0 + 128, :], in_=bounce[:]), rd=("dbg_b",),
                    wr=("dbg_b",))
            continue
        p.emit("st", lambda e, dbg_ap=dbg_ap, ap=src_ap: e.dma_start(
            out=dbg_ap, in_=ap), rd=(name,))
    if dbg:
        pass

    items = p.items
    n = len(items)
    hard = [None] * n
    for idx, it in enumerate(items):
        my_eng = QINFO[it["q"]][0]
        hd = []
        for d in it["deps"]:
            dq = items[d]["q"]
            d_eng, d_inc = QINFO[dq][0], QINFO[dq][1]
            if FIFO_SKIP and d_inc == 1 and d_eng == my_eng == "tensor" \
                    and it["q"] == "tensor":
                continue
            hd.append(d)
        hard[idx] = hd
    needed = set()
    for hd in hard:
        needed.update(d for d in hd if QINFO[items[d]["q"]][1] == 1)
    NLANE = 8
    val = [0] * n
    sem_of = [None] * n
    cnt = {}
    dseq = {q: 0 for q in QUEUES}
    for idx, it in enumerate(items):
        q = it["q"]
        if QINFO[q][1] == 16:
            lane = dseq[q] % NLANE
            dseq[q] += 1
            sid = f"{q}{lane}"
            cnt[sid] = cnt.get(sid, 0) + 16
            val[idx] = cnt[sid]
            sem_of[idx] = sid
            it["inc"] = 16
        elif idx in needed:
            cnt[q] = cnt.get(q, 0) + 1
            val[idx] = cnt[q]
            sem_of[idx] = q
            it["inc"] = 1
        else:
            it["inc"] = 0
        it["sem"] = sem_of[idx]
    satisfied = {q: {} for q in QUEUES}
    waits = [[] for _ in range(n)]
    for idx, it in enumerate(items):
        q = it["q"]
        for d in sorted(hard[idx]):
            sid = sem_of[d]
            v = val[d]
            assert v > 0 and sid is not None
            if satisfied[q].get(sid, 0) >= v:
                continue
            waits[idx].append((sid, v))
            satisfied[q][sid] = v
        it["waits"] = waits[idx]

    sems = {}
    for sid in cnt:
        sems[sid] = root.enter_context(nc.semaphore("sem_" + sid))
    blk = root.enter_context(nc.Block())

    final_counts = dict(cnt)

    def run_body(body_eng):
        def body(eng):
            for it in items:
                if QINFO[it["q"]][0] != body_eng:
                    continue
                for (dq, v) in it["waits"]:
                    eng.wait_ge(sems[dq], v)
                inst = it["fn"](eng)
                if it["inc"]:
                    inst.then_inc(sems[it["sem"]], it["inc"])
            if body_eng == "sync":
                for sid, v in final_counts.items():
                    eng.wait_ge(sems[sid], v)
        return body

    blk.sync(run_body("sync"))
    blk.tensor(run_body("tensor"))
    blk.vector(run_body("vector"))
    blk.scalar(run_body("scalar"))
    blk.gpsimd(run_body("gpsimd"))
    return nc, p


def build_program(upto=UPTO, debug=()):
    nc = bass.Bass("TRN2", target_bir_lowering=False)

    x = nc.dram_tensor("x", [T, D], F32, kind="ExternalInput").ap()
    wqT = nc.dram_tensor("wqT", [128, NDC * 1024], F32, kind="ExternalInput").ap()
    wkT = nc.dram_tensor("wkT", [128, NDC * 1024], F32, kind="ExternalInput").ap()
    wvT = nc.dram_tensor("wvT", [128, NDC * 1024], F32, kind="ExternalInput").ap()
    woT = nc.dram_tensor("woT", [128, NHL * D], F32, kind="ExternalInput").ap()
    w1T = nc.dram_tensor("w1T", [D, HF], F32, kind="ExternalInput").ap()
    w1B = nc.dram_tensor("w1B", [128, NHC * D], F32, kind="ExternalInput").ap()
    w2T = nc.dram_tensor("w2T", [D, HF], F32, kind="ExternalInput").ap()
    w2B = nc.dram_tensor("w2B", [128, NHC * D], F32, kind="ExternalInput").ap()
    w3T = nc.dram_tensor("w3T", [HF, D], F32, kind="ExternalInput").ap()
    out = nc.dram_tensor("out", [T, D], F32, kind="ExternalOutput").ap()

    xnT_s = nc.dram_tensor("xnT_s", [NTC, 128, NDC * 512], BF16, kind="Internal").ap()
    kT_s = nc.dram_tensor("kT_s", [NHL, 128, T], BF16, kind="Internal").ap()
    qT_s = nc.dram_tensor("qT_s", [NHL, 128, T], BF16, kind="Internal").ap()
    v_s = nc.dram_tensor("v_s", [NHL, NTT, 128, 128], BF16, kind="Internal").ap()
    w1q_s = nc.dram_tensor("w1q_s", [NHC, 128, D], BF16, kind="Internal").ap()
    w2q_s = nc.dram_tensor("w2q_s", [NHC, 128, D], BF16, kind="Internal").ap()
    w3q_s = nc.dram_tensor("w3q_s", [NHC, 128, D], BF16, kind="Internal").ap()
    rows_idx = nc.dram_tensor("rows_idx", [128, NTT], mybir.dt.int32,
                              kind="ExternalInput").ap()
    s_pool = nc.dram_tensor("s_pool", [2 * T, D], F32, kind="Internal",
                            addr_space="Shared").ap()
    f_pool = nc.dram_tensor("f_pool", [2 * T, D], F32, kind="Internal",
                            addr_space="Shared").ap()
    bar_in = nc.dram_tensor("bar_in", [1, 4], F32, kind="Internal").ap()
    bar_out = nc.dram_tensor("bar_out", [1, 4], F32, kind="Internal").ap()
    gam_part = nc.dram_tensor("gam_part", [1, 4], F32, kind="Internal").ap()
    gam_full = nc.dram_tensor("gam_full", [1, 4], F32, kind="Internal").ap()
    colb = nc.dram_tensor("colb", [128], F32, kind="Internal").ap()

    dbg = {}
    scratch_map = {"xnT_s": xnT_s, "kT_s": kT_s, "qT_s": qT_s, "v_s": v_s,
                   "w1q_s": w1q_s, "w2q_s": w2q_s, "w3q_s": w3q_s,
                   "s_pool": s_pool, "f_pool": f_pool, "gam_full": gam_full}
    for name in debug:
        if name not in scratch_map:
            continue
        src = scratch_map[name]
        dbg[name] = (nc.dram_tensor("dbg_" + name, list(src.shape),
                                    src.dtype, kind="ExternalOutput").ap(), src)

    ident_dram = nc.inline_tensor(np.eye(128, dtype=ml_dtypes.bfloat16), "ident_c")
    ones_dram = nc.inline_tensor(np.ones((1, 128), dtype=np.float32), "ones_c")
    _mk = np.zeros((4, 128, 512), dtype=np.float32)
    for _i in range(4):
        _p = np.arange(128)[:, None]
        _f = np.arange(512)[None, :]
        _mk[_i] = np.where(128 * _i + _p - _f >= 0, 1.0, 0.0)
    cmask_dram = nc.inline_tensor(_mk.astype(ml_dtypes.bfloat16), "cmask_c")

    p = Prog()

    with ExitStack() as root:
        sb = lambda n_, s_, dt: root.enter_context(nc.sbuf_tensor(n_, s_, dt))
        ident = sb("ident_sb", [128, 128], BF16)
        ones_f = sb("ones_f", [1, 128], F32)
        h_bc = sb("h_bc", [128, 4], F32)
        hn_bc = sb("hn_bc", [128, 4], F32)
        cmask = sb("cmask_sb", [128, 4 * 512], BF16)
        rows_sb = sb("rows_sb", [128, NTT], mybir.dt.int32)
        barv = sb("barv", [1, 4], F32)

        p.emit("ld", lambda e: e.dma_start(out=ident[:], in_=ident_dram.ap()),
               wr=("ident",))
        p.emit("ld", lambda e: e.dma_start(out=ones_f[:], in_=ones_dram.ap()),
               wr=("ones",))
        p.emit("ld", lambda e: e.dma_start(
            out=cmask[:].rearrange("p (c f) -> p c f", c=4),
            in_=cmask_dram.ap().rearrange("c p f -> p c f")), wr=("cmask",))
        p.emit("ld", lambda e: e.dma_start(out=rows_sb[:], in_=rows_idx),
               wr=("rows",))
        p.emit("vector", lambda e: e.memset(barv[:], 1.0), wr=("barv",))
        bar_ctr = [0]

        def pair_barrier(after=()):
            bar_ctr[0] += 1
            bid = p.emit("st", lambda e: e.dma_start(out=bar_in, in_=barv[:]),
                         rd=("barv",), after=after)
            return p.emit("gpsimd", lambda e: e.collective_compute(
                "AllReduce", ALU.add, replica_groups=GROUPS,
                ins=[bar_in.opt()], outs=[bar_out.opt()]), after=(bid,))

        with ExitStack() as ph:
            sb1 = lambda n_, s_, dt: ph.enter_context(nc.sbuf_tensor(n_, s_, dt))
            xt = [sb1(f"xt{j}", [128, D], F32) for j in range(2)]
            xsq = sb1("xsq", [128, D], F32)
            xn4 = [sb1(f"xn{i}", [128, D], BF16) for i in range(4)]
            xnTc = [sb1(f"xnTc{j}", [128, NDC * 512], BF16) for j in range(2)]
            lnv = [sb1(f"lnv{j}", [128, 8], F32) for j in range(2)]
            ptr = [ph.enter_context(nc.psum_tensor(f"ptr{j}", [128, 512], BF16))
                   for j in range(2)]

            for tc in range(NTC):
                cp = tc % 2
                for i in range(4):
                    ti = tc * 4 + i
                    j = ti % 2
                    xb, lv, xi = xt[j], lnv[j], xn4[i]
                    kx, kl, kxn = f"xt{j}", f"lnv{j}", f"xn{i}"
                    nm, ssq, t1v = lv[:, 0:1], lv[:, 1:2], lv[:, 2:3]
                    msq, veps, stdv, rstd = lv[:, 3:4], lv[:, 4:5], lv[:, 5:6], lv[:, 6:7]
                    p.emit("ld", lambda e, ti=ti, xb=xb: e.dma_start(
                        out=xb[:], in_=x[ti * 128:(ti + 1) * 128, :]), wr=(kx,))
                    p.emit("vector", lambda e, xb=xb, nm=nm: e.reduce_sum(
                        out=nm, in_=xb[:], axis=AX.X, negate=True),
                        rd=(kx,), wr=(kl,))
                    p.emit("vector", lambda e, nm=nm: e.tensor_scalar_mul(
                        nm, nm, INV_D), rd=(kl,), wr=(kl,))
                    p.emit("scalar", lambda e, xb=xb, ssq=ssq: e.activation(
                        out=xsq[:], in_=xb[:], func=AF.Square, accum_out=ssq),
                        rd=(kx,), wr=("xsq", kl + "s"))
                    p.emit("vector", lambda e, ssq=ssq, t1v=t1v: e.tensor_scalar(
                        out=t1v, in0=ssq, scalar1=INV_D, scalar2=LN_EPS,
                        op0=ALU.mult, op1=ALU.add), rd=(kl + "s",), wr=(kl,))
                    p.emit("vector", lambda e, nm=nm, msq=msq: e.tensor_single_scalar(
                        out=msq, in_=nm, scalar=nm, op=ALU.mult), rd=(kl,), wr=(kl,))
                    p.emit("vector", lambda e, t1v=t1v, msq=msq, veps=veps:
                           e.tensor_sub(veps, t1v, msq), rd=(kl,), wr=(kl,))
                    p.emit("scalar", lambda e, veps=veps, stdv=stdv: e.activation(
                        out=stdv, in_=veps, func=AF.Sqrt), rd=(kl,), wr=(kl + "s",))
                    p.emit("vector", lambda e, stdv=stdv, rstd=rstd: e.reciprocal(
                        rstd, stdv), rd=(kl + "s",), wr=(kl,))
                    p.emit("vector", lambda e, xb=xb, nm=nm, rstd=rstd, xi=xi:
                           e.tensor_scalar(
                        out=xi[:], in0=xb[:], scalar1=nm, scalar2=rstd,
                        op0=ALU.add, op1=ALU.mult), rd=(kx, kl), wr=(kxn,))
                for dc in range(NDC):
                    pj = dc % 2
                    for i in range(4):
                        p.emit("tensor", lambda e, i=i, dc=dc, pj=pj: e.transpose(
                            ptr[pj][:, i * 128:(i + 1) * 128],
                            xn4[i][:, dc * 128:(dc + 1) * 128], ident[:]),
                            rd=(f"xn{i}", "ident"), wr=(f"ptr{pj}",))
                    p.emit("scalar", lambda e, dc=dc, pj=pj, cp=cp: e.copy(
                        out=xnTc[cp][:, dc * 512:(dc + 1) * 512], in_=ptr[pj][:]),
                        rd=(f"ptr{pj}",), wr=(f"xnTc{cp}", f"ptr{pj}"))
                p.emit("st", lambda e, tc=tc, cp=cp: e.dma_start(
                    out=xnT_s[tc], in_=xnTc[cp][:]), rd=(f"xnTc{cp}",))
            gw = [sb1(f"gw{j}", [128, 2816], F32) for j in range(2)]
            gcols = sb1("gcols", [128, 40], F32)
            gcol = sb1("gcol", [128, 1], F32)
            grow = sb1("grow", [1, 128], F32)
            gpart = sb1("gpart", [1, 4], F32)
            p.emit("vector", lambda e: e.memset(gpart[:], 0.0), wr=("gpart",))
            li = 0
            for wi, wsrc in enumerate((w1T, w2T, w3T)):
                nblk = NDC if wi < 2 else NHC
                ncol = HF if wi < 2 else D
                for blk_i in range(nblk):
                    j = li % 2
                    li += 1
                    p.emit("ld", lambda e, wsrc=wsrc, blk_i=blk_i, ncol=ncol, j=j:
                           e.dma_start(
                        out=gw[j][:, 0:ncol],
                        in_=wsrc[blk_i * 128:(blk_i + 1) * 128, :]), wr=(f"gw{j}",))
                    p.emit("vector", lambda e, blk_i=blk_i, ncol=ncol, j=j:
                           e.reduce_sum(
                        out=gcols[:, blk_i:blk_i + 1],
                        in_=gw[j][:, 0:ncol],
                        axis=AX.X, apply_absolute_value=True),
                        rd=(f"gw{j}",), wr=(f"gcols{blk_i}", f"gw{j}"))
                p.emit("vector", lambda e, nblk=nblk: e.reduce_sum(
                    out=gcol[:], in_=gcols[:, 0:nblk], axis=AX.X),
                    rd=tuple(f"gcols{b2}" for b2 in range(nblk)), wr=("gcol",))
                p.emit("st", lambda e: e.dma_start(out=colb, in_=gcol[:]),
                       rd=("gcol",), wr=("colb",))
                p.emit("ld", lambda e: e.dma_start(
                    out=grow[:], in_=colb.rearrange("(o p) -> o p", o=1)),
                    rd=("colb",), wr=("grow",))
                p.emit("vector", lambda e, wi=wi: e.reduce_sum(
                    out=gpart[:, wi:wi + 1], in_=grow[:], axis=AX.X),
                    rd=("grow",), wr=("gpart", "grow", "gcol"))
            p.emit("st", lambda e: e.dma_start(out=gam_part, in_=gpart[:]),
                   rd=("gpart",), wr=("gam_part",))
            p.emit("gpsimd", lambda e: e.collective_compute(
                "AllReduce", ALU.add, replica_groups=GROUPS,
                ins=[gam_part.opt()], outs=[gam_full.opt()]),
                rd=("gam_part",), wr=("gam_full",))
        p.barrier()
        if upto == "p1a":
            return _finish(nc, p, dbg, root)

        with ExitStack() as ph:
            sb1 = lambda n_, s_, dt: ph.enter_context(nc.sbuf_tensor(n_, s_, dt))
            wq_sb = sb1("wq_sb", [128, NDC * 1024], BF16)
            wk_sb = sb1("wk_sb", [128, NDC * 1024], BF16)
            wv_sb = sb1("wv_sb", [128, NDC * 1024], BF16)
            wload = [sb1(f"wload{j}", [128, 2816], F32) for j in range(2)]
            li = 0
            for wname, wdst, wsrc in (("q", wq_sb, wqT), ("k", wk_sb, wkT),
                                      ("v", wv_sb, wvT)):
                for dc in range(NDC):
                    j = li % 2
                    li += 1
                    p.emit("ld", lambda e, wsrc=wsrc, dc=dc, j=j: e.dma_start(
                        out=wload[j][:, 0:1024],
                        in_=wsrc[:, dc * 1024:(dc + 1) * 1024]), wr=(f"wload{j}",))
                    p.emit("scalar", lambda e, wdst=wdst, dc=dc, j=j: e.copy(
                        out=wdst[:, dc * 1024:(dc + 1) * 1024],
                        in_=wload[j][:, 0:1024]),
                        rd=(f"wload{j}",), wr=("w" + wname, f"wload{j}"))

            grow4 = sb1("grow4", [1, 4], F32)
            hrow = sb1("hrow", [1, 4], F32)
            hnrow = sb1("hnrow", [1, 4], F32)
            p.emit("ld", lambda e: e.dma_start(out=grow4[:], in_=gam_full),
                   rd=("gam_full",), wr=("grow4",))
            p.emit("vector", lambda e: e.tensor_scalar(
                out=hrow[:], in0=grow4[:], scalar1=Q_S1, scalar2=Q_S2,
                op0=ALU.mult, op1=ALU.add), rd=("grow4",), wr=("hrow",))
            p.emit("vector", lambda e: e.tensor_scalar_mul(hnrow[:], hrow[:], -1.0),
                   rd=("hrow",), wr=("hnrow",))
            pbc = ph.enter_context(nc.psum_tensor("pbc", [128, 4], F32))
            p.emit("tensor", lambda e: e.matmul(
                pbc[:], ones_f[:], hrow[:], start=True, stop=True),
                rd=("ones", "hrow"), wr=("pbc",))
            p.emit("vector", lambda e: e.tensor_copy(out=h_bc[:], in_=pbc[:]),
                   rd=("pbc",), wr=("h_bc", "pbc"))
            p.emit("tensor", lambda e: e.matmul(
                pbc[:], ones_f[:], hnrow[:], start=True, stop=True),
                rd=("ones", "hnrow"), wr=("pbc",))
            p.emit("vector", lambda e: e.tensor_copy(out=hn_bc[:], in_=pbc[:]),
                   rd=("pbc",), wr=("hn_bc", "pbc"))

            wfp = [sb1(f"wfp{j}", [128, D], F32) for j in range(2)]
            tneg = [sb1(f"tneg{j}", [128, D], BF16) for j in range(2)]
            wqt = [sb1(f"wqt{j}", [128, D], BF16) for j in range(2)]
            quant_work = []
            qi = 0
            for wi, (wsrc, wdst_s) in enumerate(((w1B, w1q_s), (w2B, w2q_s),
                                                 (w3T, w3q_s))):
                for hc in range(NHC):
                    j = qi % 2
                    qi += 1

                    def q_block(wi=wi, wsrc=wsrc, wdst_s=wdst_s, hc=hc, j=j):
                        if wi < 2:
                            p.emit("ld", lambda e, wsrc=wsrc, hc=hc, j=j:
                                   e.dma_start(
                                out=wfp[j][:],
                                in_=wsrc[:, hc * D:(hc + 1) * D]),
                                wr=(f"wfp{j}",))
                        else:
                            p.emit("ld", lambda e: e.dma_start(
                                out=wfp[j][:], in_=wsrc[hc * 128:(hc + 1) * 128, :]),
                                wr=(f"wfp{j}",))
                        p.emit("vector", lambda e: e.tensor_single_scalar(
                            out=tneg[j][:], in_=wfp[j][:], scalar=hn_bc[:, wi:wi + 1],
                            op=ALU.is_lt), rd=(f"wfp{j}", "hn_bc"), wr=(f"tneg{j}",))
                        p.emit("vector", lambda e: e.scalar_tensor_tensor(
                            out=wqt[j][:], in0=wfp[j][:], scalar=h_bc[:, wi:wi + 1],
                            in1=tneg[j][:], op0=ALU.is_gt, op1=ALU.subtract),
                            rd=(f"wfp{j}", f"tneg{j}", "h_bc"),
                            wr=(f"wqt{j}", f"wfp{j}"))
                        p.emit("st", lambda e: e.dma_start(
                            out=wdst_s[hc], in_=wqt[j][:]), rd=(f"wqt{j}",))
                    quant_work.append(q_block)
            if upto == "quant":
                for qb in quant_work:
                    qb()
                p.barrier()
                return _finish(nc, p, dbg, root)

            xnT_sb = [sb1(f"xnT_sb{j}", [128, NDC * 512], BF16) for j in range(2)]
            pstage = [sb1(f"pstage{j}", [128, 512], BF16) for j in range(2)]
            pproj = [ph.enter_context(nc.psum_tensor(f"pproj{j}", [128, 512], F32))
                     for j in range(2)]
            ei = 0
            nq = (len(quant_work) + NTC - 1) // NTC
            for tc in range(NTC):
                cj = tc % 2
                p.emit("ld", lambda e, tc=tc, cj=cj: e.dma_start(
                    out=xnT_sb[cj][:], in_=xnT_s[tc]), rd=("xnTw",), wr=(f"xnT{cj}",))
                for qb in quant_work[tc * nq:(tc + 1) * nq]:
                    qb()
                for wkey, wsb, dst_s in (("wk", wk_sb, kT_s), ("wq", wq_sb, qT_s)):
                    for oc in range(NHL):
                        j = ei % 2
                        ei += 1
                        for dc in range(NDC):
                            p.emit("tensor", lambda e, wsb=wsb, oc=oc, dc=dc, j=j,
                                   cj=cj: e.matmul(
                                pproj[j][:],
                                wsb[:, dc * 1024 + oc * 128: dc * 1024 + (oc + 1) * 128],
                                xnT_sb[cj][:, dc * 512:(dc + 1) * 512],
                                start=(dc == 0), stop=(dc == NDC - 1)),
                                rd=(wkey, f"xnT{cj}"), wr=(f"pproj{j}",))
                        p.emit("vector", lambda e, j=j: e.tensor_copy(
                            out=pstage[j][:], in_=pproj[j][:]),
                            rd=(f"pproj{j}",), wr=(f"pstage{j}", f"pproj{j}"))
                        p.emit("st", lambda e, dst_s=dst_s, oc=oc, tc=tc, j=j:
                               e.dma_start(
                            out=dst_s[oc, :, tc * 512:(tc + 1) * 512],
                            in_=pstage[j][:]), rd=(f"pstage{j}",))
                for sv in range(4):
                    st_g = tc * 4 + sv
                    for dvc in range(2):
                        j = ei % 2
                        ei += 1
                        for dc in range(NDC):
                            p.emit("tensor", lambda e, sv=sv, dvc=dvc, dc=dc, j=j,
                                   cj=cj: e.matmul(
                                pproj[j][:],
                                xnT_sb[cj][:, dc * 512 + sv * 128: dc * 512 + (sv + 1) * 128],
                                wv_sb[:, dc * 1024 + dvc * 512: dc * 1024 + (dvc + 1) * 512],
                                start=(dc == 0), stop=(dc == NDC - 1)),
                                rd=("wv", f"xnT{cj}"), wr=(f"pproj{j}",))
                        p.emit("vector", lambda e, j=j: e.tensor_copy(
                            out=pstage[j][:], in_=pproj[j][:]),
                            rd=(f"pproj{j}",), wr=(f"pstage{j}", f"pproj{j}"))
                        for u in range(4):
                            p.emit("st", lambda e, dvc=dvc, u=u, st_g=st_g, j=j:
                                   e.dma_start(
                                out=v_s[4 * dvc + u, st_g],
                                in_=pstage[j][:, u * 128:(u + 1) * 128]),
                                rd=(f"pstage{j}",))
        p.barrier()
        if upto == "proj":
            return _finish(nc, p, dbg, root)

        with ExitStack() as ph:
            sb1 = lambda n_, s_, dt: ph.enter_context(nc.sbuf_tensor(n_, s_, dt))
            zT_sb = sb1("zT_sb", [128, NHL * NTT * 128], BF16)
            pa = ph.enter_context(ExitStack())
            sb2 = lambda n_, s_, dt: pa.enter_context(nc.sbuf_tensor(n_, s_, dt))
            v_hb = [sb2(f"v_hb{j}", [128, NTT * 128], BF16) for j in range(2)]
            kT_sb = [sb2(f"kT_sb{j}", [128, T], BF16) for j in range(2)]
            qT_sb = [sb2(f"qT_sb{j}", [128, T], BF16) for j in range(2)]
            pbuf = [sb2(f"pbuf{j}", [128, T], BF16) for j in range(4)]
            ptbuf = sb2("ptbuf", [128, NTT * 512], BF16)
            smv = [sb2(f"smv{j}", [128, 4], F32) for j in range(4)]
            dtmp = [sb2(f"dtmp{j}", [128, 512], BF16) for j in range(4)]
            lsums = [sb2(f"lsums{j}", [128, 12], F32) for j in range(4)]
            psum_pair = [pa.enter_context(
                nc.psum_tensor(f"psum_pair{j}", [128, 1024], F32))
                for j in range(2)]
            psum_s = [psum_pair[0][:, 0:512], psum_pair[0][:, 512:1024],
                      psum_pair[1][:, 0:512], psum_pair[1][:, 512:1024]]
            psum_pt = [pa.enter_context(nc.psum_tensor(f"psum_pt{j}", [128, 512], BF16))
                       for j in range(2)]
            psum_z = [pa.enter_context(nc.psum_tensor(f"psum_z{j}", [128, 512], F32))
                      for j in range(2)]
            s_ctr = [0]
            pt_ctr = [0]

            def tile_S(ti, tj, hj):
                st = ti // 4
                i = ti % 4
                p.emit("vector", lambda e, tj=tj: e.memset(
                    lsums[tj][:], 0.0), wr=(f"ls{tj}",))
                kcs = list(range(st + 1))
                while kcs:
                    if len(kcs) >= 2 and kcs[0] != st and kcs[1] != st \
                            and s_ctr[0] % 2 == 0:
                        pair_k = [kcs.pop(0), kcs.pop(0)]
                    else:
                        pair_k = [kcs.pop(0)]
                    sjs = []
                    for kc in pair_k:
                        sj = s_ctr[0] % 4
                        s_ctr[0] += 1
                        sjs.append(sj)
                        p.emit("tensor", lambda e, ti=ti, kc=kc, sj=sj, hj=hj:
                               e.matmul(
                            psum_s[sj][:],
                            qT_sb[hj][:, ti * 128:(ti + 1) * 128],
                            kT_sb[hj][:, kc * 512:(kc + 1) * 512],
                            start=True, stop=True),
                            rd=(f"q{hj}", f"k{hj}"), wr=(f"ps{sj}",))
                    kc0 = pair_k[0]
                    if len(pair_k) == 2:
                        src = psum_pair[sjs[0] // 2]
                        p.emit("scalar", lambda e, kc0=kc0, tj=tj, src=src:
                               e.activation(
                            out=pbuf[tj][:, kc0 * 512:(kc0 + 2) * 512],
                            in_=src[:], func=AF.Exp, scale=SCALE,
                            accum_out=lsums[tj][:, kc0 // 2:kc0 // 2 + 1]),
                            rd=(f"ps{sjs[0]}", f"ps{sjs[1]}"),
                            wr=(f"pb{tj}", f"ls{tj}",
                                f"ps{sjs[0]}", f"ps{sjs[1]}"))
                    elif kc0 == st:
                        p.emit("scalar", lambda e, tj=tj, sj=sjs[0]:
                               e.activation(
                            out=dtmp[tj][:], in_=psum_s[sj][:],
                            func=AF.Exp, scale=SCALE),
                            rd=(f"ps{sjs[0]}",), wr=(f"dt{tj}", f"ps{sjs[0]}"))
                    else:
                        p.emit("scalar", lambda e, kc=kc0, tj=tj, sj=sjs[0]:
                               e.activation(
                            out=pbuf[tj][:, kc * 512:(kc + 1) * 512],
                            in_=psum_s[sj][:], func=AF.Exp, scale=SCALE,
                            accum_out=lsums[tj][:, st + 1 + kc % 2:
                                              st + 2 + kc % 2]),
                            rd=(f"ps{sj}",),
                            wr=(f"pb{tj}", f"ls{tj}", f"ps{sj}"))

            def tile_soft(ti, tj):
                st = ti // 4
                i = ti % 4
                lsum = smv[tj][:, 1:2]
                rl = smv[tj][:, 2:3]
                p.emit("vector", lambda e, i=i, tj=tj, st=st:
                       e.scalar_tensor_tensor(
                    out=pbuf[tj][:, st * 512:(st + 1) * 512],
                    in0=dtmp[tj][:], scalar=1.0,
                    in1=cmask[:, i * 512:(i + 1) * 512],
                    op0=ALU.mult, op1=ALU.mult,
                    accum_out=lsums[tj][:, st + 3:st + 4]),
                    rd=(f"dt{tj}", "cmask"),
                    wr=(f"pb{tj}", f"ls{tj}", f"dt{tj}"))
                p.emit("vector", lambda e, st=st, tj=tj, lsum=lsum:
                       e.reduce_sum(
                    out=lsum, in_=lsums[tj][:, 0:st + 4], axis=AX.X),
                    rd=(f"ls{tj}",), wr=(f"smv{tj}s",))
                p.emit("vector", lambda e, rl=rl, lsum=lsum: e.reciprocal(
                    rl, lsum), rd=(f"smv{tj}s",), wr=(f"smv{tj}",))
                for scq in range(st + 1):
                    p.emit("vector", lambda e, scq=scq, tj=tj, rl=rl:
                           e.tensor_single_scalar(
                        out=pbuf[tj][:, scq * 512:(scq + 1) * 512],
                        in_=pbuf[tj][:, scq * 512:(scq + 1) * 512],
                        scalar=rl, op=ALU.mult),
                        rd=(f"pb{tj}", f"smv{tj}"), wr=(f"pbn{tj}_{scq}",))

            def tile_pt(ti, tj):
                st = ti // 4
                i = ti % 4
                nsc_i = ti + 1
                nsc = 4 * (st + 1)
                for scq in range(st + 1):
                    cnt = min(4, nsc_i - 4 * scq)
                    pj = pt_ctr[0] % 2
                    pt_ctr[0] += 1
                    for u in range(cnt):
                        sc = scq * 4 + u
                        p.emit("tensor", lambda e, sc=sc, u=u, tj=tj, pj=pj:
                               e.transpose(
                            psum_pt[pj][:, u * 128:(u + 1) * 128],
                            pbuf[tj][:, sc * 128:(sc + 1) * 128], ident[:]),
                            rd=(f"pbn{tj}_{scq}", f"pb{tj}", "ident"),
                            wr=(f"ppt{pj}",))
                    dst = ptbuf[:].rearrange("p (c f) -> p c f", f=512)[
                        :, scq * 4: scq * 4 + cnt, i * 128:(i + 1) * 128]
                    src = psum_pt[pj][:, 0:cnt * 128].rearrange(
                        "p (c f) -> p c f", f=128)
                    p.emit("vector", lambda e, dst=dst, src=src:
                           e.tensor_copy(out=dst, in_=src),
                           rd=(f"ppt{pj}",),
                           wr=(f"ppt{pj}",) + tuple(
                               f"ptc{scq * 4 + u}_{i}" for u in range(cnt)))
                for sc in range(nsc_i, nsc):
                    p.emit("vector", lambda e, sc=sc, i=i: e.memset(
                        ptbuf[:, sc * 512 + i * 128: sc * 512 + (i + 1) * 128],
                        0.0), wr=(f"ptc{sc}_{i}",))

            def do_z(st, vj, h):
                nsc = 4 * (st + 1)
                zj = st % 2
                for sc in range(nsc):
                    p.emit("tensor", lambda e, sc=sc, vj=vj, zj=zj: e.matmul(
                        psum_z[zj][:],
                        v_hb[vj][:, sc * 128:(sc + 1) * 128],
                        ptbuf[:, sc * 512:(sc + 1) * 512],
                        start=(sc == 0), stop=(sc == nsc - 1)),
                        rd=(f"v{vj}",) + tuple(
                            f"ptc{sc}_{i2}" for i2 in range(4)),
                        wr=(f"pz{zj}",))
                p.emit("vector", lambda e, h=h, st=st, zj=zj: e.tensor_copy(
                    out=zT_sb[:, (h * NTT + st * 4) * 128:
                              (h * NTT + st * 4 + 4) * 128],
                    in_=psum_z[zj][:]),
                    rd=(f"pz{zj}",), wr=("zT", f"pz{zj}"))

            for h in range(NHL):
                hj = h % 2
                vj = hj
                p.emit("ld", lambda e, h=h, hj=hj: e.dma_start(
                    out=v_hb[hj][:].rearrange("p (c f) -> p c f", c=NTT),
                    in_=v_s[h].rearrange("c p f -> p c f")), wr=(f"v{hj}",))
                p.emit("ld", lambda e, h=h, hj=hj: e.dma_start(
                    out=kT_sb[hj][:], in_=kT_s[h]), wr=(f"k{hj}",))
                p.emit("ld", lambda e, h=h, hj=hj: e.dma_start(
                    out=qT_sb[hj][:], in_=qT_s[h]), wr=(f"q{hj}",))
                pend = []

                def pop_one():
                    pti, ptj = pend.pop(0)
                    tile_soft(pti, ptj)
                    tile_pt(pti, ptj)
                    if pti % 4 == 3:
                        do_z(pti // 4, vj, h)

                for st in range(NTC):
                    for i in range(4):
                        ti = st * 4 + i
                        tile_S(ti, ti % 4, hj)
                        pend.append((ti, ti % 4))
                        if len(pend) > 2:
                            pop_one()
                while pend:
                    pop_one()
            if upto == "p2":
                p.barrier()
                return _finish(nc, p, dbg, root)

            pa.close()
            p.barrier()
            wo_sb = sb1("wo_sb", [128, NHL * D], BF16)
            wol_t = sb1("wol_t", [128, D], F32)
            sst_t = sb1("sst_t", [128, 2 * D], F32)
            wol = wol_t[:]
            sstage = [sst_t[:, 0:D], sst_t[:, D:2 * D]]
            psum_wo = [ph.enter_context(
                nc.psum_tensor(f"psum_wo{j}", [128, 512], F32)) for j in range(2)]
            for hc in range(NHL):
                p.emit("ld", lambda e, hc=hc: e.dma_start(
                    out=wol, in_=woT[:, hc * D:(hc + 1) * D]), wr=("wol",))
                p.emit("scalar", lambda e, hc=hc: e.copy(
                    out=wo_sb[:, hc * D:(hc + 1) * D], in_=wol),
                    rd=("wol",), wr=("wo", "wol"))
            sstores = []
            for tt in range(NTT):
                sj = tt % 2
                for jc in range(4):
                    wj = jc % 2
                    for hc in range(NHL):
                        p.emit("tensor", lambda e, tt=tt, jc=jc, hc=hc, wj=wj:
                               e.matmul(
                            psum_wo[wj][:],
                            zT_sb[:, (hc * NTT + tt) * 128:(hc * NTT + tt + 1) * 128],
                            wo_sb[:, hc * D + jc * 512: hc * D + (jc + 1) * 512],
                            start=(hc == 0), stop=(hc == NHL - 1)),
                            rd=("zT", "wo"), wr=(f"pwo{wj}",))
                    p.emit("vector", lambda e, jc=jc, wj=wj, sj=sj: e.tensor_copy(
                        out=sstage[sj][:, jc * 512:(jc + 1) * 512],
                        in_=psum_wo[wj][:]),
                        rd=(f"pwo{wj}",), wr=(f"sst{sj}", f"pwo{wj}"))
                sstores.append(p.emit("gd", lambda e, tt=tt, sj=sj:
                                      e.indirect_dma_start(
                    out=s_pool,
                    out_offset=IndirectOffsetOnAxis(ap=rows_sb[:, tt:tt + 1],
                                                    axis=0),
                    in_=sstage[sj][:], in_offset=None),
                    rd=(f"sst{sj}", "rows")))
        p.barrier()
        sbar = pair_barrier(after=tuple(sstores))
        p.res["s_ready"] = {"w": sbar, "r": {}}
        if upto == "p2c":
            return _finish(nc, p, dbg, root)

        with ExitStack() as ph:
            sb1 = lambda n_, s_, dt: ph.enter_context(nc.sbuf_tensor(n_, s_, dt))
            xnT0 = sb1("xnT0", [128, NDC * 512], BF16)
            xnT1 = sb1("xnT1", [128, NDC * 512], BF16)
            w1q_sb = [sb1(f"w1q_sb{j}", [128, D], BF16) for j in range(2)]
            w2q_sb = [sb1(f"w2q_sb{j}", [128, D], BF16) for j in range(2)]
            w3f_sb = [sb1(f"w3f_sb{j}", [128, D], BF16) for j in range(3)]
            _h1 = sb1("hidT0", [128, NHC * 1024], BF16)
            hidT = [_h1, _h1]
            sig = [sb1("sig0", [128, 512], F32)] * 2
            fstage = [sb1(f"fstage{j}", [128, D], F32) for j in range(2)]
            P8 = [ph.enter_context(nc.psum_tensor(f"P8_{k}", [128, 512], F32))
                  for k in range(8)]
            px1 = [P8[0], P8[1]]
            px2 = [P8[2], P8[3]]
            fsb = [sb1(f"fsb{j}", [128, D], F32) for j in range(2)]
            ssb = [sb1(f"ssb{j}", [128, D], F32) for j in range(2)]
            gsb = [sb1(f"gsb{j}", [128, D], F32) for j in range(2)]
            osb = [sb1(f"osb{j}", [128, D], F32) for j in range(2)]

            def combine_chunk(pair):
                fbar = pair_barrier(after=tuple(fstores[pair]))
                p.res[f"f_ready{pair}"] = {"w": fbar, "r": {}}
                for ttl in range(8):
                    tt = pair * 8 + ttl
                    j = tt % 2
                    p.emit("st", lambda e, tt=tt, j=j: e.dma_start(
                        out=fsb[j][:], in_=f_pool[tt * 128:(tt + 1) * 128, :]),
                        rd=(f"f_ready{pair}",), wr=(f"fsb{j}",))
                    p.emit("st", lambda e, tt=tt, j=j: e.dma_start(
                        out=gsb[j][:],
                        in_=f_pool[T + tt * 128:T + (tt + 1) * 128, :]),
                        rd=(f"f_ready{pair}",), wr=(f"gsb{j}",))
                    p.emit("vector", lambda e, j=j: e.tensor_add(
                        fsb[j][:], fsb[j][:], gsb[j][:]),
                        rd=(f"fsb{j}", f"gsb{j}"), wr=(f"fsb{j}",))
                    p.emit("scalar", lambda e, j=j: e.activation(
                        out=gsb[j][:], in_=fsb[j][:], func=AF.Silu),
                        rd=(f"fsb{j}",), wr=(f"gsb{j}", f"fsb{j}"))
                    p.emit("st", lambda e, tt=tt, j=j: e.dma_start(
                        out=ssb[j][:], in_=s_pool[tt * 128:(tt + 1) * 128, :]),
                        rd=("s_ready",), wr=(f"ssb{j}",))
                    p.emit("st", lambda e, tt=tt, j=j: e.dma_start(
                        out=osb[j][:],
                        in_=s_pool[T + tt * 128:T + (tt + 1) * 128, :]),
                        rd=("s_ready",), wr=(f"osb{j}",))
                    p.emit("vector", lambda e, j=j: e.tensor_add(
                        ssb[j][:], ssb[j][:], osb[j][:]),
                        rd=(f"ssb{j}", f"osb{j}"), wr=(f"ssb{j}",))
                    p.emit("vector", lambda e, j=j: e.tensor_mul(
                        gsb[j][:], gsb[j][:], ssb[j][:]),
                        rd=(f"gsb{j}", f"ssb{j}"), wr=(f"gsb{j}",))
                    p.emit("st", lambda e, tt=tt, j=j: e.dma_start(
                        out=osb[j][:], in_=x[tt * 128:(tt + 1) * 128, :]),
                        rd=(f"ssb{j}",), wr=(f"osb{j}",))
                    p.emit("vector", lambda e, j=j: e.tensor_add(
                        osb[j][:], gsb[j][:], osb[j][:]),
                        rd=(f"gsb{j}", f"osb{j}"), wr=(f"osb{j}",))
                    p.emit("st", lambda e, tt=tt, j=j: e.dma_start(
                        out=out[tt * 128:(tt + 1) * 128, :], in_=osb[j][:]),
                        rd=(f"osb{j}",))

            fstores = [[] for _ in range(4)]
            for pair in range(4):
                hp = pair % 2
                p.emit("ld", lambda e, pair=pair: e.dma_start(
                    out=xnT0[:], in_=xnT_s[2 * pair]), wr=("x0",))
                p.emit("ld", lambda e, pair=pair: e.dma_start(
                    out=xnT1[:], in_=xnT_s[2 * pair + 1]), wr=("x1",))
                for hc in range(NHC):
                    wj = hc % 2
                    p.emit("ld", lambda e, hc=hc, wj=wj: e.dma_start(
                        out=w1q_sb[wj][:], in_=w1q_s[hc]), wr=(f"w1{wj}",))
                    p.emit("ld", lambda e, hc=hc, wj=wj: e.dma_start(
                        out=w2q_sb[wj][:], in_=w2q_s[hc]), wr=(f"w2{wj}",))
                    for sub, (xnT_b, xk) in ((0, (xnT0, "x0")), (1, (xnT1, "x1"))):
                        sj = (2 * hc + sub) % 2
                        for wq_b, wkk, px, pxk in (
                                (w1q_sb[wj], f"w1{wj}", px1[sj], f"P{sj}"),
                                (w2q_sb[wj], f"w2{wj}", px2[sj], f"P{2 + sj}")):
                            for dc in range(NDC):
                                p.emit("tensor", lambda e, wq_b=wq_b, px=px,
                                       xnT_b=xnT_b, dc=dc: e.matmul(
                                    px[:],
                                    wq_b[:, dc * 128:(dc + 1) * 128],
                                    xnT_b[:, dc * 512:(dc + 1) * 512],
                                    start=(dc == 0), stop=(dc == NDC - 1)),
                                    rd=(wkk, xk), wr=(pxk,))
                        p.emit("scalar", lambda e, px=px1[sj], sg=sig[sj]:
                               e.activation(
                            out=sg[:], in_=px[:], func=AF.Sigmoid),
                            rd=(f"P{sj}",), wr=(f"sig{sj}", f"P{sj}"))
                        p.emit("vector", lambda e, hc=hc, sub=sub, px=px2[sj],
                               sg=sig[sj], hb=hidT[hp]: e.tensor_mul(
                            hb[:, hc * 1024 + sub * 512:
                               hc * 1024 + (sub + 1) * 512],
                            sg[:], px[:]),
                            rd=(f"sig{sj}", f"P{2 + sj}"),
                            wr=("hidT0", f"P{2 + sj}"),
                            tag=f"hid_p{pair}_hc{hc}_s{sub}")
                if pair == 0 and "hid0" in debug:
                    hd = nc.dram_tensor("dbg_hid0", [128, NHC * 1024], BF16,
                                        kind="ExternalOutput").ap()
                    p.emit("st", lambda e, hd=hd: e.dma_start(
                        out=hd, in_=hidT[0][:]), rd=("hidT0",))
                for tpair in range(4):
                    for hc in range(NHC):
                        wj = hc % 3
                        p.emit("ld", lambda e, hc=hc, wj=wj: e.dma_start(
                            out=w3f_sb[wj][:], in_=w3q_s[hc]), wr=(f"w3{wj}",))
                        for jc in range(4):
                            for t2 in range(2):
                                ttl = tpair * 2 + t2
                                p.emit("tensor", lambda e, hc=hc, ttl=ttl,
                                       pfk=P8[jc * 2 + t2], wj=wj, jc=jc,
                                       hb=hidT[hp]: e.matmul(
                                    pfk[:],
                                    hb[:, hc * 1024 + ttl * 128:
                                       hc * 1024 + (ttl + 1) * 128],
                                    w3f_sb[wj][:, jc * 512:(jc + 1) * 512],
                                    start=(hc == 0), stop=(hc == NHC - 1)),
                                    rd=("hidT0", f"w3{wj}"),
                                    wr=(f"P{jc * 2 + t2}",))
                    for t2 in range(2):
                        ttl = tpair * 2 + t2
                        tt = pair * 8 + ttl
                        fj = ttl % 2
                        for jc in range(4):
                            p.emit("scalar", lambda e, pfk=P8[jc * 2 + t2],
                                   fj=fj, jc=jc: e.copy(
                                out=fstage[fj][:, jc * 512:(jc + 1) * 512],
                                in_=pfk[:]),
                                rd=(f"P{jc * 2 + t2}",),
                                wr=(f"fst{fj}", f"P{jc * 2 + t2}"))
                        fstores[pair].append(p.emit(
                            "gd", lambda e, tt=tt, fj=fj:
                            e.indirect_dma_start(
                                out=f_pool,
                                out_offset=IndirectOffsetOnAxis(
                                    ap=rows_sb[:, tt:tt + 1], axis=0),
                                in_=fstage[fj][:], in_offset=None),
                            rd=(f"fst{fj}", "rows")))
                if not COMBINE_DEFER:
                    combine_chunk(pair)
            if COMBINE_DEFER:
                p.barrier()
                for pr in range(4):
                    combine_chunk(pr)
        if upto == "p3":
            return _finish(nc, p, dbg, root)
        return _finish(nc, p, dbg, root)


_NC_CACHE = {}


def _get_program():
    key = (UPTO, tuple(DEBUG_DUMPS))
    if key not in _NC_CACHE:
        _NC_CACHE[key] = build_program(UPTO, DEBUG_DUMPS)
    return _NC_CACHE[key]


def kernel(L0_attn_x, L0_attn_Wq, L0_attn_Wk, L0_attn_Wv, L0_attn_Wo,
           L0_ffn_w1, L0_ffn_w2, L0_ffn_w3):
    x = np.ascontiguousarray(np.asarray(L0_attn_x, dtype=np.float32))
    wq = np.asarray(L0_attn_Wq, dtype=np.float32)
    wk = np.asarray(L0_attn_Wk, dtype=np.float32)
    wv = np.asarray(L0_attn_Wv, dtype=np.float32)
    wo = np.asarray(L0_attn_Wo, dtype=np.float32)
    w1 = np.asarray(L0_ffn_w1, dtype=np.float32)
    w2 = np.asarray(L0_ffn_w2, dtype=np.float32)
    w3 = np.asarray(L0_ffn_w3, dtype=np.float32)

    def blk_proj(wt, g):
        m = wt[:, g * 1024:(g + 1) * 1024]
        return np.ascontiguousarray(
            m.reshape(16, 128, 1024).transpose(1, 0, 2).reshape(128, 16384))

    def blk_wo(wt, g):
        m = wt[g * 1024:(g + 1) * 1024, :]
        return np.ascontiguousarray(
            m.reshape(8, 128, 2048).transpose(1, 0, 2).reshape(128, 16384))

    wqT = wq.T
    wkT = wk.T
    wvT = wv.T
    woT = wo.T
    w1T = np.ascontiguousarray(w1.T)
    w2T = np.ascontiguousarray(w2.T)

    def blk_ffn(wt, g):
        m = wt[:, g * HF:(g + 1) * HF]
        return np.ascontiguousarray(
            m.reshape(16, 128, 22, 128).transpose(1, 2, 0, 3).reshape(
                128, 22 * 2048))
    w3T = np.ascontiguousarray(w3.T)

    nc, _ = _get_program()
    in_maps = []
    for c in range(8):
        b, g = c // 2, c % 2
        in_maps.append({
            "x": np.ascontiguousarray(x[b]),
            "wqT": blk_proj(wqT, g),
            "wkT": blk_proj(wkT, g),
            "wvT": blk_proj(wvT, g),
            "woT": blk_wo(woT, g),
            "w1T": np.ascontiguousarray(w1T[:, g * HF:(g + 1) * HF]),
            "w2T": np.ascontiguousarray(w2T[:, g * HF:(g + 1) * HF]),
            "w1B": blk_ffn(w1T, g),
            "w2B": blk_ffn(w2T, g),
            "w3T": np.ascontiguousarray(w3T[g * HF:(g + 1) * HF, :]),
            "rows_idx": (g * T + np.arange(NTT, dtype=np.int32)[None, :] * 128
                         + np.arange(128, dtype=np.int32)[:, None]).astype(np.int32),
        })
    res = run_bass_kernel_spmd(nc, in_maps, core_ids=list(range(8)))
    out = np.empty((4, T, D), dtype=np.float32)
    for b in range(4):
        out[b] = res.results[2 * b]["out"]
    return out
